# version 15
# baseline (speedup 1.0000x reference)
import math
from contextlib import ExitStack
from dataclasses import dataclass

import numpy as np
import ml_dtypes

import concourse.bass as bass
import concourse.mybir as mybir
from concourse.bass_utils import run_bass_kernel_spmd

F32 = mybir.dt.float32
BF16 = mybir.dt.bfloat16
I32 = mybir.dt.int32
AF = mybir.ActivationFunctionType
ALU = mybir.AluOpType
AX = mybir.AxisListType

BF = ml_dtypes.bfloat16
NEG = -60000.0
EPS_LN = 1e-5
EPS_Q = 1e-8
L0_SCALE = 1.0 / math.sqrt(128.0)
L1_SCALE = math.sqrt(128.0)


@dataclass(frozen=True)
class Cfg:
    D: int = 2048
    HQ: int = 16
    HKV: int = 4
    W: int = 1024
    T: int = 4096
    B: int = 2
    F0: int = 5632
    F1: int = 4096
    NC: int = 8

    @property
    def DC(self):
        return self.D // 128

    @property
    def R(self):
        return self.B * self.T // self.NC

    @property
    def FC0(self):
        return self.F0 // 128

    @property
    def FC1(self):
        return self.F1 // 128

    @property
    def GQ(self):
        return self.HQ // self.HKV


CFG = Cfg()


def col_chunks(total, size, start=0):
    out = []
    s = start
    end = start + total
    while s < end:
        out.append((s, min(size, end - s)))
        s += size
    return out



class Sched:

    DMA_POOL = 20

    def __init__(self, nc, ctx):
        self.nc = nc
        self.ctx = ctx
        self.ops = []
        self.writer = {}
        self.readers = {}
        self.last_of = {}
        self.ns = ""

    def _k(self, b):
        return b if b.startswith("@") else self.ns + b

    def add(self, engine, fn, reads=(), writes=(), dma=False, dmagrp=None,
            tag=""):
        oid = len(self.ops)
        raw_w0 = writes[0] if writes else None
        reads = [self._k(b) for b in reads]
        writes = [self._k(b) for b in writes]
        deps = set()
        for b in reads:
            w = self.writer.get(b)
            if w is not None:
                deps.add(w)
        for b in writes:
            w = self.writer.get(b)
            if w is not None:
                deps.add(w)
            deps.update(self.readers.get(b, ()))
        dmakey = (dmagrp or raw_w0) if dma else None
        self.ops.append(dict(oid=oid, e=engine, fn=fn, deps=deps, dma=dma,
                             dmakey=dmakey, tag=self.ns + tag))
        self.last_of[("d", engine, dmakey) if dma else ("c", engine)] = oid
        for b in reads:
            self.readers.setdefault(b, set()).add(oid)
        for b in writes:
            self.writer[b] = oid
            self.readers[b] = set()
        return oid

    def dep(self, oid_from_buf):
        return self.writer.get(oid_from_buf)

    def barrier(self, tag="bar"):
        snap = dict(self.last_of)
        for e in ("sp", "g", "t", "v", "a"):
            oid = len(self.ops)
            deps = set(snap.values())
            self.ops.append(dict(oid=oid, e=e, fn=(lambda eng: eng.nop()),
                                 deps=deps, dma=False, dmakey=None,
                                 tag=f"{tag}.{e}"))
            self.last_of[("c", e)] = oid

    def emit(self, block):
        ops = self.ops
        for o in ops:
            o["rdeps"] = set()
            for d in o["deps"]:
                if ops[d]["e"] == "t" and o["e"] == "t" and not ops[d]["dma"] and not o["dma"]:
                    continue
                if d == o["oid"]:
                    continue
                o["rdeps"].add(d)
        awaited = set()
        for o in ops:
            awaited.update(o["rdeps"])

        sems = {}

        def get_sem(key):
            if key not in sems:
                h = self.ctx.enter_context(self.nc.semaphore(f"s_{len(sems)}"))
                sems[key] = [h, 0]
            return sems[key]

        for o in ops:
            if o["dma"]:
                s = get_sem(("d", o["e"], o["dmakey"]))
                s[1] += 16
                o["sem"], o["tick"] = s, s[1]
            elif o["oid"] in awaited:
                s = get_sem(("c", o["e"]))
                s[1] += 1
                o["sem"], o["tick"] = s, s[1]
            else:
                o["sem"], o["tick"] = None, None

        hw = {}
        dma_ops = {}
        for o in ops:
            if o["dma"]:
                dma_ops.setdefault(id(o["sem"]), []).append((o["oid"], o["tick"]))

        import bisect

        def dma_wait_val(sem, waiter_oid):
            lst = dma_ops[id(sem)]
            i = bisect.bisect_left(lst, (waiter_oid, -1)) - 1
            return lst[i][1]

        def body_for(ename):
            ops_e = [o for o in ops if o["e"] == ename]

            def body(eng):
                for o in ops_e:
                    for d in sorted(o["rdeps"]):
                        od = ops[d]
                        sem, tick = od["sem"], od["tick"]
                        assert sem is not None, (ename, o["tag"], od["tag"])
                        if od["dma"]:
                            tick = dma_wait_val(sem, o["oid"])
                        k = (ename, id(sem))
                        if hw.get(k, 0) < tick:
                            eng.wait_ge(sem[0], tick)
                            hw[k] = tick
                    inst = o["fn"](eng)
                    if o["sem"] is not None:
                        if inst is None:
                            assert not o["dma"], o["tag"]
                            eng.sem_inc(o["sem"][0], 1)
                        else:
                            inst.then_inc(o["sem"][0], 16 if o["dma"] else 1)
            return body

        for ename in ("sp", "g", "t", "v", "a"):
            if any(o["e"] == ename for o in self.ops):
                getattr(block, {"sp": "sync", "g": "gpsimd", "t": "tensor",
                                "v": "vector", "a": "scalar"}[ename])(body_for(ename))



def host_sincos(cfg):
    hd, t, theta = 128, cfg.T, 10000.0
    fraction = np.arange(0, hd, 2, dtype=np.float64) / hd
    timescale = theta ** fraction
    ang = np.arange(t, dtype=np.float64)[:, None] / timescale[None, :]
    ang = np.concatenate([ang, ang], axis=-1)
    return (np.ascontiguousarray(np.sin(ang).astype(np.float32).T),
            np.ascontiguousarray(np.cos(ang).astype(np.float32).T))


def l0_key_windows(cfg, H, R):
    out = []
    for kc in range((H + R) // 128):
        c0 = kc * 128
        qs = max(0, c0 - H)
        qe = min(R, (c0 + 127) - H + cfg.W + 1)
        if qs < qe:
            out.append((kc, qs, qe))
    return out


def l0_mask_tiles(cfg, H, R):
    tiles, uniq = {}, {}
    for kc, qs, qe in l0_key_windows(cfg, H, R):
        c0 = kc * 128
        for qb in range(qs - qs % 128, qe, 128):
            j = np.arange(128)[:, None]
            q = qb + np.arange(128)[None, :]
            valid = (q >= (c0 + j) - H) & (q <= (c0 + j) - H + cfg.W)
            inplay = (q >= max(qb, qs)) & (q < qe)
            m = np.where(valid | ~inplay, np.float32(0.0), np.float32(NEG))
            if not (m == 0).all():
                key = m.tobytes()
                if key not in uniq:
                    uniq[key] = f"msk{len(uniq)}"
                tiles[(kc, qb)] = uniq[key]
    consts = {}
    for key, name in uniq.items():
        consts[name] = np.frombuffer(key, dtype=np.float32).reshape(128, 128).copy()
    return tiles, consts



def make_ln_scratch(cfg, nc, ctx, name):
    DC = cfg.DC
    CH = 128
    return dict(
        CH=CH, sid=name,
        xf=ctx.enter_context(nc.sbuf_tensor(f"{name}xf", [128, 2 * DC * CH], F32)),
        xb=ctx.enter_context(nc.sbuf_tensor(f"{name}xb", [128, DC * CH], BF16)),
        sq=ctx.enter_context(nc.sbuf_tensor(f"{name}sq", [128, DC * CH], BF16)),
        st=ctx.enter_context(nc.sbuf_tensor(f"{name}st", [128, 4 * CH], F32)),
        df=ctx.enter_context(nc.sbuf_tensor(f"{name}df", [128, CH], F32)),
        ps=ctx.enter_context(nc.psum_tensor(f"{name}ps", [128, 1024], F32)),
    )


def emit_ln(cfg, nc, scr, S, src_dram, col0, ncols, xn_sb, xn_stride, dst0,
            pfx, src_deps=()):
    DC, D = cfg.DC, cfg.D
    CH = scr["CH"]
    xf, xb, sq, st, df, ps = (scr[k] for k in ("xf", "xb", "sq", "st", "df", "ps"))
    sid = scr["sid"]
    ones_bf = S.consts["ones_bf"]
    src_r = src_dram.rearrange("(dc p) t -> p dc t", p=128)

    for i, (t0, tw) in enumerate(col_chunks(ncols, CH, start=col0)):
        b = i % 2
        xfb = xf[:, b * DC * CH: b * DC * CH + DC * tw]
        S.add("sp", lambda sp, xfb=xfb, t0=t0, tw=tw: sp.dma_start(
            out=xfb.rearrange("p (dc t) -> p dc t", dc=DC),
            in_=src_r[:, :, t0:t0 + tw]),
            reads=tuple(src_deps), writes=(f"{sid}xf{b}",), dma=True,
            tag=f"{pfx}ld{i}")
        for dc in range(DC):
            S.add("v", lambda v, dc=dc, xfb=xfb, tw=tw: v.tensor_copy(
                xb[:, dc * CH: dc * CH + tw], xfb[:, dc * tw:(dc + 1) * tw]),
                reads=(f"{sid}xf{b}",), writes=(f"{sid}xb{dc}",), tag=f"{pfx}cb{i}.{dc}")
            S.add("a", lambda a, dc=dc, xfb=xfb, tw=tw: a.activation(
                sq[:, dc * CH: dc * CH + tw], xfb[:, dc * tw:(dc + 1) * tw], AF.Square),
                reads=(f"{sid}xf{b}",), writes=(f"{sid}sq{dc}",), tag=f"{pfx}sq{i}.{dc}")
        for dc in range(DC):
            S.add("t", lambda t, dc=dc, tw=tw: t.matmul(
                ps[:, 0:tw], ones_bf[:, 0:128], xb[:, dc * CH: dc * CH + tw],
                start=(dc == 0), stop=(dc == DC - 1)),
                reads=("@ones_bf", f"{sid}xb{dc}"), writes=(f"{sid}psm",), tag=f"{pfx}mm{i}.{dc}")
        for dc in range(DC):
            S.add("t", lambda t, dc=dc, tw=tw: t.matmul(
                ps[:, 512:512 + tw], ones_bf[:, 0:128], sq[:, dc * CH: dc * CH + tw],
                start=(dc == 0), stop=(dc == DC - 1)),
                reads=("@ones_bf", f"{sid}sq{dc}"), writes=(f"{sid}pss",), tag=f"{pfx}ms{i}.{dc}")
        mean = st[:, 0:tw]
        m2 = st[:, CH:CH + tw]
        var = st[:, 2 * CH:2 * CH + tw]
        rstd = st[:, 3 * CH:3 * CH + tw]
        S.add("v", lambda v, mean=mean, tw=tw: v.tensor_scalar(
            mean, ps[:, 0:tw], 1.0 / D, None, ALU.mult),
            reads=(f"{sid}psm",), writes=(f"{sid}mean",), tag=f"{pfx}mean{i}")
        S.add("v", lambda v, mean=mean, m2=m2: v.tensor_tensor(m2, mean, mean, ALU.mult),
            reads=(f"{sid}mean",), writes=(f"{sid}m2",), tag=f"{pfx}m2{i}")
        S.add("v", lambda v, var=var, m2=m2, tw=tw: v.scalar_tensor_tensor(
            var, ps[:, 512:512 + tw], 1.0 / D, m2, ALU.mult, ALU.subtract),
            reads=(f"{sid}pss", f"{sid}m2"), writes=(f"{sid}var",), tag=f"{pfx}var{i}")
        S.add("a", lambda a, var=var: a.activation(var, var, AF.Ln,
                                                   bias=S.consts["epsln"][:, 0:1]),
            reads=(f"{sid}var", "@epsln"), writes=(f"{sid}var",), tag=f"{pfx}lg{i}")
        S.add("a", lambda a, var=var, rstd=rstd: a.activation(rstd, var, AF.Exp, scale=-0.5),
            reads=(f"{sid}var",), writes=(f"{sid}rstd",), tag=f"{pfx}rs{i}")
        for dc in range(DC):
            S.add("v", lambda v, dc=dc, xfb=xfb, mean=mean, tw=tw: v.tensor_tensor(
                df[:, 0:tw], xfb[:, dc * tw:(dc + 1) * tw], mean, ALU.subtract),
                reads=(f"{sid}xf{b}", f"{sid}mean"), writes=(f"{sid}df",),
                tag=f"{pfx}sub{i}.{dc}")
            d0 = dc * xn_stride + dst0 + (t0 - col0)
            S.add("v", lambda v, d0=d0, rstd=rstd, tw=tw: v.tensor_tensor(
                xn_sb[:, d0:d0 + tw], df[:, 0:tw], rstd, ALU.mult),
                reads=(f"{sid}df", f"{sid}rstd"),
                writes=(f"{pfx}xn", f"{pfx}xn.{dc}.{t0}"), tag=f"{pfx}mul{i}.{dc}")


def emit_proj(cfg, nc, ps, psname, S, w_sb, wbuf, woc, xn_sb, xn_stride,
              xn_col0, xn_buf, ncols, oc_list, out_cb, pfx, kdc=None,
              extra_mm=None):
    DC = kdc if kdc is not None else cfg.DC
    ncc = len(col_chunks(ncols, 512))
    for oi, oc in enumerate(oc_list):
        for ci, (t0, tw) in enumerate(col_chunks(ncols, 512)):
            b = (oi * ncc + ci) % 2
            ps_ap = ps[:, b * 512: b * 512 + tw]
            psbuf = f"{psname}{b}"
            for dc in range(DC):
                S.add("t", lambda t, dc=dc, oi=oi, t0=t0, tw=tw, ps_ap=ps_ap: t.matmul(
                    ps_ap,
                    w_sb[:, (dc * woc + oi) * 128: (dc * woc + oi) * 128 + 128],
                    xn_sb[:, dc * xn_stride + xn_col0 + t0:
                          dc * xn_stride + xn_col0 + t0 + tw],
                    start=(dc == 0), stop=(dc == DC - 1 and extra_mm is None)),
                    reads=(wbuf, xn_buf), writes=(psbuf,), tag=f"{pfx}mm{oc}.{t0}.{dc}")
            if extra_mm is not None:
                extra_mm(S, oi, oc, t0, tw, ps_ap, psbuf)
            out_cb(S, oi, oc, t0, tw, ps_ap, psbuf)


def emit_rope(cfg, nc, S, buf_sb, bufname, col0, ncols, sc_col0, pfx):
    rot = S.consts["rope_rot"]
    t1 = S.consts["rope_tmp"]
    t2 = S.consts["rope_tmp2"]
    sin, cos = S.consts["sinT"], S.consts["cosT"]
    for t0, tw in col_chunks(ncols, 512):
        c = col0 + t0
        s = sc_col0 + t0
        S.add("sp", lambda sp, c=c, tw=tw: sp.dma_start(
            out=rot[0:64, 0:tw], in_=buf_sb[64:128, c:c + tw]),
            reads=(bufname,), writes=("rope_rot_l",), dma=True, dmagrp="ropedma",
            tag=f"{pfx}d1.{t0}")
        S.add("sp", lambda sp, c=c, tw=tw: sp.dma_start(
            out=rot[64:128, 0:tw], in_=buf_sb[0:64, c:c + tw]),
            reads=(bufname,), writes=("rope_rot_h",), dma=True, dmagrp="ropedma",
            tag=f"{pfx}d2.{t0}")
        S.add("g", lambda g, c=c, s=s, tw=tw: g.tensor_tensor(
            t1[:, 0:tw], buf_sb[:, c:c + tw], cos[:, s:s + tw], ALU.mult),
            reads=(bufname, "sincos"), writes=("rope_t1",), tag=f"{pfx}r1.{t0}")
        S.add("g", lambda g, s=s, tw=tw: g.tensor_tensor(
            t2[:, 0:tw], rot[:, 0:tw], sin[:, s:s + tw], ALU.mult),
            reads=("rope_rot_l", "rope_rot_h", "sincos"), writes=("rope_t2",),
            tag=f"{pfx}r2.{t0}")
        S.add("g", lambda g, c=c, tw=tw: g.tensor_tensor(
            buf_sb[:, c:c + tw], t1[:, 0:tw], t2[:, 0:tw], ALU.add),
            reads=("rope_t1", "rope_t2"), writes=(bufname,), tag=f"{pfx}r3.{t0}")



def emit_attn(cfg, nc, ctx, S, pfx, *, heads, kv_of, Tkeys, R, kwin, masks,
              scale, prep_head, kt_ap, q_ap, out_cb, v_tile=None, v_dmaj=None,
              key_bias=None):
    ident_bf = S.consts["ident_bf"]
    ones_bf = S.consts["ones_bf"]

    ntiles = R // 128
    tlo, thi = [], []
    for ti in range(ntiles):
        qt0 = ti * 128
        lo = min(kc * 128 for kc, qs, qe in kwin if qs < qt0 + 128 and qe > qt0)
        hi = max(kc * 128 + 128 for kc, qs, qe in kwin if qs < qt0 + 128 and qe > qt0)
        tlo.append(lo)
        thi.append(hi)
    PMAX = max(h - l for l, h in zip(tlo, thi))
    NPMAX = (PMAX + 511) // 512

    sa_ps = ctx.enter_context(nc.psum_tensor(f"{pfx}sa", [128, 1024], F32))
    tp_ps = [ctx.enter_context(nc.psum_tensor(f"{pfx}tp{i}", [128, 512], BF16))
             for i in range(2)]
    vt_ps = [ctx.enter_context(nc.psum_tensor(f"{pfx}vt{i}", [128, 512], BF16))
             for i in range(2)]
    den_ps = ctx.enter_context(nc.psum_tensor(f"{pfx}den", [128, 128], F32))
    out_ps = ctx.enter_context(nc.psum_tensor(f"{pfx}out", [128, 128], F32))

    pp = ctx.enter_context(nc.sbuf_tensor(f"{pfx}pp", [128, 2 * PMAX], BF16))
    ptsb = ctx.enter_context(nc.sbuf_tensor(f"{pfx}pts", [128, 2 * 512], BF16))
    mg = ctx.enter_context(nc.sbuf_tensor(f"{pfx}mg", [128, 2 * NPMAX], F32))
    mps = ctx.enter_context(nc.sbuf_tensor(f"{pfx}mps", [128, 2 * NPMAX], F32))
    alp = ctx.enter_context(nc.sbuf_tensor(f"{pfx}alp", [128, 2 * NPMAX], F32))
    mts = ctx.enter_context(nc.sbuf_tensor(f"{pfx}mts", [128, 2], F32))
    den_sb = ctx.enter_context(nc.sbuf_tensor(f"{pfx}densb", [128, 2 * 128], F32))
    rec_sb = ctx.enter_context(nc.sbuf_tensor(f"{pfx}rec", [128, 2 * 128], F32))
    if v_dmaj is not None:
        vtt = ctx.enter_context(nc.sbuf_tensor(f"{pfx}vtt", [128, 2 * Tkeys], BF16))
        vkcs = sorted({kc for kc, _, _ in kwin})

        def v_tile(h, kc):
            hs = h % 2
            return vtt[:, hs * Tkeys + kc * 128: hs * Tkeys + kc * 128 + 128],                 f"{pfx}vtt{hs}"

    sa_bank = [0]
    pgrp = [0]
    vgrp = [0]

    def vtr_groups(h):
        hs = h % 2
        vap, vbuf = v_dmaj(h)
        vgroups = [vkcs[i:i + 4] for i in range(0, len(vkcs), 4)]
        for gi, g4 in enumerate(vgroups):
            sl = vgrp[0] % 2
            vgrp[0] += 1
            for i, kc in enumerate(g4):
                S.add("t", lambda t, sl=sl, i=i, kc=kc, vap=vap: t.transpose(
                    vt_ps[sl][:, i * 128: i * 128 + 128],
                    vap[:, kc * 128: kc * 128 + 128], ident_bf[:, :]),
                    reads=(vbuf, "@ident_bf"), writes=(f"{pfx}vt{sl}",),
                    tag=f"{pfx}VT{h}.{kc}")
            gn = len(g4)
            c0 = g4[0] * 128
            if gi % 2:
                S.add("v", lambda v, sl=sl, gn=gn, c0=c0, hs=hs: v.tensor_copy(
                    vtt[:, hs * Tkeys + c0: hs * Tkeys + c0 + gn * 128],
                    vt_ps[sl][:, 0: gn * 128]),
                    reads=(f"{pfx}vt{sl}",), writes=(f"{pfx}vtt{hs}",),
                    tag=f"{pfx}VE{h}.{gi}")
            else:
                S.add("a", lambda a, sl=sl, gn=gn, c0=c0, hs=hs: a.activation(
                    vtt[:, hs * Tkeys + c0: hs * Tkeys + c0 + gn * 128],
                    vt_ps[sl][:, 0: gn * 128], AF.Copy),
                    reads=(f"{pfx}vt{sl}",), writes=(f"{pfx}vtt{hs}",),
                    tag=f"{pfx}VE{h}.{gi}")
            yield

    def genA(j):
        h, ti = divmod(j, ntiles)
        hs = j % 2
        ktap, ktbuf = kt_ap(h)
        qap, qbuf = q_ap(h)
        qt0 = ti * 128
        lo = tlo[ti]
        pieces = col_chunks(thi[ti] - lo, 512, start=lo)
        np_ = len(pieces)
        vg = vtr_groups(h) if (v_dmaj is not None and ti == 0) else None
        for pi, (p0, pw) in enumerate(pieces):
            b = sa_bank[0]
            sa_bank[0] = (b + 1) % 2
            sap = sa_ps[:, b * 512: b * 512 + pw]
            sbf = f"{pfx}sa{b}"
            S.add("t", lambda t, sap=sap, p0=p0, pw=pw, ktap=ktap, qap=qap,
                  qt0=qt0, stp=(key_bias is None): t.matmul(
                  sap, qap[:, qt0:qt0 + 128], ktap[:, p0:p0 + pw],
                  start=True, stop=stp),
                  reads=(qbuf, ktbuf), writes=(sbf,),
                  tag=f"{pfx}A{h}.{ti}.{p0}")
            if key_bias is not None:
                kb_ap, kb_buf = key_bias(p0, pw)
                S.add("t", lambda t, sap=sap, kb_ap=kb_ap, pw=pw: t.matmul(
                    sap, ones_bf[0:1, 0:128], kb_ap, start=False, stop=True),
                    reads=("@ones_bf", kb_buf), writes=(sbf,),
                    tag=f"{pfx}Ah{h}.{ti}.{p0}")
            for kc in range(p0 // 128, (p0 + pw) // 128):
                mt = masks.get((kc, qt0))
                if mt is not None:
                    S.add("v", lambda v, b=b, kc=kc, p0=p0, mt=mt: v.tensor_tensor(
                        sa_ps[:, b * 512 + kc * 128 - p0:
                              b * 512 + kc * 128 - p0 + 128],
                        sa_ps[:, b * 512 + kc * 128 - p0:
                              b * 512 + kc * 128 - p0 + 128],
                        S.consts[mt + "T"][:, :], ALU.add),
                        reads=(sbf, "@" + mt + "T"), writes=(sbf,),
                        tag=f"{pfx}Am{h}.{ti}.{kc}")
            S.add("v", lambda v, sap=sap, pi=pi, hs=hs: v.reduce_max(
                mg[:, hs * NPMAX + pi: hs * NPMAX + pi + 1], sap, axis=AX.X),
                reads=(sbf,), writes=(f"{pfx}mg{hs}",),
                tag=f"{pfx}Ar{h}.{ti}.{pi}")
            S.add("v", lambda v, pi=pi, hs=hs: v.tensor_scalar(
                mps[:, hs * NPMAX + pi: hs * NPMAX + pi + 1],
                mg[:, hs * NPMAX + pi: hs * NPMAX + pi + 1], -float(scale),
                None, ALU.mult),
                reads=(f"{pfx}mg{hs}",), writes=(f"{pfx}mps{hs}",),
                tag=f"{pfx}As{h}.{ti}.{pi}")
            S.add("a", lambda a, sap=sap, p0=p0, pw=pw, lo=lo, pi=pi, hs=hs:
                  a.activation(
                pp[:, hs * PMAX + p0 - lo: hs * PMAX + p0 - lo + pw], sap,
                AF.Exp, scale=float(scale),
                bias=mps[:, hs * NPMAX + pi: hs * NPMAX + pi + 1]),
                reads=(sbf, f"{pfx}mps{hs}"), writes=(f"{pfx}pp{hs}",),
                tag=f"{pfx}Ae{h}.{ti}.{pi}")
            if vg is not None:
                next(vg, None)
            yield
        if vg is not None:
            for _ in vg:
                yield
        S.add("v", lambda v, np_=np_, hs=hs: v.reduce_max(
            mts[:, hs:hs + 1], mg[:, hs * NPMAX: hs * NPMAX + np_], axis=AX.X),
            reads=(f"{pfx}mg{hs}",), writes=(f"{pfx}mtp{hs}",),
            tag=f"{pfx}At{h}.{ti}")
        S.add("v", lambda v, hs=hs: v.tensor_scalar(
            mts[:, hs:hs + 1], mts[:, hs:hs + 1], -float(scale), None, ALU.mult),
            reads=(f"{pfx}mtp{hs}",), writes=(f"{pfx}mts{hs}",),
            tag=f"{pfx}Ats{h}.{ti}")
        S.add("a", lambda a, np_=np_, hs=hs: a.activation(
            alp[:, hs * NPMAX: hs * NPMAX + np_],
            mg[:, hs * NPMAX: hs * NPMAX + np_], AF.Exp, scale=float(scale),
            bias=mts[:, hs:hs + 1]),
            reads=(f"{pfx}mg{hs}", f"{pfx}mts{hs}"), writes=(f"{pfx}alp{hs}",),
            tag=f"{pfx}Aa{h}.{ti}")
        if np_ > 1:
            for pi, (p0, pw) in enumerate(pieces):
                S.add("v", lambda v, p0=p0, pw=pw, lo=lo, pi=pi, hs=hs:
                      v.tensor_scalar(
                    pp[:, hs * PMAX + p0 - lo: hs * PMAX + p0 - lo + pw],
                    pp[:, hs * PMAX + p0 - lo: hs * PMAX + p0 - lo + pw],
                    alp[:, hs * NPMAX + pi: hs * NPMAX + pi + 1], None,
                    ALU.mult),
                    reads=(f"{pfx}pp{hs}", f"{pfx}alp{hs}"),
                    writes=(f"{pfx}pp{hs}",),
                    tag=f"{pfx}Ax{h}.{ti}.{pi}")
        yield

    def genC(j):
        h, ti = divmod(j, ntiles)
        hs = j % 2
        qt0 = ti * 128
        lo = tlo[ti]
        kcs = list(range(lo // 128, thi[ti] // 128))
        groups = [kcs[i:i + 4] for i in range(0, len(kcs), 4)]

        def tr_group(gi):
            sl = pgrp[0] % 2
            pgrp[0] += 1
            for i, kc in enumerate(groups[gi]):
                S.add("t", lambda t, sl=sl, i=i, kc=kc, hs=hs, lo=lo: t.transpose(
                    tp_ps[sl][:, i * 128: i * 128 + 128],
                    pp[:, hs * PMAX + kc * 128 - lo:
                       hs * PMAX + kc * 128 - lo + 128], ident_bf[:, :]),
                    reads=(f"{pfx}pp{hs}", "@ident_bf"), writes=(f"{pfx}tp{sl}",),
                    tag=f"{pfx}T{h}.{ti}.{kc}")
            gn = len(groups[gi])
            if gi % 2:
                S.add("v", lambda v, sl=sl, gn=gn: v.tensor_copy(
                    ptsb[:, sl * 512: sl * 512 + gn * 128],
                    tp_ps[sl][:, 0: gn * 128]),
                    reads=(f"{pfx}tp{sl}",), writes=(f"{pfx}pts{sl}",),
                    tag=f"{pfx}Te{h}.{ti}.{gi}")
            else:
                S.add("a", lambda a, sl=sl, gn=gn: a.activation(
                    ptsb[:, sl * 512: sl * 512 + gn * 128],
                    tp_ps[sl][:, 0: gn * 128], AF.Copy),
                    reads=(f"{pfx}tp{sl}",), writes=(f"{pfx}pts{sl}",),
                    tag=f"{pfx}Te{h}.{ti}.{gi}")
            return sl

        sls = {0: tr_group(0)}
        nk = len(kcs)
        ki = 0
        for gi in range(len(groups)):
            if gi + 1 < len(groups):
                sls[gi + 1] = tr_group(gi + 1)
            sl = sls[gi]
            for i, kc in enumerate(groups[gi]):
                vap, vbuf = v_tile(h, kc)
                first = ki == 0
                last = ki == nk - 1
                S.add("t", lambda t, sl=sl, i=i, first=first, last=last: t.matmul(
                    den_ps[:, 0:128], ones_bf[:, 0:128],
                    ptsb[:, sl * 512 + i * 128: sl * 512 + i * 128 + 128],
                    start=first, stop=last),
                    reads=("@ones_bf", f"{pfx}pts{sl}"), writes=(f"{pfx}denps",),
                    tag=f"{pfx}Bd{h}.{ti}.{kc}")
                S.add("t", lambda t, sl=sl, i=i, vap=vap, vbuf=vbuf, first=first,
                      last=last: t.matmul(
                    out_ps[:, 0:128], vap,
                    ptsb[:, sl * 512 + i * 128: sl * 512 + i * 128 + 128],
                    start=first, stop=last),
                    reads=(vbuf, f"{pfx}pts{sl}"), writes=(f"{pfx}outps",),
                    tag=f"{pfx}Bv{h}.{ti}.{kc}")
                ki += 1
            yield
        S.add("v", lambda v, hs=hs: v.tensor_copy(
            den_sb[:, hs * 128: hs * 128 + 128], den_ps[:, 0:128]),
            reads=(f"{pfx}denps",), writes=(f"{pfx}densb{hs}",),
            tag=f"{pfx}F1{h}.{ti}")
        S.add("a", lambda a, hs=hs: a.activation(
            den_sb[:, hs * 128: hs * 128 + 128],
            den_sb[:, hs * 128: hs * 128 + 128], AF.Ln),
            reads=(f"{pfx}densb{hs}",), writes=(f"{pfx}densb{hs}",),
            tag=f"{pfx}F2{h}.{ti}")
        S.add("a", lambda a, hs=hs: a.activation(
            rec_sb[:, hs * 128: hs * 128 + 128],
            den_sb[:, hs * 128: hs * 128 + 128], AF.Exp, scale=-1.0),
            reads=(f"{pfx}densb{hs}",), writes=(f"{pfx}recsb{hs}",),
            tag=f"{pfx}F3{h}.{ti}")
        out_cb(S, h, qt0, 128, out_ps[:, 0:128], f"{pfx}outps",
               rec_sb[:, hs * 128: hs * 128 + 128], f"{pfx}recsb{hs}")
        yield

    ntot = heads * ntiles
    prep_ti = min(2, ntiles - 1)
    prep_head(S, 0)
    ga = genA(0)
    for _ in ga:
        pass
    for j in range(ntot):
        hj, tij = divmod(j, ntiles)
        if tij == prep_ti and hj + 1 < heads:
            prep_head(S, hj + 1)
        gc = genC(j)
        ga = genA(j + 1) if j + 1 < ntot else None
        done_a = ga is None
        done_c = False
        while not (done_a and done_c):
            if not done_a:
                try:
                    next(ga)
                except StopIteration:
                    done_a = True
            if not done_c:
                try:
                    next(gc)
                except StopIteration:
                    done_c = True


def emit_quant(S, eng, dst_ap, dstbuf, src_ap, srcbuf, qa_ap, qabuf, qb_ap,
               qbbuf, tau_ap, ntau_ap, tag):
    S.add(eng, lambda e: e.tensor_scalar(qa_ap, src_ap, tau_ap, None, ALU.is_gt),
          reads=(srcbuf, "@tau"), writes=(qabuf,), tag=f"{tag}a")
    S.add(eng, lambda e: e.tensor_scalar(qb_ap, src_ap, ntau_ap, None, ALU.is_lt),
          reads=(srcbuf, "@tau"), writes=(qbbuf,), tag=f"{tag}b")
    S.add(eng, lambda e: e.tensor_tensor(dst_ap, qa_ap, qb_ap, ALU.subtract),
          reads=(qabuf, qbbuf), writes=(dstbuf,), tag=f"{tag}s")


def setup_consts(nc, ctx, S, specs):
    for name, (shape, dt) in specs.items():
        dr = nc.dram_tensor(f"c_{name}", list(shape), dt, kind="ExternalInput").ap()
        sb = ctx.enter_context(nc.sbuf_tensor(f"cs_{name}", list(shape), dt))
        S.consts[name] = sb
        S.add("sp", lambda sp, sb=sb, dr=dr: sp.dma_start(out=sb[:, :], in_=dr),
              writes=("@" + name,), dma=True, dmagrp="consts",
              tag=f"const_{name}")


def emit_sproj(cfg, nc, ctx, S, W_dram, wname, wsb, xparts, oc0, noc, out_cb,
               pfx, ps, psname, blk=2, kdc=None, extra_mm=None, cast=True):
    DCk = kdc if kdc is not None else cfg.DC
    w_r = W_dram.rearrange("(dc p) o -> p dc o", p=128)
    eng = "g" if cast else "sp"
    for bi, (o0, ow) in enumerate(col_chunks(noc * 128, blk * 128, start=oc0 * 128)):
        sl = bi % 2
        wap = wsb[:, sl * DCk * blk * 128: sl * DCk * blk * 128 + DCk * ow]
        wnm = wsb.name if hasattr(wsb, "name") else wsb.tensor.name
        if wnm.startswith(S.ns) and S.ns:
            wnm = wnm[len(S.ns):]
        wbuf = f"W&{wnm}^{sl}"
        S.add(eng, lambda e, wap=wap, o0=o0, ow=ow: e.dma_start(
            out=wap.rearrange("p (dc o) -> p dc o", dc=DCk),
            in_=w_r[:, :, o0:o0 + ow]),
            writes=(wbuf,), dma=True, tag=f"{pfx}wld{bi}")
        for xn_sb, stride, col0, xbuf, ncols, dst0 in xparts:
            emit_proj(cfg, nc, ps, psname, S, wap, wbuf, ow // 128,
                      xn_sb, stride, col0, xbuf, ncols,
                      list(range(ow // 128)),
                      (lambda S_, oi, oc_, t0, tw, ps_ap, psbuf,
                       o0=o0, dst0=dst0:
                       out_cb(S_, o0 // 128 + oi, dst0 + t0, tw, ps_ap, psbuf)),
                      f"{pfx}b{bi}", kdc=DCk,
                      extra_mm=(None if extra_mm is None else
                                (lambda S_, oi, oc_, t0, tw, ps_ap, psbuf,
                                 o0=o0, dst0=dst0:
                                 extra_mm(S_, o0 // 128 + oi, dst0 + t0, tw,
                                          ps_ap, psbuf))))



def emit_l0_gamma(cfg, nc, ctx, S, io):
    D, DC, F0, FC0 = cfg.D, cfg.DC, cfg.F0, cfg.FC0
    W1T, W2T, W3T = io["W1T"], io["W2T"], io["W3T"]
    tau = io["tau"]
    with ExitStack() as ctxG:
        gt = ctxG.enter_context(nc.sbuf_tensor("gt", [128, 2 * F0], F32))
        gacc = ctxG.enter_context(nc.sbuf_tensor("gacc", [128, 4], F32))
        gtmp = ctxG.enter_context(nc.sbuf_tensor("gtmp", [128, 1], F32))
        ps_g = ctxG.enter_context(nc.psum_tensor("ps_g", [128, 512], F32))
        S.add("v", lambda v: v.memset(gacc[:, :], 0.0), writes=("gacc",), tag="gz")
        for m, (Wd, rows, cols) in enumerate(
                [(W1T, DC, F0), (W2T, DC, F0), (W3T, FC0, D)]):
            for r in range(rows):
                sl = r % 2
                gap = gt[:, sl * F0: sl * F0 + cols]
                S.add("sp", lambda sp, gap=gap, Wd=Wd, r=r, cols=cols:
                      sp.dma_start(out=gap, in_=Wd[r * 128:(r + 1) * 128, :]),
                      writes=(f"gt{sl}",), dma=True, tag=f"gld{m}.{r}")
                S.add("v", lambda v, gap=gap: v.reduce_sum(
                    gtmp[:, :], gap, axis=AX.X, apply_absolute_value=True),
                    reads=(f"gt{sl}",), writes=("gtmp",), tag=f"gr{m}.{r}")
                S.add("v", lambda v, m=m: v.tensor_tensor(
                    gacc[:, m:m + 1], gacc[:, m:m + 1], gtmp[:, :], ALU.add),
                    reads=("gtmp", "gacc"), writes=("gacc",), tag=f"ga{m}.{r}")
        S.add("t", lambda t: t.matmul(ps_g[:, 0:4], S.consts["ones_f32"][:, :],
                                      gacc[:, :], start=True, stop=True),
              reads=("gacc", "@ones_f32"), writes=("psg",), tag="gmm")
        S.add("v", lambda v: v.tensor_copy(gacc[:, :], ps_g[:, 0:4]),
              reads=("psg",), writes=("gacc",), tag="gev")
        Ne = float(D * F0)
        for m in range(3):
            S.add("a", lambda a, m=m: a.activation(
                tau[:, m:m + 1], gacc[:, m:m + 1], AF.Copy,
                bias=0.5 * EPS_Q, scale=0.5 / Ne),
                reads=("gacc",), writes=("@tau",), tag=f"gt{m}")
            S.add("a", lambda a, m=m: a.activation(
                tau[:, 4 + m:5 + m], gacc[:, m:m + 1], AF.Copy,
                bias=-0.5 * EPS_Q, scale=-0.5 / Ne),
                reads=("gacc",), writes=("@tau",), tag=f"gnt{m}")
    S.barrier("b_gam")


def emit_l0_pass(cfg, nc, S, io, kwin, masks):
    R, D, DC = cfg.R, cfg.D, cfg.DC
    H = cfg.W
    T = H + R
    HKV, HQ, GQ, F0, FC0 = cfg.HKV, cfg.HQ, cfg.GQ, cfg.F0, cfg.FC0
    FB = 256
    XT, SINT, COST, HIND = io["XT"], io["SINT"], io["COST"], io["HIND"]
    WQ, WK, WV, WO = io["WQ"], io["WK"], io["WV"], io["WO"]
    W1T, W2T, W3T = io["W1T"], io["W2T"], io["W3T"]
    PS1, X1dst, tau = io["PS1"], io["X1dst"], io["tau"]
    x1keyp = io["x1keyp"]
    XT_r = XT.rearrange("(dc p) t -> p dc t", p=128)
    PS1_r = PS1.rearrange("(dc p) t -> p dc t", p=128)
    X1_r = X1dst.rearrange("(dc p) t -> p dc t", p=128)

    with ExitStack() as ctx:
        xno = ctx.enter_context(nc.sbuf_tensor(S.ns + "xno", [128, DC * R], BF16))
        hind = ctx.enter_context(nc.sbuf_tensor(S.ns + "hind", [1, T], BF16))
        S.add("g", lambda g: g.dma_start(out=hind[:, :], in_=HIND),
              writes=("hind",), dma=True, dmagrp="passld", tag="ldhind")

        ctxAP = ctx.enter_context(ExitStack())
        attnT = ctxAP.enter_context(nc.sbuf_tensor(S.ns + "attnT", [128, HQ * R], BF16))
        with ExitStack() as ctxA:
            QT = ctxA.enter_context(nc.sbuf_tensor(S.ns + "QT", [128, HQ * R], BF16))
            KT = ctxA.enter_context(nc.sbuf_tensor(S.ns + "KT", [128, HKV * T], BF16))
            Vtok = ctxA.enter_context(nc.sbuf_tensor(S.ns + "Vtok", [128, HKV * T], BF16))
            with ExitStack() as ctx1:
                xnh = ctx1.enter_context(nc.sbuf_tensor(S.ns + "xnh", [128, DC * H], BF16))
                with ExitStack() as ctxLN:
                    scr = make_ln_scratch(cfg, nc, ctxLN, S.ns + "lns")
                    emit_ln(cfg, nc, scr, S, XT, 0, H, xnh, H, 0, "lnh")
                    emit_ln(cfg, nc, scr, S, XT, H, R, xno, R, 0, "lno")
                S.barrier("b_ln")
                xp_kv = [(xnh, H, 0, "lnhxn", H, 0), (xno, R, 0, "lnoxn", R, H)]
                xp_q = [(xno, R, 0, "lnoxn", R, 0)]
                with ExitStack() as ctxW:
                    sinT = ctxW.enter_context(nc.sbuf_tensor(S.ns + "sinT", [128, T], F32))
                    cosT = ctxW.enter_context(nc.sbuf_tensor(S.ns + "cosT", [128, T], F32))
                    S.consts["sinT"] = sinT
                    S.consts["cosT"] = cosT
                    S.add("sp", lambda sp: sp.dma_start(out=sinT[:, :], in_=SINT),
                          writes=("sincos_s",), dma=True, dmagrp="passld",
                          tag="ldsin")
                    S.add("sp", lambda sp: sp.dma_start(out=cosT[:, :], in_=COST),
                          writes=("sincos",), dma=True, dmagrp="passld",
                          tag="ldcos")
                    S.add("sp", lambda sp: None, reads=("sincos_s",),
                          writes=("sincos",), tag="sincos_join")
                    S.consts["rope_tmp"] = ctxW.enter_context(
                        nc.sbuf_tensor(S.ns + "ropet1", [128, 512], F32))
                    S.consts["rope_tmp2"] = ctxW.enter_context(
                        nc.sbuf_tensor(S.ns + "ropet2", [128, 512], F32))
                    S.consts["rope_rot"] = ctxW.enter_context(
                        nc.sbuf_tensor(S.ns + "roperot", [128, 512], BF16))
                    wsb = ctxW.enter_context(
                        nc.sbuf_tensor(S.ns + "wsb", [128, 2 * DC * 1 * 128], BF16))
                    vtmp = ctxW.enter_context(nc.sbuf_tensor(S.ns + "vtmp", [128, 512], BF16))
                    ps_p = ctxW.enter_context(nc.psum_tensor(S.ns + "ps_p", [128, 1024], F32))
                    ps_tr = ctxW.enter_context(nc.psum_tensor(S.ns + "ps_tr", [128, 128], BF16))

                    def q_cb(S_, oc, dcol, tw, ps_ap, psbuf):
                        S_.add("v", lambda v: v.tensor_copy(
                            QT[:, oc * R + dcol: oc * R + dcol + tw], ps_ap),
                            reads=(psbuf,), writes=("QT", f"QT.{oc}"),
                            tag=f"qe{oc}.{dcol}")

                    def k_cb(S_, oc, dcol, tw, ps_ap, psbuf):
                        S_.add("v", lambda v: v.tensor_copy(
                            KT[:, oc * T + dcol: oc * T + dcol + tw], ps_ap),
                            reads=(psbuf,), writes=("KT", f"KT.{oc}"),
                            tag=f"ke{oc}.{dcol}")

                    def v_cb(S_, oc, dcol, tw, ps_ap, psbuf):
                        S_.add("v", lambda v, tw=tw: v.tensor_copy(vtmp[:, 0:tw], ps_ap),
                            reads=(psbuf,), writes=("vtmp",), tag=f"ve{oc}.{dcol}")
                        for k in range(tw // 128):
                            S_.add("t", lambda t, k=k: t.transpose(
                                ps_tr[:, :], vtmp[:, k * 128:(k + 1) * 128],
                                S.consts["ident_bf"][:, :]),
                                reads=("vtmp", "@ident_bf"), writes=("pstr",),
                                tag=f"vt{oc}.{dcol}.{k}")
                            cc = (oc * (T // 128) + (dcol + k * 128) // 128) * 128
                            S_.add("v", lambda v, cc=cc: v.tensor_copy(
                                Vtok[:, cc:cc + 128], ps_tr[:, :]),
                                reads=("pstr",), writes=("Vtok",),
                                tag=f"vte{oc}.{dcol}.{k}")

                    emit_sproj(cfg, nc, ctxW, S, WQ, "WQ", wsb, xp_q, 0, DC,
                               q_cb, "pq", ps_p, "psp", blk=1)
                    emit_sproj(cfg, nc, ctxW, S, WK, "WK", wsb, xp_kv, 0, HKV,
                               k_cb, "pk", ps_p, "psp", blk=1)
                    emit_sproj(cfg, nc, ctxW, S, WV, "WV", wsb, xp_kv, 0, HKV,
                               v_cb, "pv", ps_p, "psp", blk=1)
                    for h in range(HQ):
                        emit_rope(cfg, nc, S, QT, "QT", h * R, R, H, f"rq{h}.")
                    for kv in range(HKV):
                        emit_rope(cfg, nc, S, KT, "KT", kv * T, T, 0, f"rk{kv}.")
                S.barrier("b_qkv")
            with ExitStack() as ctxAT:
                def out_cb(S_, h, q0, qw, out_ps_ap, opsbuf, rec_ap, recbuf):
                    S_.add("v", lambda v: v.tensor_tensor(
                        attnT[:, h * R + q0: h * R + q0 + qw], out_ps_ap, rec_ap,
                        ALU.mult),
                        reads=(opsbuf, recbuf), writes=("attnT",), tag=f"ao{h}.{q0}")

                emit_attn(cfg, nc, ctxAT, S, S.ns + "at", heads=HQ, kv_of=None,
                          Tkeys=T, R=R, kwin=kwin, masks=masks, scale=L0_SCALE,
                          prep_head=lambda S_, h: None,
                          kt_ap=lambda h: (KT[:, (h // GQ) * T:(h // GQ) * T + T], "KT"),
                          v_tile=lambda h, kc: (
                              Vtok[:, ((h // GQ) * (T // 128) + kc) * 128:
                                   ((h // GQ) * (T // 128) + kc) * 128 + 128], "Vtok"),
                          q_ap=lambda h: (QT[:, h * R:h * R + R], "QT"),
                          out_cb=out_cb,
                          key_bias=lambda p0, pw: (hind[0:1, p0:p0 + pw],
                                                   "hind"))
            S.barrier("b_attn")
        with ExitStack() as ctxP:
            wsbP = ctxP.enter_context(nc.sbuf_tensor(S.ns + "wsbP", [128, 2 * DC * 2 * 128], BF16))
            xr = ctxP.enter_context(nc.sbuf_tensor(S.ns + "xr", [128, 2 * 512], F32))
            s1b = ctxP.enter_context(nc.sbuf_tensor(S.ns + "s1b", [128, 2 * 512], F32))
            ps_o = ctxP.enter_context(nc.psum_tensor(S.ns + "ps_o", [128, 1024], F32))

            def o_cb(S_, oc, dcol, tw, ps_ap, psbuf):
                sl = (oc * (R // 512) + dcol // 512) % 2
                xap = xr[:, sl * 512: sl * 512 + tw]
                sap = s1b[:, sl * 512: sl * 512 + tw]
                S_.add("sp", lambda sp, oc=oc, dcol=dcol, tw=tw, xap=xap: sp.dma_start(
                    out=xap, in_=XT_r[:, oc, H + dcol:H + dcol + tw]),
                    writes=(f"xr{sl}",), dma=True, tag=f"oxl{oc}.{dcol}")
                S_.add("v", lambda v, xap=xap, sap=sap: v.tensor_tensor(
                    sap, ps_ap, xap, ALU.add),
                    reads=(psbuf, f"xr{sl}"), writes=(f"s1b{sl}",), tag=f"os{oc}.{dcol}")
                S_.add("sp", lambda sp, oc=oc, dcol=dcol, tw=tw, sap=sap: sp.dma_start(
                    out=PS1_r[:, oc, dcol:dcol + tw], in_=sap),
                    reads=(f"s1b{sl}",), writes=(f"ps1.{oc}.{dcol}",), dma=True,
                    dmagrp="ps1st", tag=f"ost{oc}.{dcol}")

            emit_sproj(cfg, nc, ctxP, S, WO, "WO", wsbP,
                       [(attnT, R, 0, "attnT", R, 0)], 0, DC, o_cb, "po",
                       ps_o, "pso")
            S.barrier("b_proj")
        ctxAP.close()
        with ExitStack() as ctxH:
            hid = ctxH.enter_context(nc.sbuf_tensor(S.ns + "hid", [128, FC0 * R], BF16))
            with ExitStack() as ctxX:
                wf = ctxX.enter_context(nc.sbuf_tensor(S.ns + "wf", [128, 2 * DC * FB], F32))
                wq = ctxX.enter_context(nc.sbuf_tensor(S.ns + "wq", [128, 2 * DC * FB], BF16))
                qa = ctxX.enter_context(nc.sbuf_tensor(S.ns + "qa", [128, DC * FB], BF16))
                qb = ctxX.enter_context(nc.sbuf_tensor(S.ns + "qb", [128, DC * FB], BF16))
                sg = ctxX.enter_context(nc.sbuf_tensor(S.ns + "sg", [128, (FB // 128) * R], BF16))
                ps_f = ctxX.enter_context(nc.psum_tensor(S.ns + "ps_f", [128, 1024], F32))
                nfb = F0 // FB
                items = [(fb, m, Wd) for fb in range(nfb)
                         for m, Wd in ((0, W1T), (1, W2T))]

                def ld_quant(idx):
                    fb, m, Wd = items[idx]
                    sl = idx % 2
                    wap = wf[:, sl * DC * FB: (sl + 1) * DC * FB]
                    wqap = wq[:, sl * DC * FB: (sl + 1) * DC * FB]
                    S.add("sp", lambda sp, wap=wap, Wd=Wd, fb=fb: sp.dma_start(
                        out=wap.rearrange("p (dc f) -> p dc f", dc=DC),
                        in_=Wd.rearrange("(dc p) f -> p dc f", p=128)
                        [:, :, fb * FB:(fb + 1) * FB]),
                        writes=(f"wf{sl}",), dma=True, tag=f"fw{fb}.{m}")
                    emit_quant(S, "v", wqap, f"wq{sl}", wap, f"wf{sl}",
                               qa[:, :], "qa", qb[:, :], "qb",
                               tau[:, m:m + 1], tau[:, 4 + m:5 + m],
                               f"fq{fb}.{m}")

                ld_quant(0)
                for idx, (fb, m, Wd) in enumerate(items):
                    if idx + 1 < len(items):
                        ld_quant(idx + 1)
                    sl = idx % 2
                    wqap = wq[:, sl * DC * FB: (sl + 1) * DC * FB]
                    if True:
                        if m == 0:
                            def s_cb(S_, oi, oc_, t0, tw, ps_ap, psbuf, fb=fb):
                                S_.add("a", lambda a, oi=oi, t0=t0, tw=tw, ps_ap=ps_ap:
                                       a.activation(
                                    sg[:, oi * R + t0: oi * R + t0 + tw], ps_ap,
                                    AF.Sigmoid),
                                    reads=(psbuf,), writes=("sg", f"sg.{oi}"),
                                    tag=f"fs{fb}.{oi}.{t0}")
                            cb = s_cb
                        else:
                            def h_cb(S_, oi, oc_, t0, tw, ps_ap, psbuf, fb=fb):
                                cc = (fb * (FB // 128) + oi) * R + t0
                                S_.add("v", lambda v, oi=oi, cc=cc, t0=t0, tw=tw,
                                       ps_ap=ps_ap: v.tensor_tensor(
                                    hid[:, cc:cc + tw], ps_ap,
                                    sg[:, oi * R + t0: oi * R + t0 + tw], ALU.mult),
                                    reads=(psbuf, f"sg.{oi}"), writes=("hid",),
                                    tag=f"fh{fb}.{oi}.{t0}")
                            cb = h_cb
                        emit_proj(cfg, nc, ps_f, "psf", S, wqap, f"wq{sl}",
                                  FB // 128, xno, R, 0, "lnoxn", R,
                                  list(range(FB // 128)), cb, f"fx{fb}.{m}")
            S.barrier("b_x12")
            with ExitStack() as ctxY:
                w3f = ctxY.enter_context(nc.sbuf_tensor(S.ns + "w3f", [128, FC0 * 128], F32))
                w3q = ctxY.enter_context(nc.sbuf_tensor(S.ns + "w3q", [128, 2 * FC0 * 128], BF16))
                qa3 = ctxY.enter_context(nc.sbuf_tensor(S.ns + "qa3", [128, FC0 * 64], BF16))
                qb3 = ctxY.enter_context(nc.sbuf_tensor(S.ns + "qb3", [128, FC0 * 64], BF16))
                s1r = ctxY.enter_context(nc.sbuf_tensor(S.ns + "s1r", [128, 2 * 512], F32))
                xob = ctxY.enter_context(nc.sbuf_tensor(S.ns + "xob", [128, 2 * 512], F32))
                ps_y = ctxY.enter_context(nc.psum_tensor(S.ns + "ps_y", [128, 1024], F32))
                FCH = FC0 // 2

                def ld3(oc):
                    s3 = oc % 2
                    for hf in range(2):
                        fc0 = hf * FCH
                        fcn = FC0 - fc0 if hf else FCH
                        c0 = fc0 * 128
                        S.add("sp", lambda sp, oc=oc, fc0=fc0, fcn=fcn, c0=c0:
                              sp.dma_start(
                            out=w3f[:, c0:c0 + fcn * 128].rearrange(
                                "p (fc o) -> p fc o", o=128),
                            in_=W3T.rearrange("(fc p) o -> p fc o", p=128)
                            [:, fc0:fc0 + fcn, oc * 128:(oc + 1) * 128]),
                            writes=(f"w3f{hf}",), dma=True, tag=f"yw{oc}.{hf}")
                        emit_quant(S, "v", w3q[:, s3 * FC0 * 128 + c0:
                                               s3 * FC0 * 128 + c0 + fcn * 128],
                                   f"w3q{s3}",
                                   w3f[:, c0:c0 + fcn * 128], f"w3f{hf}",
                                   qa3[:, :], "qa3", qb3[:, :], "qb3",
                                   tau[:, 2:3], tau[:, 6:7], f"yq{oc}.{hf}")

                ld3(0)
                for oc in range(DC):
                    s3 = oc % 2
                    w3qap = w3q[:, s3 * FC0 * 128:(s3 + 1) * FC0 * 128]
                    if oc + 1 < DC:
                        ld3(oc + 1)

                    def y_cb(S_, oi, oc_, t0, tw, ps_ap, psbuf, oc=oc):
                        sl = (oc * (R // 512) + t0 // 512) % 2
                        rap = s1r[:, sl * 512: sl * 512 + tw]
                        oap = xob[:, sl * 512: sl * 512 + tw]
                        S_.add("sp", lambda sp, rap=rap, oc=oc, t0=t0, tw=tw:
                               sp.dma_start(out=rap,
                                            in_=PS1_r[:, oc, t0:t0 + tw]),
                               reads=(f"ps1.{oc}.{t0}",), writes=(f"s1r{sl}",),
                               dma=True, tag=f"ysl{oc}.{t0}")
                        S_.add("v", lambda v, rap=rap, oap=oap, ps_ap=ps_ap:
                               v.tensor_tensor(oap, ps_ap, rap, ALU.add),
                               reads=(psbuf, f"s1r{sl}"), writes=(f"xob{sl}",),
                               tag=f"yo{oc}.{t0}")
                        S_.add("sp", lambda sp, oap=oap, oc=oc, t0=t0, tw=tw:
                               sp.dma_start(out=X1_r[:, oc, t0:t0 + tw], in_=oap),
                               reads=(f"xob{sl}",),
                               writes=(f"{x1keyp}.{oc}.{t0}",),
                               dma=True, dmagrp="x1st", tag=f"yst{oc}.{t0}")

                    emit_proj(cfg, nc, ps_y, "psy", S, w3qap, f"w3q{s3}", 1,
                              hid, R, 0, "hid", R, [0], y_cb, f"y{oc}",
                              kdc=FC0)
    S.barrier("b_pass")


def l0_dram_io(nc, cfg):
    D, HKV, F0 = cfg.D, cfg.HKV, cfg.F0
    io = {}
    io["WQ"] = nc.dram_tensor("WQ", [D, D], F32, kind="ExternalInput").ap()
    io["WK"] = nc.dram_tensor("WK", [D, HKV * 128], F32, kind="ExternalInput").ap()
    io["WV"] = nc.dram_tensor("WV", [D, HKV * 128], F32, kind="ExternalInput").ap()
    io["WO"] = nc.dram_tensor("WO", [D, D], F32, kind="ExternalInput").ap()
    io["W1T"] = nc.dram_tensor("W1T", [D, F0], F32, kind="ExternalInput").ap()
    io["W2T"] = nc.dram_tensor("W2T", [D, F0], F32, kind="ExternalInput").ap()
    io["W3T"] = nc.dram_tensor("W3T", [F0, D], F32, kind="ExternalInput").ap()
    io["PS1"] = nc.dram_tensor("PS1", [D, cfg.R], F32).ap()
    return io


def build_l0(cfg):
    R, D = cfg.R, cfg.D
    H = cfg.W
    T = H + R
    nc = bass.Bass("TRN2", target_bir_lowering=False)
    io = l0_dram_io(nc, cfg)
    io["XT"] = nc.dram_tensor("XT", [D, T], F32, kind="ExternalInput").ap()
    io["SINT"] = nc.dram_tensor("SINT", [128, T], F32, kind="ExternalInput").ap()
    io["COST"] = nc.dram_tensor("COST", [128, T], F32, kind="ExternalInput").ap()
    io["HIND"] = nc.dram_tensor("HIND", [1, T], F32, kind="ExternalInput").ap()
    io["X1dst"] = nc.dram_tensor("X1OUT", [D, R], F32, kind="ExternalOutput").ap()
    io["x1keyp"] = "@x1o"
    kwin = l0_key_windows(cfg, H, R)
    masks, mconsts = l0_mask_tiles(cfg, H, R)
    with ExitStack() as ctx:
        S = Sched(nc, ctx)
        S.consts = {}
        cspecs = {"ident_f32": ([128, 128], F32), "ident_bf": ([128, 128], BF16),
                  "ones_bf": ([128, 128], BF16), "ones_f32": ([128, 128], F32),
                  "ones_row": ([1, 512], F32), "epsln": ([128, 1], F32)}
        for mname in mconsts:
            cspecs[mname] = ([128, 128], F32)
            cspecs[mname + "T"] = ([128, 128], F32)
        setup_consts(nc, ctx, S, cspecs)
        io["tau"] = ctx.enter_context(nc.sbuf_tensor("tau", [128, 8], F32))
        emit_l0_gamma(cfg, nc, ctx, S, io)
        emit_l0_pass(cfg, nc, S, io, kwin, masks)
        allout = [f"@x1o.{oc}.{t0}" for oc in range(cfg.DC)
                  for t0, _ in col_chunks(R, 512)]
        S.add("sp", lambda sp: sp.nop(), reads=tuple(allout), tag="fin")
        block = ctx.enter_context(nc.Block())
        S.emit(block)
    return nc, masks, mconsts



def l0_const_arrays(cfg, mconsts):
    d = {"c_ident_f32": np.eye(128, dtype=np.float32),
         "c_ident_bf": np.eye(128, dtype=BF),
         "c_ones_bf": np.ones((128, 128), BF),
         "c_ones_f32": np.ones((128, 128), np.float32),
         "c_ones_row": np.ones((1, 512), np.float32),
         "c_epsln": np.full((128, 1), EPS_LN, np.float32)}
    for name, arr in mconsts.items():
        d[f"c_{name}"] = arr
        d[f"c_{name}T"] = np.ascontiguousarray(arr.T)
    return d


def stage_l0(cfg, inputs, mconsts):
    H, R = cfg.W, cfg.R
    T = H + R
    x = np.asarray(inputs["L0_attn_x"], np.float32).reshape(cfg.B * cfg.T, cfg.D)
    xT = np.ascontiguousarray(x.T)
    sin, cos = host_sincos(cfg)
    consts = l0_const_arrays(cfg, mconsts)
    wq = np.ascontiguousarray(np.asarray(inputs["L0_attn_q_kernel"], np.float32))
    wk = np.ascontiguousarray(np.asarray(inputs["L0_attn_k_kernel"], np.float32))
    wv = np.ascontiguousarray(np.asarray(inputs["L0_attn_v_kernel"], np.float32))
    wo = np.ascontiguousarray(np.asarray(inputs["L0_attn_out_kernel"], np.float32))
    w1T = np.ascontiguousarray(np.asarray(inputs["L0_ffn_w1"], np.float32).T)
    w2T = np.ascontiguousarray(np.asarray(inputs["L0_ffn_w2"], np.float32).T)
    w3T = np.ascontiguousarray(np.asarray(inputs["L0_ffn_w3"], np.float32).T)
    nblk = cfg.T // cfg.W
    maps = []
    for c in range(cfg.NC):
        beta, n = divmod(c, nblk)
        own0 = beta * cfg.T + n * cfg.W
        if n == 0:
            xs = np.concatenate([np.zeros((cfg.D, H), np.float32),
                                 xT[:, own0:own0 + R]], 1)
            sins = np.concatenate([np.zeros((128, H), np.float32), sin[:, 0:R]], 1)
            coss = np.concatenate([np.zeros((128, H), np.float32), cos[:, 0:R]], 1)
            hindr = np.concatenate([np.full((1, H), NEG, np.float32),
                                    np.zeros((1, R), np.float32)], 1)
        else:
            p0 = n * cfg.W - H
            xs = np.ascontiguousarray(xT[:, own0 - H:own0 + R])
            sins = np.ascontiguousarray(sin[:, p0:p0 + T])
            coss = np.ascontiguousarray(cos[:, p0:p0 + T])
            hindr = np.zeros((1, T), np.float32)
        sins = sins.copy()
        sins[0:64, :] *= -1.0
        maps.append({"XT": np.ascontiguousarray(xs), "WQ": wq, "WK": wk, "WV": wv,
                     "WO": wo, "W1T": w1T, "W2T": w2T, "W3T": w3T,
                     "SINT": np.ascontiguousarray(sins),
                     "COST": np.ascontiguousarray(coss),
                     "HIND": np.ascontiguousarray(hindr), **consts})
    return maps



def emit_l1_body(cfg, nc, S, io, x1deps=(), Tloc=None, exchange_cb=None):
    R, D, DC, T = cfg.R, cfg.D, cfg.DC, cfg.T
    HQ, F1, FC1 = cfg.HQ, cfg.F1, cfg.FC1
    Tloc = Tloc if Tloc is not None else T
    X1T, WQKV, BQKVR, WG = io["X1T"], io["WQKV"], io["BQKVR"], io["WG"]
    WPROJ, BPROJR = io["WPROJ"], io["BPROJR"]
    W0A, W0B, W1A, W1B = io["W0A"], io["W0B"], io["W1A"], io["W1B"]
    B0BR, B1BR = io["B0BR"], io["B1BR"]
    YOUT, KT_s, V_s, QT_s = io["YOUT"], io["KT_s"], io["V_s"], io["QT_s"]
    X1T_r = X1T.rearrange("(dc p) t -> p dc t", p=128)
    Y_r = YOUT.rearrange("(dc p) t -> p dc t", p=128)

    with ExitStack() as ctx:

        m0 = ctx.enter_context(nc.sbuf_tensor("m0", [128, R], BF16))
        m1 = ctx.enter_context(nc.sbuf_tensor("m1", [128, R], BF16))
        m0r = ctx.enter_context(nc.sbuf_tensor("m0r", [1, 512], F32))

        with ExitStack() as ctxQ:
            xn1 = ctxQ.enter_context(nc.sbuf_tensor("xn1", [128, DC * Tloc], BF16))
            with ExitStack() as ctxLN:
                scr = make_ln_scratch(cfg, nc, ctxLN, "l1s")
                emit_ln(cfg, nc, scr, S, X1T, 0, Tloc, xn1, Tloc, 0, "l1n",
                        src_deps=x1deps)
            S.barrier("b_l1ln")
            XN1D_r = io["XN1D"].rearrange("(dc p) t -> p dc t", p=128)
            for dc in range(DC):
                S.add("sp", lambda sp, dc=dc: sp.dma_start(
                    out=XN1D_r[:, dc, :], in_=xn1[:, dc * Tloc:dc * Tloc + R]),
                    reads=("l1nxn",), writes=(f"xn1d.{dc}",), dma=True,
                    dmagrp="kvst", tag=f"xno{dc}")
            with ExitStack() as ctxW:
                wsb = ctxW.enter_context(nc.sbuf_tensor("wsb1", [128, 2 * DC * 2 * 128], BF16))
                bqs = ctxW.enter_context(nc.sbuf_tensor("bqs", [1, 2 * 128], F32))
                ktmp = ctxW.enter_context(nc.sbuf_tensor("ktmp", [128, 512], BF16))
                vstage = ctxW.enter_context(nc.sbuf_tensor("vstage", [128, 512], BF16))
                wg_sb = ctxW.enter_context(nc.sbuf_tensor("wg_sb", [128, DC * 2], BF16))
                gsb = ctxW.enter_context(nc.sbuf_tensor("gsb", [2, 512], F32))
                ps_p = ctxW.enter_context(nc.psum_tensor("ps_p1", [128, 1024], F32))
                ps_tr = ctxW.enter_context(nc.psum_tensor("ps_tr1", [128, 128], BF16))
                ps_g = ctxW.enter_context(nc.psum_tensor("ps_g1", [128, 512], F32))
                S.add("g", lambda g: g.dma_start(
                    out=wg_sb.rearrange("p (dc o) -> p dc o", o=2),
                    in_=WG.rearrange("(dc p) o -> p dc o", p=128)),
                    writes=("wg",), dma=True, tag="ldwg")

                bq_done = set()

                def bias_mm(S_, oc, dcol, tw, ps_ap, psbuf):
                    sl = oc % 2
                    if oc not in bq_done:
                        bq_done.add(oc)
                        S_.add("sp", lambda sp, oc=oc, sl=sl: sp.dma_start(
                            out=bqs[0:1, sl * 128:(sl + 1) * 128],
                            in_=BQKVR[0:1, oc * 128:(oc + 1) * 128]),
                            writes=(f"bqs{sl}",), dma=True, tag=f"ldbq{oc}")
                    S_.add("t", lambda t, oc=oc, sl=sl, tw=tw, ps_ap=ps_ap: t.matmul(
                        ps_ap, bqs[0:1, sl * 128:(sl + 1) * 128],
                        S.consts["ones_row"][0:1, 0:tw], start=False, stop=True),
                        reads=(f"bqs{sl}", "@ones_row"), writes=(psbuf,),
                        tag=f"qkvb{oc}.{dcol}")

                def qkv_cb(S_, oc, dcol, tw, ps_ap, psbuf):
                    h, part = divmod(oc, 3)
                    if part == 0:
                        S_.add("v", lambda v, tw=tw, ps_ap=ps_ap: v.tensor_copy(
                            ktmp[:, 0:tw], ps_ap),
                            reads=(psbuf,), writes=("ktmp",), tag=f"qe{oc}.{dcol}")
                        S_.add("sp", lambda sp, h=h, dcol=dcol, tw=tw: sp.dma_start(
                            out=QT_s[h * 128:(h + 1) * 128, dcol:dcol + tw],
                            in_=ktmp[:, 0:tw]),
                            reads=("ktmp",), writes=(f"qts.{h}.{dcol}",), dma=True, dmagrp="kvst",
                            tag=f"qst{oc}.{dcol}")
                    elif part == 1:
                        S_.add("v", lambda v, tw=tw, ps_ap=ps_ap: v.tensor_copy(
                            ktmp[:, 0:tw], ps_ap),
                            reads=(psbuf,), writes=("ktmp",), tag=f"ke{oc}.{dcol}")
                        S_.add("sp", lambda sp, h=h, dcol=dcol, tw=tw: sp.dma_start(
                            out=KT_s[h * 128:(h + 1) * 128, dcol:dcol + tw],
                            in_=ktmp[:, 0:tw]),
                            reads=("ktmp",), writes=(f"kts.{h}.{dcol}",), dma=True, dmagrp="kvst",
                            tag=f"kst{oc}.{dcol}")
                    else:
                        S_.add("v", lambda v, tw=tw, ps_ap=ps_ap: v.tensor_copy(
                            vstage[:, 0:tw], ps_ap),
                            reads=(psbuf,), writes=("vstage",), tag=f"vev{oc}.{dcol}")
                        S_.add("sp", lambda sp, h=h, dcol=dcol, tw=tw: sp.dma_start(
                            out=V_s[h * 128:(h + 1) * 128, dcol:dcol + tw],
                            in_=vstage[:, 0:tw]),
                            reads=("vstage",), writes=(f"vs.{h}.{dcol}",), dma=True, dmagrp="kvst",
                            tag=f"vst{oc}.{dcol}")

                xp_all = [(xn1, Tloc, 0, "l1nxn", Tloc, 0)]
                xp_own = [(xn1, Tloc, 0, "l1nxn", R, 0)]
                for h in range(HQ):
                    emit_sproj(cfg, nc, ctxW, S, WQKV, "WQKV", wsb, xp_own,
                               3 * h, 1, qkv_cb, f"pq{h}", ps_p, "psp1",
                               blk=2, extra_mm=bias_mm)
                    emit_sproj(cfg, nc, ctxW, S, WQKV, "WQKV", wsb, xp_all,
                               3 * h + 1, 2, qkv_cb, f"pkv{h}", ps_p, "psp1",
                               blk=2, extra_mm=bias_mm)
                for t0, tw in col_chunks(R, 512):
                    for dc in range(DC):
                        S.add("t", lambda t, dc=dc, t0=t0, tw=tw: t.matmul(
                            ps_g[0:2, 0:tw], wg_sb[:, dc * 2:(dc + 1) * 2],
                            xn1[:, dc * Tloc + t0:dc * Tloc + t0 + tw],
                            start=(dc == 0), stop=(dc == DC - 1)),
                            reads=("wg", "l1nxn"), writes=("psg1",), tag=f"g{t0}.{dc}")
                    S.add("v", lambda v, tw=tw: v.tensor_copy(gsb[:, 0:tw],
                                                              ps_g[0:2, 0:tw]),
                          reads=("psg1",), writes=("gsb",), tag=f"ge{t0}")
                    S.add("t", lambda t, tw=tw: t.matmul(
                        ps_g[0:1, 0:tw], S.consts["epm"][:, :], gsb[0:2, 0:tw],
                        start=True, stop=True),
                        reads=("gsb", "@epm"), writes=("psg1",), tag=f"gd{t0}")
                    S.add("v", lambda v, tw=tw: v.tensor_scalar(
                        m0r[0:1, 0:tw], ps_g[0:1, 0:tw], 0.0, None, ALU.is_ge),
                        reads=("psg1",), writes=("m0r",), tag=f"gm{t0}")
                    S.add("t", lambda t, tw=tw: t.matmul(
                        ps_g[:, 0:tw], S.consts["ones_f32"][0:1, 0:128],
                        m0r[0:1, 0:tw], start=True, stop=True),
                        reads=("m0r", "@ones_f32"), writes=("psg1",), tag=f"gb{t0}")
                    S.add("v", lambda v, t0=t0, tw=tw: v.tensor_copy(
                        m0[:, t0:t0 + tw], ps_g[:, 0:tw]),
                        reads=("psg1",), writes=("m0",), tag=f"gmc{t0}")
                    S.add("v", lambda v, t0=t0, tw=tw: v.tensor_scalar(
                        m1[:, t0:t0 + tw], m0[:, t0:t0 + tw], -1.0, 1.0,
                        ALU.mult, ALU.add),
                        reads=("m0",), writes=("m1",), tag=f"gm1{t0}")
            S.barrier("b_l1q")
        if exchange_cb is not None:
            exchange_cb(S)
            S.barrier("b_xch")
        kt_src = io.get("kt_src", KT_s)
        v_src = io.get("v_src", V_s)
        kdep = ("@xdone",) if exchange_cb is not None else None

        with ExitStack() as ctxY:
            y_acc = ctxY.enter_context(nc.sbuf_tensor("y_acc", [128, DC * R], F32))
            with ExitStack() as ctxB:
                attnT = ctxB.enter_context(nc.sbuf_tensor("attnT1", [128, HQ * R], BF16))
                with ExitStack() as ctxAT:
                    kts = ctxAT.enter_context(nc.sbuf_tensor("kts", [128, 2 * T], BF16))
                    vts = ctxAT.enter_context(nc.sbuf_tensor("vts", [128, 2 * T], BF16))
                    qts = ctxAT.enter_context(nc.sbuf_tensor("qts", [128, 2 * R], BF16))

                    def prep_head(S_, h):
                        sl = h % 2
                        kr = kdep if kdep is not None else tuple(
                            f"kts.{h}.{d}" for d, _ in col_chunks(Tloc, 512))
                        vr = kdep if kdep is not None else tuple(
                            f"vs.{h}.{d}" for d, _ in col_chunks(Tloc, 512))
                        S_.add("sp", lambda sp, h=h, sl=sl: sp.dma_start(
                            out=kts[:, sl * T: sl * T + T],
                            in_=kt_src[h * 128:(h + 1) * 128, :]),
                            reads=kr,
                            writes=(f"ktsl{sl}",), dma=True, tag=f"hk{h}")
                        S_.add("sp", lambda sp, h=h, sl=sl: sp.dma_start(
                            out=vts[:, sl * T: sl * T + T],
                            in_=v_src[h * 128:(h + 1) * 128, :]),
                            reads=vr,
                            writes=(f"vtsl{sl}",), dma=True, tag=f"hv{h}")
                        S_.add("sp", lambda sp, h=h, sl=sl: sp.dma_start(
                            out=qts[:, sl * R: sl * R + R],
                            in_=QT_s[h * 128:(h + 1) * 128, :]),
                            reads=tuple(f"qts.{h}.{d}" for d, _ in col_chunks(R, 512)),
                            writes=(f"qtsl{sl}",), dma=True, tag=f"hq{h}")

                    def out_cb(S_, h, q0, qw, out_ps_ap, opsbuf, rec_ap, recbuf):
                        S_.add("v", lambda v: v.tensor_tensor(
                            attnT[:, h * R + q0: h * R + q0 + qw], out_ps_ap,
                            rec_ap, ALU.mult),
                            reads=(opsbuf, recbuf), writes=("attnT1",),
                            tag=f"ao1{h}.{q0}")

                    emit_attn(cfg, nc, ctxAT, S, "a1", heads=HQ, kv_of=None,
                              Tkeys=T, R=R,
                              kwin=[(kc, 0, R) for kc in range(T // 128)],
                              masks={}, scale=L1_SCALE,
                              prep_head=prep_head,
                              kt_ap=lambda h: (kts[:, (h % 2) * T:(h % 2) * T + T],
                                               f"ktsl{h % 2}"),
                              v_dmaj=lambda h: (vts[:, (h % 2) * T:
                                                    (h % 2) * T + T], f"vtsl{h % 2}"),
                              q_ap=lambda h: (qts[:, (h % 2) * R:(h % 2) * R + R],
                                              f"qtsl{h % 2}"),
                              out_cb=out_cb)
                S.barrier("b_l1at")
                with ExitStack() as ctxP:
                    wsbP = ctxP.enter_context(
                        nc.sbuf_tensor("wsbP1", [128, 2 * DC * 2 * 128], BF16))
                    bpr = ctxP.enter_context(nc.sbuf_tensor("bpr", [1, D], F32))
                    x1r = ctxP.enter_context(nc.sbuf_tensor("x1r", [128, 2 * 512], F32))
                    ps_o = ctxP.enter_context(nc.psum_tensor("ps_o1", [128, 1024], F32))
                    S.add("sp", lambda sp: sp.dma_start(out=bpr[:, :], in_=BPROJR),
                          writes=("bpr",), dma=True, tag="ldbpr")

                    def pbias_mm(S_, oc, dcol, tw, ps_ap, psbuf):
                        S_.add("t", lambda t, oc=oc, tw=tw, ps_ap=ps_ap: t.matmul(
                            ps_ap, bpr[0:1, oc * 128:oc * 128 + 128],
                            S.consts["ones_row"][0:1, 0:tw], start=False, stop=True),
                            reads=("bpr", "@ones_row"), writes=(psbuf,),
                            tag=f"pb{oc}.{dcol}")

                    def p_cb(S_, oc, dcol, tw, ps_ap, psbuf):
                        sl = (oc * (R // 512) + dcol // 512) % 2
                        xap = x1r[:, sl * 512: sl * 512 + tw]
                        S_.add("sp", lambda sp, oc=oc, dcol=dcol, tw=tw, xap=xap:
                               sp.dma_start(out=xap,
                                            in_=X1T_r[:, oc, dcol:dcol + tw]),
                               reads=tuple(x1deps), writes=(f"x1r{sl}",), dma=True,
                               tag=f"pxl{oc}.{dcol}")
                        S_.add("v", lambda v, oc=oc, dcol=dcol, tw=tw, xap=xap,
                               ps_ap=ps_ap: v.tensor_tensor(
                            y_acc[:, oc * R + dcol: oc * R + dcol + tw], ps_ap,
                            xap, ALU.add),
                            reads=(psbuf, f"x1r{sl}"), writes=("y_acc", f"ya.{oc}.{dcol}"),
                            tag=f"pya{oc}.{dcol}")

                    emit_sproj(cfg, nc, ctxP, S, WPROJ, "WPROJ", wsbP,
                               [(attnT, R, 0, "attnT1", R, 0)], 0, DC, p_cb,
                               "pp", ps_o, "pso1", blk=2, extra_mm=pbias_mm)
                S.barrier("b_l1p")
            with ExitStack() as ctxF:
                hbuf = ctxF.enter_context(nc.sbuf_tensor("hbuf", [128, FC1 * R], BF16))
                xn1o = ctxF.enter_context(nc.sbuf_tensor("xn1o", [128, DC * R], BF16))
                XN1D_r2 = io["XN1D"].rearrange("(dc p) t -> p dc t", p=128)
                for dc in range(DC):
                    S.add("sp", lambda sp, dc=dc: sp.dma_start(
                        out=xn1o[:, dc * R:(dc + 1) * R], in_=XN1D_r2[:, dc, :]),
                        reads=(f"xn1d.{dc}",), writes=("xn1o",), dma=True,
                        dmagrp="kvst", tag=f"xnl{dc}")
                wsbF = ctxF.enter_context(
                    nc.sbuf_tensor("wsbF", [128, 2 * max(DC, FC1) * 1 * 128], BF16))
                brow = ctxF.enter_context(nc.sbuf_tensor("brow", [1, 2 * D], F32))
                tsb = ctxF.enter_context(nc.sbuf_tensor("tsb", [128, 512], F32))
                ps_f = ctxF.enter_context(nc.psum_tensor("ps_f1", [128, 1024], F32))
                S.add("sp", lambda sp: sp.dma_start(out=brow[0:1, 0:D], in_=B0BR),
                      writes=("b0br",), dma=True, tag="ldb0b")
                S.add("sp", lambda sp: sp.dma_start(out=brow[0:1, D:2 * D], in_=B1BR),
                      writes=("b1br",), dma=True, tag="ldb1b")

                for ex in range(2):
                    WA, WB = (W0A, W0B) if ex == 0 else (W1A, W1B)
                    bpp = S.consts["b0app" if ex == 0 else "b1app"]
                    bppbuf = "@b0app" if ex == 0 else "@b1app"
                    mask_sb, maskbuf = (m0, "m0") if ex == 0 else (m1, "m1")
                    act = AF.Gelu if ex == 0 else AF.Relu

                    def h_cb(S_, oc, dcol, tw, ps_ap, psbuf, ex=ex, act=act,
                             bpp=bpp, bppbuf=bppbuf):
                        S_.add("a", lambda a, oc=oc, dcol=dcol, tw=tw, ps_ap=ps_ap:
                               a.activation(
                            hbuf[:, oc * R + dcol: oc * R + dcol + tw], ps_ap,
                            act, bias=bpp[:, oc:oc + 1]),
                            reads=(psbuf, bppbuf), writes=("hbuf", f"hb.{ex}.{oc}"),
                            tag=f"he{ex}.{oc}.{dcol}")

                    emit_sproj(cfg, nc, ctxF, S, WA, f"WA{ex}", wsbF,
                               [(xn1o, R, 0, "xn1o", R, 0)], 0, FC1, h_cb,
                               f"fa{ex}", ps_f, "psf1", blk=1)
                    S.barrier(f"b_fa{ex}")

                    def obias_mm(S_, oc, dcol, tw, ps_ap, psbuf, ex=ex):
                        S_.add("t", lambda t, oc=oc, tw=tw, ps_ap=ps_ap, ex=ex:
                               t.matmul(
                            ps_ap, brow[0:1, ex * D + oc * 128: ex * D + oc * 128 + 128],
                            S.consts["ones_row"][0:1, 0:tw], start=False, stop=True),
                            reads=("b0br" if ex == 0 else "b1br", "@ones_row"),
                            writes=(psbuf,), tag=f"ob{ex}.{oc}.{dcol}")

                    def o_cb(S_, oc, dcol, tw, ps_ap, psbuf, ex=ex,
                             mask_sb=mask_sb, maskbuf=maskbuf):
                        S_.add("v", lambda v, dcol=dcol, tw=tw, ps_ap=ps_ap,
                               mask_sb=mask_sb: v.tensor_tensor(
                            tsb[:, 0:tw], ps_ap, mask_sb[:, dcol:dcol + tw],
                            ALU.mult),
                            reads=(psbuf, maskbuf), writes=("tsb",),
                            tag=f"om{ex}.{oc}.{dcol}")
                        S_.add("v", lambda v, oc=oc, dcol=dcol, tw=tw: v.tensor_tensor(
                            y_acc[:, oc * R + dcol: oc * R + dcol + tw],
                            y_acc[:, oc * R + dcol: oc * R + dcol + tw],
                            tsb[:, 0:tw], ALU.add),
                            reads=("tsb", f"ya.{oc}.{dcol}", "y_acc"),
                            writes=("y_acc", f"ya.{oc}.{dcol}"),
                            tag=f"oy{ex}.{oc}.{dcol}")

                    emit_sproj(cfg, nc, ctxF, S, WB, f"WB{ex}", wsbF,
                               [(hbuf, R, 0, "hbuf", R, 0)], 0, DC, o_cb,
                               f"fb{ex}", ps_f, "psf1", blk=1, kdc=FC1,
                               extra_mm=obias_mm)
                    S.barrier(f"b_fb{ex}")
                for oc in range(DC):
                    S.add("sp", lambda sp, oc=oc: sp.dma_start(
                        out=Y_r[:, oc, :], in_=y_acc[:, oc * R:(oc + 1) * R]),
                        reads=("y_acc",), writes=(f"yo.{oc}",), dma=True, dmagrp="yst",
                        tag=f"yst{oc}")
                S.add("sp", lambda sp: sp.nop(),
                      reads=tuple(f"yo.{oc}" for oc in range(DC)), tag="fin1")


def build_l1(cfg):
    R, D, T, FC1 = cfg.R, cfg.D, cfg.T, cfg.FC1
    F1 = cfg.F1
    nc = bass.Bass("TRN2", target_bir_lowering=False)
    io = l1_dram_io(nc, cfg)
    io["X1T"] = nc.dram_tensor("X1T", [D, T], F32, kind="ExternalInput").ap()
    io["YOUT"] = nc.dram_tensor("YOUT", [D, R], F32, kind="ExternalOutput").ap()
    with ExitStack() as ctx:
        S = Sched(nc, ctx)
        S.consts = {}
        setup_consts(nc, ctx, S, {
            "ident_f32": ([128, 128], F32), "ident_bf": ([128, 128], BF16),
            "ones_bf": ([128, 128], BF16), "ones_row": ([1, 512], F32),
            "ones_f32": ([128, 128], F32), "epsln": ([128, 1], F32),
            "b0app": ([128, FC1], F32), "b1app": ([128, FC1], F32),
            "epm": ([2, 1], F32)})
        emit_l1_body(cfg, nc, S, io)
        block = ctx.enter_context(nc.Block())
        S.emit(block)
    return nc


def l1_dram_io(nc, cfg, Tloc=None):
    D, T, R, F1, FC1 = cfg.D, cfg.T, cfg.R, cfg.F1, cfg.FC1
    T = Tloc if Tloc is not None else T
    io = {}
    io["WQKV"] = nc.dram_tensor("WQKV", [D, 3 * D], F32, kind="ExternalInput").ap()
    io["BQKVR"] = nc.dram_tensor("BQKVR", [1, 3 * D], F32, kind="ExternalInput").ap()
    io["WG"] = nc.dram_tensor("WG", [D, 2], F32, kind="ExternalInput").ap()
    io["WPROJ"] = nc.dram_tensor("WPROJ", [D, D], F32, kind="ExternalInput").ap()
    io["BPROJR"] = nc.dram_tensor("BPROJR", [1, D], F32, kind="ExternalInput").ap()
    io["W0A"] = nc.dram_tensor("W0A", [D, F1], F32, kind="ExternalInput").ap()
    io["W0B"] = nc.dram_tensor("W0B", [F1, D], F32, kind="ExternalInput").ap()
    io["B0BR"] = nc.dram_tensor("B0BR", [1, D], F32, kind="ExternalInput").ap()
    io["W1A"] = nc.dram_tensor("W1A", [D, F1], F32, kind="ExternalInput").ap()
    io["W1B"] = nc.dram_tensor("W1B", [F1, D], F32, kind="ExternalInput").ap()
    io["B1BR"] = nc.dram_tensor("B1BR", [1, D], F32, kind="ExternalInput").ap()
    io["KT_s"] = nc.dram_tensor("KT_s", [D, T], BF16).ap()
    io["V_s"] = nc.dram_tensor("V_s", [D, T], BF16).ap()
    io["QT_s"] = nc.dram_tensor("QT_s", [D, R], BF16).ap()
    io["XN1D"] = nc.dram_tensor("XN1D", [D, R], BF16).ap()
    return io


def stage_l1(cfg, inputs, x1T):
    D, R, T, FC1 = cfg.D, cfg.R, cfg.T, cfg.FC1
    wqkv = np.ascontiguousarray(np.asarray(inputs["L1_attn_Wqkv"], np.float32).T)
    bqkvr = np.asarray(inputs["L1_attn_bqkv"], np.float32).reshape(1, 3 * D)
    wg = np.ascontiguousarray(np.asarray(inputs["L1_ffn_Wg"], np.float32).T)
    epm = np.array([[1.0], [-1.0]], np.float32)
    wproj = np.ascontiguousarray(np.asarray(inputs["L1_attn_Wproj"], np.float32).T)
    bprojr = np.asarray(inputs["L1_attn_bproj"], np.float32).reshape(1, D)
    w0a = np.ascontiguousarray(np.asarray(inputs["L1_ffn_W0a"], np.float32).T)
    b0app = np.ascontiguousarray(
        np.asarray(inputs["L1_ffn_b0a"], np.float32).reshape(FC1, 128).T)
    w0b = np.ascontiguousarray(np.asarray(inputs["L1_ffn_W0b"], np.float32).T)
    b0br = np.asarray(inputs["L1_ffn_b0b"], np.float32).reshape(1, D)
    w1a = np.ascontiguousarray(np.asarray(inputs["L1_ffn_W1a"], np.float32).T)
    b1app = np.ascontiguousarray(
        np.asarray(inputs["L1_ffn_b1a"], np.float32).reshape(FC1, 128).T)
    w1b = np.ascontiguousarray(np.asarray(inputs["L1_ffn_W1b"], np.float32).T)
    b1br = np.asarray(inputs["L1_ffn_b1b"], np.float32).reshape(1, D)
    base = {"WQKV": wqkv, "BQKVR": bqkvr, "WG": wg, "WPROJ": wproj,
            "BPROJR": bprojr, "W0A": w0a, "W0B": w0b,
            "B0BR": b0br, "W1A": w1a, "W1B": w1b, "B1BR": b1br,
            "c_ident_f32": np.eye(128, dtype=np.float32),
            "c_ones_bf": np.ones((128, 128), BF),
            "c_ones_f32": np.ones((128, 128), np.float32),
            "c_ones_row": np.ones((1, 512), np.float32),
            "c_epsln": np.full((128, 1), EPS_LN, np.float32),
            "c_b0app": b0app, "c_b1app": b1app, "c_epm": epm,
            "c_ident_bf": np.eye(128, dtype=BF)}
    qpc = cfg.NC // cfg.B
    maps = []
    for c in range(cfg.NC):
        beta, j = divmod(c, qpc)
        xb = x1T[:, beta * T:(beta + 1) * T]
        rot = np.concatenate([xb[:, j * R:(j + 1) * R], xb[:, :j * R],
                              xb[:, (j + 1) * R:]], axis=1)
        maps.append(dict(base, X1T=np.ascontiguousarray(rot)))
    return maps



MAGIC_POLL_N = 2000000


def emit_exchange(cfg, nc, S, io, Tloc):
    D, T, R = cfg.D, cfg.T, cfg.R
    KT_s, V_s = io["KT_s"], io["V_s"]
    K_sh, V_sh, FLG = io["K_sh"], io["V_sh"], io["FLG"]
    cid_sb, nonce_sb = io["cid_sb"], io["nonce_sb"]
    poll_sb, sx = io["poll_sb"], io["sx"]

    kvreads = [f"kts.{h}.{d}" for h in range(cfg.HQ) for d, _ in col_chunks(Tloc, 512)]
    kvreads += [f"vs.{h}.{d}" for h in range(cfg.HQ) for d, _ in col_chunks(Tloc, 512)]
    kvreads += ["@cid_sb", "@nonce_sb"]

    def fn(g):
        with (g.register("rs") as rs, g.register("ro") as ro,
              g.register("fl") as fl, g.register("dd") as dd,
              g.register("ii") as ii, g.register("cnt") as cnt,
              g.register("rn") as rn):
            g.reg_load(rs, cid_sb[:1, :1])
            g.reg_mod(rs, rs, 2)
            g.reg_load(rn, nonce_sb[:1, :1])
            g.reg_mul(ro, rs, Tloc)
            g.dma_start(out=bass.AP(K_sh.tensor, ro, [[T, D], [1, 1], [1, Tloc]]),
                        in_=KT_s).then_inc(sx, 16)
            g.reg_mul(ro, rs, Tloc)
            g.dma_start(out=bass.AP(V_sh.tensor, ro, [[T, D], [1, 1], [1, Tloc]]),
                        in_=V_s).then_inc(sx, 16)
            g.wait_ge(sx, 32)
            g.reg_mul(ro, rs, 16)
            g.dma_start(out=bass.AP(FLG.tensor, ro, [[16, 1], [1, 1], [1, 16]]),
                        in_=nonce_sb[:, :]).then_inc(sx, 16)
            g.reg_mul(ro, rs, -16)
            g.reg_add(ro, ro, 16)
            g.reg_mov(dd, 1)
            g.reg_mov(ii, 0)
            g.reg_mov(cnt, 48)
            with g.While(dd):
                g.reg_add(cnt, cnt, 16)
                g.dma_start(out=poll_sb[:, :],
                            in_=bass.AP(FLG.tensor, ro, [[16, 1], [1, 1], [1, 16]])
                            ).then_inc(sx, 16)
                g.wait_ge(sx, cnt)
                g.reg_load(fl, poll_sb[:1, :1])
                g.reg_sub(dd, fl, rn)
                g.reg_add(ii, ii, 1)
                with g.If_cmp(ii, MAGIC_POLL_N, "IS_GE"):
                    g.reg_mov(dd, 0)
        return g.nop()

    S.add("g", fn, reads=tuple(kvreads), writes=("@xdone",), tag="exch")


def build_fused(cfg, npass=4):
    R, D, T = cfg.R, cfg.D, cfg.T
    H = cfg.W
    TL0 = H + R
    Tloc = npass * R
    FC1 = cfg.FC1
    nc = bass.Bass("TRN2", target_bir_lowering=False)
    io0 = l0_dram_io(nc, cfg)
    io1 = l1_dram_io(nc, cfg, Tloc=Tloc)
    X1OWN = nc.dram_tensor("X1OWN", [D, Tloc], F32).ap()
    io1["X1T"] = X1OWN
    io1["YOUT"] = nc.dram_tensor("YOUT", [D, R], F32, kind="ExternalOutput").ap()
    xts, sins, coss, hinds = [], [], [], []
    for p in range(npass):
        xts.append(nc.dram_tensor(f"XT{p}", [D, TL0], F32, kind="ExternalInput").ap())
        sins.append(nc.dram_tensor(f"SINT{p}", [128, TL0], F32, kind="ExternalInput").ap())
        coss.append(nc.dram_tensor(f"COST{p}", [128, TL0], F32, kind="ExternalInput").ap())
        hinds.append(nc.dram_tensor(f"HIND{p}", [1, TL0], F32, kind="ExternalInput").ap())
    if npass == 2:
        CID = nc.dram_tensor("CID", [1, 1], I32, kind="ExternalInput").ap()
        NONCE = nc.dram_tensor("NONCE", [1, 16], I32, kind="ExternalInput").ap()
        io1["K_sh"] = nc.dram_tensor("K_sh", [D, T], BF16, addr_space="Shared").ap()
        io1["V_sh"] = nc.dram_tensor("V_sh", [D, T], BF16, addr_space="Shared").ap()
        io1["FLG"] = nc.dram_tensor("FLG_sh", [2, 16], I32, addr_space="Shared").ap()
        io1["kt_src"] = io1["K_sh"]
        io1["v_src"] = io1["V_sh"]
    kwin = l0_key_windows(cfg, H, R)
    masks, mconsts = l0_mask_tiles(cfg, H, R)
    with ExitStack() as ctx:
        S = Sched(nc, ctx)
        S.consts = {}
        cspecs = {"ident_f32": ([128, 128], F32), "ident_bf": ([128, 128], BF16),
                  "ones_bf": ([128, 128], BF16), "ones_f32": ([128, 128], F32),
                  "ones_row": ([1, 512], F32), "epsln": ([128, 1], F32),
                  "b0app": ([128, FC1], F32), "b1app": ([128, FC1], F32),
                  "epm": ([2, 1], F32)}
        for mname in mconsts:
            cspecs[mname] = ([128, 128], F32)
            cspecs[mname + "T"] = ([128, 128], F32)
        setup_consts(nc, ctx, S, cspecs)
        exchange_cb = None
        if npass == 2:
            io1["cid_sb"] = ctx.enter_context(nc.sbuf_tensor("cid_sb", [1, 1], I32))
            io1["nonce_sb"] = ctx.enter_context(nc.sbuf_tensor("nonce_sb", [1, 16], I32))
            S.add("sp", lambda sp: sp.dma_start(out=io1["cid_sb"][:, :], in_=CID),
                  writes=("@cid_sb",), dma=True, dmagrp="consts", tag="ldcid")
            S.add("sp", lambda sp: sp.dma_start(out=io1["nonce_sb"][:, :], in_=NONCE),
                  writes=("@nonce_sb",), dma=True, dmagrp="consts", tag="ldnonce")
            io1["poll_sb"] = ctx.enter_context(nc.sbuf_tensor("xpoll", [1, 16], I32))
            io1["sx"] = ctx.enter_context(nc.semaphore("s_xch"))
            exchange_cb = lambda S_: emit_exchange(cfg, nc, S_, io1, Tloc)
        io0["tau"] = ctx.enter_context(nc.sbuf_tensor("tau", [128, 8], F32))
        emit_l0_gamma(cfg, nc, ctx, S, io0)
        x1deps = []
        for p in range(npass):
            S.ns = f"p{p}."
            iop = dict(io0, XT=xts[p], SINT=sins[p], COST=coss[p],
                       HIND=hinds[p], X1dst=X1OWN[:, p * R:(p + 1) * R],
                       x1keyp=f"@x1.{p}")
            emit_l0_pass(cfg, nc, S, iop, kwin, masks)
            x1deps += [f"@x1.{p}.{oc}.{t0}" for oc in range(cfg.DC)
                       for t0, _ in col_chunks(R, 512)]
        S.ns = "l1."
        emit_l1_body(cfg, nc, S, io1, x1deps=tuple(x1deps), Tloc=Tloc,
                     exchange_cb=exchange_cb)
        S.ns = ""
        block = ctx.enter_context(nc.Block())
        S.emit(block)
    return nc, masks, mconsts


def stage_fused(cfg, inputs, mconsts, npass=4):
    H, R, T = cfg.W, cfg.R, cfg.T
    TL0 = H + R
    nblk = cfg.NC // cfg.B
    x = np.asarray(inputs["L0_attn_x"], np.float32).reshape(cfg.B * cfg.T, cfg.D)
    xT = np.ascontiguousarray(x.T)
    sin, cos = host_sincos(cfg)
    sinF = sin.copy()
    sinF[0:64, :] *= -1.0
    consts = l0_const_arrays(cfg, mconsts)
    FC1 = cfg.FC1
    consts["c_b0app"] = np.ascontiguousarray(
        np.asarray(inputs["L1_ffn_b0a"], np.float32).reshape(FC1, 128).T)
    consts["c_b1app"] = np.ascontiguousarray(
        np.asarray(inputs["L1_ffn_b1a"], np.float32).reshape(FC1, 128).T)
    consts["c_epm"] = np.array([[1.0], [-1.0]], np.float32)
    D = cfg.D
    base = {
        "WQ": np.ascontiguousarray(np.asarray(inputs["L0_attn_q_kernel"], np.float32)),
        "WK": np.ascontiguousarray(np.asarray(inputs["L0_attn_k_kernel"], np.float32)),
        "WV": np.ascontiguousarray(np.asarray(inputs["L0_attn_v_kernel"], np.float32)),
        "WO": np.ascontiguousarray(np.asarray(inputs["L0_attn_out_kernel"], np.float32)),
        "W1T": np.ascontiguousarray(np.asarray(inputs["L0_ffn_w1"], np.float32).T),
        "W2T": np.ascontiguousarray(np.asarray(inputs["L0_ffn_w2"], np.float32).T),
        "W3T": np.ascontiguousarray(np.asarray(inputs["L0_ffn_w3"], np.float32).T),
        "WQKV": np.ascontiguousarray(np.asarray(inputs["L1_attn_Wqkv"], np.float32).T),
        "BQKVR": np.asarray(inputs["L1_attn_bqkv"], np.float32).reshape(1, 3 * D),
        "WG": np.ascontiguousarray(np.asarray(inputs["L1_ffn_Wg"], np.float32).T),
        "WPROJ": np.ascontiguousarray(np.asarray(inputs["L1_attn_Wproj"], np.float32).T),
        "BPROJR": np.asarray(inputs["L1_attn_bproj"], np.float32).reshape(1, D),
        "W0A": np.ascontiguousarray(np.asarray(inputs["L1_ffn_W0a"], np.float32).T),
        "W0B": np.ascontiguousarray(np.asarray(inputs["L1_ffn_W0b"], np.float32).T),
        "B0BR": np.asarray(inputs["L1_ffn_b0b"], np.float32).reshape(1, D),
        "W1A": np.ascontiguousarray(np.asarray(inputs["L1_ffn_W1a"], np.float32).T),
        "W1B": np.ascontiguousarray(np.asarray(inputs["L1_ffn_W1b"], np.float32).T),
        "B1BR": np.asarray(inputs["L1_ffn_b1b"], np.float32).reshape(1, D),
        **consts}
    nonce = np.random.randint(1, 2 ** 31 - 1, dtype=np.int32)
    maps = []
    for c in range(cfg.NC):
        beta, j = divmod(c, nblk)
        d = dict(base)
        if npass == 2:
            order = [[0, 1], [2, 3], [1, 0], [3, 2]][j]
            d["CID"] = np.full((1, 1), c, np.int32)
            d["NONCE"] = np.full((1, 16), nonce, np.int32)
        else:
            order = [(j + p) % nblk for p in range(nblk)]
        for p in range(npass):
            n = order[p]
            own0 = beta * T + n * R
            if n == 0:
                xs = np.concatenate([np.zeros((cfg.D, H), np.float32),
                                     xT[:, own0:own0 + R]], 1)
                ss = np.concatenate([np.zeros((128, H), np.float32),
                                     sinF[:, 0:R]], 1)
                cc = np.concatenate([np.zeros((128, H), np.float32),
                                     cos[:, 0:R]], 1)
                hh = np.concatenate([np.full((1, H), NEG, np.float32),
                                     np.zeros((1, R), np.float32)], 1)
            else:
                p0 = n * R - H
                xs = xT[:, own0 - H:own0 + R]
                ss = sinF[:, p0:p0 + TL0]
                cc = cos[:, p0:p0 + TL0]
                hh = np.zeros((1, TL0), np.float32)
            d[f"XT{p}"] = np.ascontiguousarray(xs)
            d[f"SINT{p}"] = np.ascontiguousarray(ss)
            d[f"COST{p}"] = np.ascontiguousarray(cc)
            d[f"HIND{p}"] = np.ascontiguousarray(hh)
        maps.append(d)
    return maps


def run_fused(cfg, inputs, runner=None, npass=4):
    nc, masks, mconsts = build_fused(cfg, npass=npass)
    maps = stage_fused(cfg, inputs, mconsts, npass=npass)
    if runner is None:
        res = run_bass_kernel_spmd(nc, maps, core_ids=list(range(cfg.NC)))
        outs = [res.results[c] for c in range(cfg.NC)]
    else:
        outs = runner(nc, maps)
    yT = np.zeros((cfg.D, cfg.B * cfg.T), np.float32)
    nblk = cfg.NC // cfg.B
    for c in range(cfg.NC):
        beta, j = divmod(c, nblk)
        qb = [[0, 1], [2, 3], [1, 0], [3, 2]][j][0] if npass == 2 else j
        yT[:, beta * cfg.T + qb * cfg.R: beta * cfg.T + (qb + 1) * cfg.R] = \
            np.asarray(outs[c]["YOUT"])
    return np.ascontiguousarray(yT.T).reshape(cfg.B, cfg.T, cfg.D)



FUSED = True
NPASS = 2


def kernel(**inputs):
    cfg = CFG
    inputs.pop("L1_attn_key", None)
    inputs.pop("L1_attn_value", None)
    if FUSED:
        return run_fused(cfg, inputs, npass=NPASS)
    nc0, masks, mconsts = build_l0(cfg)
    maps0 = stage_l0(cfg, inputs, mconsts)
    res0 = run_bass_kernel_spmd(nc0, maps0, core_ids=list(range(cfg.NC)))
    x1T = np.zeros((cfg.D, cfg.B * cfg.T), np.float32)
    nblk = cfg.T // cfg.W
    for c in range(cfg.NC):
        beta, n = divmod(c, nblk)
        own0 = beta * cfg.T + n * cfg.W
        x1T[:, own0:own0 + cfg.R] = np.asarray(res0.results[c]["X1OUT"])
    nc1 = build_l1(cfg)
    maps1 = stage_l1(cfg, inputs, x1T)
    res1 = run_bass_kernel_spmd(nc1, maps1, core_ids=list(range(cfg.NC)))
    yT = np.zeros((cfg.D, cfg.B * cfg.T), np.float32)
    qpc = cfg.NC // cfg.B
    for c in range(cfg.NC):
        beta, j = divmod(c, qpc)
        yT[:, beta * cfg.T + j * cfg.R: beta * cfg.T + (j + 1) * cfg.R] = \
            np.asarray(res1.results[c]["YOUT"])
    return np.ascontiguousarray(yT.T).reshape(cfg.B, cfg.T, cfg.D)


# version 16 (final)
# speedup vs baseline: 2.2025x; 2.2025x over previous
import math
from contextlib import ExitStack
from dataclasses import dataclass

import numpy as np
import ml_dtypes

import concourse.bass as bass
import concourse.mybir as mybir
from concourse.bass_utils import run_bass_kernel_spmd

F32 = mybir.dt.float32
BF16 = mybir.dt.bfloat16
I32 = mybir.dt.int32
AF = mybir.ActivationFunctionType
ALU = mybir.AluOpType
AX = mybir.AxisListType

BF = ml_dtypes.bfloat16
NEG = -60000.0
EPS_LN = 1e-5
EPS_Q = 1e-8
L0_SCALE = 1.0 / math.sqrt(128.0)
L1_SCALE = math.sqrt(128.0)


@dataclass(frozen=True)
class Cfg:
    D: int = 2048
    HQ: int = 16
    HKV: int = 4
    W: int = 1024
    T: int = 4096
    B: int = 2
    F0: int = 5632
    F1: int = 4096
    NC: int = 8

    @property
    def DC(self):
        return self.D // 128

    @property
    def R(self):
        return self.B * self.T // self.NC

    @property
    def FC0(self):
        return self.F0 // 128

    @property
    def FC1(self):
        return self.F1 // 128

    @property
    def GQ(self):
        return self.HQ // self.HKV


CFG = Cfg()


def col_chunks(total, size, start=0):
    out = []
    s = start
    end = start + total
    while s < end:
        out.append((s, min(size, end - s)))
        s += size
    return out



class Sched:

    DMA_POOL = 20

    def __init__(self, nc, ctx):
        self.nc = nc
        self.ctx = ctx
        self.ops = []
        self.writer = {}
        self.readers = {}
        self.last_of = {}
        self.ns = ""

    def _k(self, b):
        return b if b.startswith("@") else self.ns + b

    def add(self, engine, fn, reads=(), writes=(), dma=False, dmagrp=None,
            tag=""):
        oid = len(self.ops)
        raw_w0 = writes[0] if writes else None
        reads = [self._k(b) for b in reads]
        writes = [self._k(b) for b in writes]
        deps = set()
        for b in reads:
            w = self.writer.get(b)
            if w is not None:
                deps.add(w)
        for b in writes:
            w = self.writer.get(b)
            if w is not None:
                deps.add(w)
            deps.update(self.readers.get(b, ()))
        dmakey = (dmagrp or raw_w0) if dma else None
        self.ops.append(dict(oid=oid, e=engine, fn=fn, deps=deps, dma=dma,
                             dmakey=dmakey, tag=self.ns + tag))
        self.last_of[("d", engine, dmakey) if dma else ("c", engine)] = oid
        for b in reads:
            self.readers.setdefault(b, set()).add(oid)
        for b in writes:
            self.writer[b] = oid
            self.readers[b] = set()
        return oid

    def dep(self, oid_from_buf):
        return self.writer.get(oid_from_buf)

    def barrier(self, tag="bar"):
        snap = dict(self.last_of)
        for e in ("sp", "g", "t", "v", "a"):
            oid = len(self.ops)
            deps = set(snap.values())
            self.ops.append(dict(oid=oid, e=e, fn=(lambda eng: eng.nop()),
                                 deps=deps, dma=False, dmakey=None,
                                 tag=f"{tag}.{e}"))
            self.last_of[("c", e)] = oid

    def emit(self, block):
        ops = self.ops
        for o in ops:
            o["rdeps"] = set()
            for d in o["deps"]:
                if ops[d]["e"] == "t" and o["e"] == "t" and not ops[d]["dma"] and not o["dma"]:
                    continue
                if d == o["oid"]:
                    continue
                o["rdeps"].add(d)
        awaited = set()
        for o in ops:
            awaited.update(o["rdeps"])

        sems = {}

        def get_sem(key):
            if key not in sems:
                h = self.ctx.enter_context(self.nc.semaphore(f"s_{len(sems)}"))
                sems[key] = [h, 0]
            return sems[key]

        for o in ops:
            if o["dma"]:
                s = get_sem(("d", o["e"], o["dmakey"]))
                s[1] += 16
                o["sem"], o["tick"] = s, s[1]
            elif o["oid"] in awaited:
                s = get_sem(("c", o["e"]))
                s[1] += 1
                o["sem"], o["tick"] = s, s[1]
            else:
                o["sem"], o["tick"] = None, None

        hw = {}
        dma_ops = {}
        for o in ops:
            if o["dma"]:
                dma_ops.setdefault(id(o["sem"]), []).append((o["oid"], o["tick"]))

        import bisect

        def dma_wait_val(sem, waiter_oid):
            lst = dma_ops[id(sem)]
            i = bisect.bisect_left(lst, (waiter_oid, -1)) - 1
            return lst[i][1]

        def body_for(ename):
            ops_e = [o for o in ops if o["e"] == ename]

            def body(eng):
                for o in ops_e:
                    for d in sorted(o["rdeps"]):
                        od = ops[d]
                        sem, tick = od["sem"], od["tick"]
                        assert sem is not None, (ename, o["tag"], od["tag"])
                        if od["dma"]:
                            tick = dma_wait_val(sem, o["oid"])
                        k = (ename, id(sem))
                        if hw.get(k, 0) < tick:
                            eng.wait_ge(sem[0], tick)
                            hw[k] = tick
                    inst = o["fn"](eng)
                    if o["sem"] is not None:
                        if inst is None:
                            assert not o["dma"], o["tag"]
                            eng.sem_inc(o["sem"][0], 1)
                        else:
                            inst.then_inc(o["sem"][0], 16 if o["dma"] else 1)
            return body

        for ename in ("sp", "g", "t", "v", "a"):
            if any(o["e"] == ename for o in self.ops):
                getattr(block, {"sp": "sync", "g": "gpsimd", "t": "tensor",
                                "v": "vector", "a": "scalar"}[ename])(body_for(ename))



def host_sincos(cfg):
    hd, t, theta = 128, cfg.T, 10000.0
    fraction = np.arange(0, hd, 2, dtype=np.float64) / hd
    timescale = theta ** fraction
    ang = np.arange(t, dtype=np.float64)[:, None] / timescale[None, :]
    ang = np.concatenate([ang, ang], axis=-1)
    return (np.ascontiguousarray(np.sin(ang).astype(np.float32).T),
            np.ascontiguousarray(np.cos(ang).astype(np.float32).T))


def l0_key_windows(cfg, H, R):
    out = []
    for kc in range((H + R) // 128):
        c0 = kc * 128
        qs = max(0, c0 - H)
        qe = min(R, (c0 + 127) - H + cfg.W + 1)
        if qs < qe:
            out.append((kc, qs, qe))
    return out


def l0_mask_tiles(cfg, H, R):
    tiles, uniq = {}, {}
    for kc, qs, qe in l0_key_windows(cfg, H, R):
        c0 = kc * 128
        for qb in range(qs - qs % 128, qe, 128):
            j = np.arange(128)[:, None]
            q = qb + np.arange(128)[None, :]
            valid = (q >= (c0 + j) - H) & (q <= (c0 + j) - H + cfg.W)
            inplay = (q >= max(qb, qs)) & (q < qe)
            m = np.where(valid | ~inplay, np.float32(0.0), np.float32(NEG))
            if not (m == 0).all():
                key = m.tobytes()
                if key not in uniq:
                    uniq[key] = f"msk{len(uniq)}"
                tiles[(kc, qb)] = uniq[key]
    consts = {}
    for key, name in uniq.items():
        consts[name] = np.frombuffer(key, dtype=np.float32).reshape(128, 128).copy()
    return tiles, consts



def make_ln_scratch(cfg, nc, ctx, name):
    DC = cfg.DC
    CH = 128
    return dict(
        CH=CH, sid=name,
        xf=ctx.enter_context(nc.sbuf_tensor(f"{name}xf", [128, 2 * DC * CH], F32)),
        xb=ctx.enter_context(nc.sbuf_tensor(f"{name}xb", [128, 2 * DC * CH], BF16)),
        sq=ctx.enter_context(nc.sbuf_tensor(f"{name}sq", [128, 2 * DC * CH], BF16)),
        st=ctx.enter_context(nc.sbuf_tensor(f"{name}st", [128, 4 * CH], F32)),
        df=ctx.enter_context(nc.sbuf_tensor(f"{name}df", [128, CH], F32)),
        ps=ctx.enter_context(nc.psum_tensor(f"{name}ps", [128, 1024], F32)),
    )


def emit_ln(cfg, nc, scr, S, src_dram, col0, ncols, xn_sb, xn_stride, dst0,
            pfx, src_deps=()):
    DC, D = cfg.DC, cfg.D
    CH = scr["CH"]
    xf, xb, sq, st, df, ps = (scr[k] for k in ("xf", "xb", "sq", "st", "df", "ps"))
    sid = scr["sid"]
    ones_bf = S.consts["ones_bf"]
    src_r = src_dram.rearrange("(dc p) t -> p dc t", p=128)

    chunks = col_chunks(ncols, CH, start=col0)

    def ln_stage(i):
        t0, tw = chunks[i]
        b = i % 2
        xfb = xf[:, b * DC * CH: b * DC * CH + DC * tw]
        S.add("sp", lambda sp, xfb=xfb, t0=t0, tw=tw: sp.dma_start(
            out=xfb.rearrange("p (dc t) -> p dc t", dc=DC),
            in_=src_r[:, :, t0:t0 + tw]),
            reads=tuple(src_deps), writes=(f"{sid}xf{b}",), dma=True,
            tag=f"{pfx}ld{i}")
        for dc in range(DC):
            S.add("v", lambda v, dc=dc, b=b, xfb=xfb, tw=tw: v.tensor_copy(
                xb[:, b * DC * CH + dc * CH: b * DC * CH + dc * CH + tw],
                xfb[:, dc * tw:(dc + 1) * tw]),
                reads=(f"{sid}xf{b}",), writes=(f"{sid}xb{b}.{dc}",),
                tag=f"{pfx}cb{i}.{dc}")
            S.add("a", lambda a, dc=dc, b=b, xfb=xfb, tw=tw: a.activation(
                sq[:, b * DC * CH + dc * CH: b * DC * CH + dc * CH + tw],
                xfb[:, dc * tw:(dc + 1) * tw], AF.Square),
                reads=(f"{sid}xf{b}",), writes=(f"{sid}sq{b}.{dc}",),
                tag=f"{pfx}sq{i}.{dc}")

    ln_stage(0)
    for i, (t0, tw) in enumerate(chunks):
        b = i % 2
        xfb = xf[:, b * DC * CH: b * DC * CH + DC * tw]
        if i + 1 < len(chunks):
            ln_stage(i + 1)
        for dc in range(DC):
            S.add("t", lambda t, dc=dc, b=b, tw=tw: t.matmul(
                ps[:, 0:tw], ones_bf[:, 0:128],
                xb[:, b * DC * CH + dc * CH: b * DC * CH + dc * CH + tw],
                start=(dc == 0), stop=(dc == DC - 1)),
                reads=("@ones_bf", f"{sid}xb{b}.{dc}"), writes=(f"{sid}psm",), tag=f"{pfx}mm{i}.{dc}")
        for dc in range(DC):
            S.add("t", lambda t, dc=dc, b=b, tw=tw: t.matmul(
                ps[:, 512:512 + tw], ones_bf[:, 0:128],
                sq[:, b * DC * CH + dc * CH: b * DC * CH + dc * CH + tw],
                start=(dc == 0), stop=(dc == DC - 1)),
                reads=("@ones_bf", f"{sid}sq{b}.{dc}"), writes=(f"{sid}pss",), tag=f"{pfx}ms{i}.{dc}")
        mean = st[:, 0:tw]
        m2 = st[:, CH:CH + tw]
        var = st[:, 2 * CH:2 * CH + tw]
        rstd = st[:, 3 * CH:3 * CH + tw]
        S.add("v", lambda v, mean=mean, tw=tw: v.tensor_scalar(
            mean, ps[:, 0:tw], 1.0 / D, None, ALU.mult),
            reads=(f"{sid}psm",), writes=(f"{sid}mean",), tag=f"{pfx}mean{i}")
        S.add("v", lambda v, mean=mean, m2=m2: v.tensor_tensor(m2, mean, mean, ALU.mult),
            reads=(f"{sid}mean",), writes=(f"{sid}m2",), tag=f"{pfx}m2{i}")
        S.add("v", lambda v, var=var, m2=m2, tw=tw: v.scalar_tensor_tensor(
            var, ps[:, 512:512 + tw], 1.0 / D, m2, ALU.mult, ALU.subtract),
            reads=(f"{sid}pss", f"{sid}m2"), writes=(f"{sid}var",), tag=f"{pfx}var{i}")
        S.add("a", lambda a, var=var: a.activation(var, var, AF.Ln,
                                                   bias=S.consts["epsln"][:, 0:1]),
            reads=(f"{sid}var", "@epsln"), writes=(f"{sid}var",), tag=f"{pfx}lg{i}")
        S.add("a", lambda a, var=var, rstd=rstd: a.activation(rstd, var, AF.Exp, scale=-0.5),
            reads=(f"{sid}var",), writes=(f"{sid}rstd",), tag=f"{pfx}rs{i}")
        for dc in range(DC):
            S.add("v", lambda v, dc=dc, xfb=xfb, mean=mean, tw=tw: v.tensor_tensor(
                df[:, 0:tw], xfb[:, dc * tw:(dc + 1) * tw], mean, ALU.subtract),
                reads=(f"{sid}xf{b}", f"{sid}mean"), writes=(f"{sid}df",),
                tag=f"{pfx}sub{i}.{dc}")
            d0 = dc * xn_stride + dst0 + (t0 - col0)
            S.add("v", lambda v, d0=d0, rstd=rstd, tw=tw: v.tensor_tensor(
                xn_sb[:, d0:d0 + tw], df[:, 0:tw], rstd, ALU.mult),
                reads=(f"{sid}df", f"{sid}rstd"),
                writes=(f"{pfx}xn", f"{pfx}xn.{dc}.{t0}"), tag=f"{pfx}mul{i}.{dc}")


def emit_proj(cfg, nc, ps, psname, S, w_sb, wbuf, woc, xn_sb, xn_stride,
              xn_col0, xn_buf, ncols, oc_list, out_cb, pfx, kdc=None,
              extra_mm=None):
    DC = kdc if kdc is not None else cfg.DC
    ncc = len(col_chunks(ncols, 512))
    for oi, oc in enumerate(oc_list):
        for ci, (t0, tw) in enumerate(col_chunks(ncols, 512)):
            b = (oi * ncc + ci) % 2
            ps_ap = ps[:, b * 512: b * 512 + tw]
            psbuf = f"{psname}{b}"
            for dc in range(DC):
                S.add("t", lambda t, dc=dc, oi=oi, t0=t0, tw=tw, ps_ap=ps_ap: t.matmul(
                    ps_ap,
                    w_sb[:, (dc * woc + oi) * 128: (dc * woc + oi) * 128 + 128],
                    xn_sb[:, dc * xn_stride + xn_col0 + t0:
                          dc * xn_stride + xn_col0 + t0 + tw],
                    start=(dc == 0), stop=(dc == DC - 1 and extra_mm is None)),
                    reads=(wbuf, xn_buf), writes=(psbuf,), tag=f"{pfx}mm{oc}.{t0}.{dc}")
            if extra_mm is not None:
                extra_mm(S, oi, oc, t0, tw, ps_ap, psbuf)
            out_cb(S, oi, oc, t0, tw, ps_ap, psbuf)


def emit_rope(cfg, nc, S, buf_sb, bufname, col0, ncols, sc_col0, pfx):
    rot = S.consts["rope_rot"]
    t1 = S.consts["rope_tmp"]
    t2 = S.consts["rope_tmp2"]
    sin, cos = S.consts["sinT"], S.consts["cosT"]
    for t0, tw in col_chunks(ncols, 512):
        c = col0 + t0
        s = sc_col0 + t0
        S.add("sp", lambda sp, c=c, tw=tw: sp.dma_start(
            out=rot[0:64, 0:tw], in_=buf_sb[64:128, c:c + tw]),
            reads=(bufname,), writes=("rope_rot_l",), dma=True, dmagrp="ropedma",
            tag=f"{pfx}d1.{t0}")
        S.add("sp", lambda sp, c=c, tw=tw: sp.dma_start(
            out=rot[64:128, 0:tw], in_=buf_sb[0:64, c:c + tw]),
            reads=(bufname,), writes=("rope_rot_h",), dma=True, dmagrp="ropedma",
            tag=f"{pfx}d2.{t0}")
        S.add("g", lambda g, c=c, s=s, tw=tw: g.tensor_tensor(
            t1[:, 0:tw], buf_sb[:, c:c + tw], cos[:, s:s + tw], ALU.mult),
            reads=(bufname, "sincos"), writes=("rope_t1",), tag=f"{pfx}r1.{t0}")
        S.add("g", lambda g, s=s, tw=tw: g.tensor_tensor(
            t2[:, 0:tw], rot[:, 0:tw], sin[:, s:s + tw], ALU.mult),
            reads=("rope_rot_l", "rope_rot_h", "sincos"), writes=("rope_t2",),
            tag=f"{pfx}r2.{t0}")
        S.add("g", lambda g, c=c, tw=tw: g.tensor_tensor(
            buf_sb[:, c:c + tw], t1[:, 0:tw], t2[:, 0:tw], ALU.add),
            reads=("rope_t1", "rope_t2"), writes=(bufname,), tag=f"{pfx}r3.{t0}")



def emit_attn(cfg, nc, ctx, S, pfx, *, heads, kv_of, Tkeys, R, kwin, masks,
              scale, prep_head, kt_ap, q_ap, out_cb, v_tile=None, v_dmaj=None,
              key_bias=None):
    ident_bf = S.consts["ident_bf"]
    ones_bf = S.consts["ones_bf"]

    ntiles = R // 128
    tlo, thi = [], []
    for ti in range(ntiles):
        qt0 = ti * 128
        lo = min(kc * 128 for kc, qs, qe in kwin if qs < qt0 + 128 and qe > qt0)
        hi = max(kc * 128 + 128 for kc, qs, qe in kwin if qs < qt0 + 128 and qe > qt0)
        tlo.append(lo)
        thi.append(hi)
    PMAX = max(h - l for l, h in zip(tlo, thi))
    NPMAX = (PMAX + 511) // 512

    sa_ps = ctx.enter_context(nc.psum_tensor(f"{pfx}sa", [128, 1024], F32))
    tp_ps = [ctx.enter_context(nc.psum_tensor(f"{pfx}tp{i}", [128, 512], BF16))
             for i in range(2)]
    vt_ps = [ctx.enter_context(nc.psum_tensor(f"{pfx}vt{i}", [128, 512], BF16))
             for i in range(2)]
    den_ps = ctx.enter_context(nc.psum_tensor(f"{pfx}den", [128, 128], F32))
    out_ps = ctx.enter_context(nc.psum_tensor(f"{pfx}out", [128, 128], F32))

    pp = ctx.enter_context(nc.sbuf_tensor(f"{pfx}pp", [128, 2 * PMAX], BF16))
    ptsb = ctx.enter_context(nc.sbuf_tensor(f"{pfx}pts", [128, 2 * 512], BF16))
    mg = ctx.enter_context(nc.sbuf_tensor(f"{pfx}mg", [128, 2 * NPMAX], F32))
    mps = ctx.enter_context(nc.sbuf_tensor(f"{pfx}mps", [128, 2 * NPMAX], F32))
    alp = ctx.enter_context(nc.sbuf_tensor(f"{pfx}alp", [128, 2 * NPMAX], F32))
    mts = ctx.enter_context(nc.sbuf_tensor(f"{pfx}mts", [128, 2], F32))
    den_sb = ctx.enter_context(nc.sbuf_tensor(f"{pfx}densb", [128, 2 * 128], F32))
    rec_sb = ctx.enter_context(nc.sbuf_tensor(f"{pfx}rec", [128, 2 * 128], F32))
    if v_dmaj is not None:
        vtt = ctx.enter_context(nc.sbuf_tensor(f"{pfx}vtt", [128, 2 * Tkeys], BF16))
        vkcs = sorted({kc for kc, _, _ in kwin})

        def v_tile(h, kc):
            hs = h % 2
            return vtt[:, hs * Tkeys + kc * 128: hs * Tkeys + kc * 128 + 128],                 f"{pfx}vtt{hs}"

    sa_bank = [0]
    pgrp = [0]
    vgrp = [0]

    def vtr_groups(h):
        hs = h % 2
        vap, vbuf = v_dmaj(h)
        vgroups = [vkcs[i:i + 4] for i in range(0, len(vkcs), 4)]
        for gi, g4 in enumerate(vgroups):
            sl = vgrp[0] % 2
            vgrp[0] += 1
            for i, kc in enumerate(g4):
                S.add("t", lambda t, sl=sl, i=i, kc=kc, vap=vap: t.transpose(
                    vt_ps[sl][:, i * 128: i * 128 + 128],
                    vap[:, kc * 128: kc * 128 + 128], ident_bf[:, :]),
                    reads=(vbuf, "@ident_bf"), writes=(f"{pfx}vt{sl}",),
                    tag=f"{pfx}VT{h}.{kc}")
            gn = len(g4)
            c0 = g4[0] * 128
            if gi % 2:
                S.add("v", lambda v, sl=sl, gn=gn, c0=c0, hs=hs: v.tensor_copy(
                    vtt[:, hs * Tkeys + c0: hs * Tkeys + c0 + gn * 128],
                    vt_ps[sl][:, 0: gn * 128]),
                    reads=(f"{pfx}vt{sl}",), writes=(f"{pfx}vtt{hs}",),
                    tag=f"{pfx}VE{h}.{gi}")
            else:
                S.add("a", lambda a, sl=sl, gn=gn, c0=c0, hs=hs: a.activation(
                    vtt[:, hs * Tkeys + c0: hs * Tkeys + c0 + gn * 128],
                    vt_ps[sl][:, 0: gn * 128], AF.Copy),
                    reads=(f"{pfx}vt{sl}",), writes=(f"{pfx}vtt{hs}",),
                    tag=f"{pfx}VE{h}.{gi}")
            yield

    def genA(j):
        h, ti = divmod(j, ntiles)
        hs = j % 2
        ktap, ktbuf = kt_ap(h)
        qap, qbuf = q_ap(h)
        qt0 = ti * 128
        lo = tlo[ti]
        pieces = col_chunks(thi[ti] - lo, 512, start=lo)
        np_ = len(pieces)
        vg = vtr_groups(h) if (v_dmaj is not None and ti == 0) else None
        for pi, (p0, pw) in enumerate(pieces):
            b = sa_bank[0]
            sa_bank[0] = (b + 1) % 2
            sap = sa_ps[:, b * 512: b * 512 + pw]
            sbf = f"{pfx}sa{b}"
            S.add("t", lambda t, sap=sap, p0=p0, pw=pw, ktap=ktap, qap=qap,
                  qt0=qt0, stp=(key_bias is None): t.matmul(
                  sap, qap[:, qt0:qt0 + 128], ktap[:, p0:p0 + pw],
                  start=True, stop=stp),
                  reads=(qbuf, ktbuf), writes=(sbf,),
                  tag=f"{pfx}A{h}.{ti}.{p0}")
            if key_bias is not None:
                kb_ap, kb_buf = key_bias(p0, pw)
                S.add("t", lambda t, sap=sap, kb_ap=kb_ap, pw=pw: t.matmul(
                    sap, ones_bf[0:1, 0:128], kb_ap, start=False, stop=True),
                    reads=("@ones_bf", kb_buf), writes=(sbf,),
                    tag=f"{pfx}Ah{h}.{ti}.{p0}")
            for kc in range(p0 // 128, (p0 + pw) // 128):
                mt = masks.get((kc, qt0))
                if mt is not None:
                    S.add("v", lambda v, b=b, kc=kc, p0=p0, mt=mt: v.tensor_tensor(
                        sa_ps[:, b * 512 + kc * 128 - p0:
                              b * 512 + kc * 128 - p0 + 128],
                        sa_ps[:, b * 512 + kc * 128 - p0:
                              b * 512 + kc * 128 - p0 + 128],
                        S.consts[mt + "T"][:, :], ALU.add),
                        reads=(sbf, "@" + mt + "T"), writes=(sbf,),
                        tag=f"{pfx}Am{h}.{ti}.{kc}")
            S.add("v", lambda v, sap=sap, pi=pi, hs=hs: v.reduce_max(
                mg[:, hs * NPMAX + pi: hs * NPMAX + pi + 1], sap, axis=AX.X),
                reads=(sbf,), writes=(f"{pfx}mg{hs}",),
                tag=f"{pfx}Ar{h}.{ti}.{pi}")
            S.add("v", lambda v, pi=pi, hs=hs: v.tensor_scalar(
                mps[:, hs * NPMAX + pi: hs * NPMAX + pi + 1],
                mg[:, hs * NPMAX + pi: hs * NPMAX + pi + 1], -float(scale),
                None, ALU.mult),
                reads=(f"{pfx}mg{hs}",), writes=(f"{pfx}mps{hs}",),
                tag=f"{pfx}As{h}.{ti}.{pi}")
            S.add("a", lambda a, sap=sap, p0=p0, pw=pw, lo=lo, pi=pi, hs=hs:
                  a.activation(
                pp[:, hs * PMAX + p0 - lo: hs * PMAX + p0 - lo + pw], sap,
                AF.Exp, scale=float(scale),
                bias=mps[:, hs * NPMAX + pi: hs * NPMAX + pi + 1]),
                reads=(sbf, f"{pfx}mps{hs}"), writes=(f"{pfx}pp{hs}.{pi}",),
                tag=f"{pfx}Ae{h}.{ti}.{pi}")
            if vg is not None:
                next(vg, None)
            yield
        if vg is not None:
            for _ in vg:
                yield
        S.add("v", lambda v, np_=np_, hs=hs: v.reduce_max(
            mts[:, hs:hs + 1], mg[:, hs * NPMAX: hs * NPMAX + np_], axis=AX.X),
            reads=(f"{pfx}mg{hs}",), writes=(f"{pfx}mtp{hs}",),
            tag=f"{pfx}At{h}.{ti}")
        S.add("v", lambda v, hs=hs: v.tensor_scalar(
            mts[:, hs:hs + 1], mts[:, hs:hs + 1], -float(scale), None, ALU.mult),
            reads=(f"{pfx}mtp{hs}",), writes=(f"{pfx}mts{hs}",),
            tag=f"{pfx}Ats{h}.{ti}")
        S.add("a", lambda a, np_=np_, hs=hs: a.activation(
            alp[:, hs * NPMAX: hs * NPMAX + np_],
            mg[:, hs * NPMAX: hs * NPMAX + np_], AF.Exp, scale=float(scale),
            bias=mts[:, hs:hs + 1]),
            reads=(f"{pfx}mg{hs}", f"{pfx}mts{hs}"), writes=(f"{pfx}alp{hs}",),
            tag=f"{pfx}Aa{h}.{ti}")
        if np_ > 1:
            for pi, (p0, pw) in enumerate(pieces):
                S.add("v", lambda v, p0=p0, pw=pw, lo=lo, pi=pi, hs=hs:
                      v.tensor_scalar(
                    pp[:, hs * PMAX + p0 - lo: hs * PMAX + p0 - lo + pw],
                    pp[:, hs * PMAX + p0 - lo: hs * PMAX + p0 - lo + pw],
                    alp[:, hs * NPMAX + pi: hs * NPMAX + pi + 1], None,
                    ALU.mult),
                    reads=(f"{pfx}pp{hs}.{pi}", f"{pfx}alp{hs}"),
                    writes=(f"{pfx}pp{hs}.{pi}",),
                    tag=f"{pfx}Ax{h}.{ti}.{pi}")
        yield

    def genC(j):
        h, ti = divmod(j, ntiles)
        hs = j % 2
        qt0 = ti * 128
        lo = tlo[ti]
        kcs = list(range(lo // 128, thi[ti] // 128))
        groups = [kcs[i:i + 4] for i in range(0, len(kcs), 4)]

        def tr_group(gi):
            sl = pgrp[0] % 2
            pgrp[0] += 1
            for i, kc in enumerate(groups[gi]):
                S.add("t", lambda t, sl=sl, i=i, kc=kc, hs=hs, lo=lo: t.transpose(
                    tp_ps[sl][:, i * 128: i * 128 + 128],
                    pp[:, hs * PMAX + kc * 128 - lo:
                       hs * PMAX + kc * 128 - lo + 128], ident_bf[:, :]),
                    reads=(f"{pfx}pp{hs}.{gi}", "@ident_bf"),
                    writes=(f"{pfx}tp{sl}",),
                    tag=f"{pfx}T{h}.{ti}.{kc}")
            gn = len(groups[gi])
            if gi % 2:
                S.add("v", lambda v, sl=sl, gn=gn: v.tensor_copy(
                    ptsb[:, sl * 512: sl * 512 + gn * 128],
                    tp_ps[sl][:, 0: gn * 128]),
                    reads=(f"{pfx}tp{sl}",), writes=(f"{pfx}pts{sl}",),
                    tag=f"{pfx}Te{h}.{ti}.{gi}")
            else:
                S.add("a", lambda a, sl=sl, gn=gn: a.activation(
                    ptsb[:, sl * 512: sl * 512 + gn * 128],
                    tp_ps[sl][:, 0: gn * 128], AF.Copy),
                    reads=(f"{pfx}tp{sl}",), writes=(f"{pfx}pts{sl}",),
                    tag=f"{pfx}Te{h}.{ti}.{gi}")
            return sl

        sls = {0: tr_group(0)}
        nk = len(kcs)
        ki = 0
        for gi in range(len(groups)):
            if gi + 1 < len(groups):
                sls[gi + 1] = tr_group(gi + 1)
                yield
            sl = sls[gi]
            for i, kc in enumerate(groups[gi]):
                vap, vbuf = v_tile(h, kc)
                first = ki == 0
                last = ki == nk - 1
                S.add("t", lambda t, sl=sl, i=i, first=first, last=last: t.matmul(
                    den_ps[:, 0:128], ones_bf[:, 0:128],
                    ptsb[:, sl * 512 + i * 128: sl * 512 + i * 128 + 128],
                    start=first, stop=last),
                    reads=("@ones_bf", f"{pfx}pts{sl}"), writes=(f"{pfx}denps",),
                    tag=f"{pfx}Bd{h}.{ti}.{kc}")
                S.add("t", lambda t, sl=sl, i=i, vap=vap, vbuf=vbuf, first=first,
                      last=last: t.matmul(
                    out_ps[:, 0:128], vap,
                    ptsb[:, sl * 512 + i * 128: sl * 512 + i * 128 + 128],
                    start=first, stop=last),
                    reads=(vbuf, f"{pfx}pts{sl}"), writes=(f"{pfx}outps",),
                    tag=f"{pfx}Bv{h}.{ti}.{kc}")
                ki += 1
            yield
        S.add("v", lambda v, hs=hs: v.tensor_copy(
            den_sb[:, hs * 128: hs * 128 + 128], den_ps[:, 0:128]),
            reads=(f"{pfx}denps",), writes=(f"{pfx}densb{hs}",),
            tag=f"{pfx}F1{h}.{ti}")
        S.add("a", lambda a, hs=hs: a.activation(
            den_sb[:, hs * 128: hs * 128 + 128],
            den_sb[:, hs * 128: hs * 128 + 128], AF.Ln),
            reads=(f"{pfx}densb{hs}",), writes=(f"{pfx}densb{hs}",),
            tag=f"{pfx}F2{h}.{ti}")
        S.add("a", lambda a, hs=hs: a.activation(
            rec_sb[:, hs * 128: hs * 128 + 128],
            den_sb[:, hs * 128: hs * 128 + 128], AF.Exp, scale=-1.0),
            reads=(f"{pfx}densb{hs}",), writes=(f"{pfx}recsb{hs}",),
            tag=f"{pfx}F3{h}.{ti}")
        out_cb(S, h, qt0, 128, out_ps[:, 0:128], f"{pfx}outps",
               rec_sb[:, hs * 128: hs * 128 + 128], f"{pfx}recsb{hs}")
        yield

    ntot = heads * ntiles
    prep_ti = min(2, ntiles - 1)
    prep_head(S, 0)
    ga = genA(0)
    for _ in ga:
        pass
    for j in range(ntot):
        hj, tij = divmod(j, ntiles)
        if tij == prep_ti and hj + 1 < heads:
            prep_head(S, hj + 1)
        gc = genC(j)
        ga = genA(j + 1) if j + 1 < ntot else None
        done_a = ga is None
        done_c = False
        while not (done_a and done_c):
            if not done_a:
                try:
                    next(ga)
                except StopIteration:
                    done_a = True
            if not done_c:
                try:
                    next(gc)
                except StopIteration:
                    done_c = True


def emit_quant(S, eng, dst_ap, dstbuf, src_ap, srcbuf, qa_ap, qabuf, qb_ap,
               qbbuf, tau_ap, ntau_ap, tag):
    S.add(eng, lambda e: e.tensor_scalar(qa_ap, src_ap, tau_ap, None, ALU.is_gt),
          reads=(srcbuf, "@tau"), writes=(qabuf,), tag=f"{tag}a")
    S.add(eng, lambda e: e.tensor_scalar(qb_ap, src_ap, ntau_ap, None, ALU.is_lt),
          reads=(srcbuf, "@tau"), writes=(qbbuf,), tag=f"{tag}b")
    S.add(eng, lambda e: e.tensor_tensor(dst_ap, qa_ap, qb_ap, ALU.subtract),
          reads=(qabuf, qbbuf), writes=(dstbuf,), tag=f"{tag}s")


def setup_consts(nc, ctx, S, specs):
    for name, (shape, dt) in specs.items():
        dr = nc.dram_tensor(f"c_{name}", list(shape), dt, kind="ExternalInput").ap()
        sb = ctx.enter_context(nc.sbuf_tensor(f"cs_{name}", list(shape), dt))
        S.consts[name] = sb
        S.add("sp", lambda sp, sb=sb, dr=dr: sp.dma_start(out=sb[:, :], in_=dr),
              writes=("@" + name,), dma=True, dmagrp="consts",
              tag=f"const_{name}")


def emit_sproj(cfg, nc, ctx, S, W_dram, wname, wsb, xparts, oc0, noc, out_cb,
               pfx, ps, psname, blk=2, kdc=None, extra_mm=None, cast=True):
    DCk = kdc if kdc is not None else cfg.DC
    w_r = W_dram.rearrange("(dc p) o -> p dc o", p=128)
    eng = "g" if cast else "sp"
    for bi, (o0, ow) in enumerate(col_chunks(noc * 128, blk * 128, start=oc0 * 128)):
        sl = bi % 2
        wap = wsb[:, sl * DCk * blk * 128: sl * DCk * blk * 128 + DCk * ow]
        wnm = wsb.name if hasattr(wsb, "name") else wsb.tensor.name
        if wnm.startswith(S.ns) and S.ns:
            wnm = wnm[len(S.ns):]
        wbuf = f"W&{wnm}^{sl}"
        S.add(eng, lambda e, wap=wap, o0=o0, ow=ow: e.dma_start(
            out=wap.rearrange("p (dc o) -> p dc o", dc=DCk),
            in_=w_r[:, :, o0:o0 + ow]),
            writes=(wbuf,), dma=True, tag=f"{pfx}wld{bi}")
        for xn_sb, stride, col0, xbuf, ncols, dst0 in xparts:
            emit_proj(cfg, nc, ps, psname, S, wap, wbuf, ow // 128,
                      xn_sb, stride, col0, xbuf, ncols,
                      list(range(ow // 128)),
                      (lambda S_, oi, oc_, t0, tw, ps_ap, psbuf,
                       o0=o0, dst0=dst0:
                       out_cb(S_, o0 // 128 + oi, dst0 + t0, tw, ps_ap, psbuf)),
                      f"{pfx}b{bi}", kdc=DCk,
                      extra_mm=(None if extra_mm is None else
                                (lambda S_, oi, oc_, t0, tw, ps_ap, psbuf,
                                 o0=o0, dst0=dst0:
                                 extra_mm(S_, o0 // 128 + oi, dst0 + t0, tw,
                                          ps_ap, psbuf))))



def emit_l0_gamma(cfg, nc, ctx, S, io):
    D, DC, F0, FC0 = cfg.D, cfg.DC, cfg.F0, cfg.FC0
    W1T, W2T, W3T = io["W1T"], io["W2T"], io["W3T"]
    tau = io["tau"]
    with ExitStack() as ctxG:
        gt = ctxG.enter_context(nc.sbuf_tensor("gt", [128, 2 * F0], F32))
        gacc = ctxG.enter_context(nc.sbuf_tensor("gacc", [128, 4], F32))
        gtmp = ctxG.enter_context(nc.sbuf_tensor("gtmp", [128, 1], F32))
        ps_g = ctxG.enter_context(nc.psum_tensor("ps_g", [128, 512], F32))
        S.add("v", lambda v: v.memset(gacc[:, :], 0.0), writes=("gacc",), tag="gz")
        for m, (Wd, rows, cols) in enumerate(
                [(W1T, DC, F0), (W2T, DC, F0), (W3T, FC0, D)]):
            for r in range(rows):
                sl = r % 2
                gap = gt[:, sl * F0: sl * F0 + cols]
                S.add("sp", lambda sp, gap=gap, Wd=Wd, r=r, cols=cols:
                      sp.dma_start(out=gap, in_=Wd[r * 128:(r + 1) * 128, :]),
                      writes=(f"gt{sl}",), dma=True, tag=f"gld{m}.{r}")
                S.add("v", lambda v, gap=gap: v.reduce_sum(
                    gtmp[:, :], gap, axis=AX.X, apply_absolute_value=True),
                    reads=(f"gt{sl}",), writes=("gtmp",), tag=f"gr{m}.{r}")
                S.add("v", lambda v, m=m: v.tensor_tensor(
                    gacc[:, m:m + 1], gacc[:, m:m + 1], gtmp[:, :], ALU.add),
                    reads=("gtmp", "gacc"), writes=("gacc",), tag=f"ga{m}.{r}")
        S.add("t", lambda t: t.matmul(ps_g[:, 0:4], S.consts["ones_f32"][:, :],
                                      gacc[:, :], start=True, stop=True),
              reads=("gacc", "@ones_f32"), writes=("psg",), tag="gmm")
        S.add("v", lambda v: v.tensor_copy(gacc[:, :], ps_g[:, 0:4]),
              reads=("psg",), writes=("gacc",), tag="gev")
        Ne = float(D * F0)
        for m in range(3):
            S.add("a", lambda a, m=m: a.activation(
                tau[:, m:m + 1], gacc[:, m:m + 1], AF.Copy,
                bias=0.5 * EPS_Q, scale=0.5 / Ne),
                reads=("gacc",), writes=("@tau",), tag=f"gt{m}")
            S.add("a", lambda a, m=m: a.activation(
                tau[:, 4 + m:5 + m], gacc[:, m:m + 1], AF.Copy,
                bias=-0.5 * EPS_Q, scale=-0.5 / Ne),
                reads=("gacc",), writes=("@tau",), tag=f"gnt{m}")
    S.barrier("b_gam")


def emit_l0_pass(cfg, nc, S, io, kwin, masks):
    R, D, DC = cfg.R, cfg.D, cfg.DC
    H = cfg.W
    T = H + R
    HKV, HQ, GQ, F0, FC0 = cfg.HKV, cfg.HQ, cfg.GQ, cfg.F0, cfg.FC0
    FB = 256
    XT, SINT, COST, HIND = io["XT"], io["SINT"], io["COST"], io["HIND"]
    WQ, WK, WV, WO = io["WQ"], io["WK"], io["WV"], io["WO"]
    W1T, W2T, W3T = io["W1T"], io["W2T"], io["W3T"]
    PS1, X1dst, tau = io["PS1"], io["X1dst"], io["tau"]
    x1keyp = io["x1keyp"]
    XT_r = XT.rearrange("(dc p) t -> p dc t", p=128)
    PS1_r = PS1.rearrange("(dc p) t -> p dc t", p=128)
    X1_r = X1dst.rearrange("(dc p) t -> p dc t", p=128)

    with ExitStack() as ctx:
        xno = ctx.enter_context(nc.sbuf_tensor(S.ns + "xno", [128, DC * R], BF16))
        hind = ctx.enter_context(nc.sbuf_tensor(S.ns + "hind", [1, T], BF16))
        S.add("g", lambda g: g.dma_start(out=hind[:, :], in_=HIND),
              writes=("hind",), dma=True, dmagrp="passld", tag="ldhind")

        ctxAP = ctx.enter_context(ExitStack())
        attnT = ctxAP.enter_context(nc.sbuf_tensor(S.ns + "attnT", [128, HQ * R], BF16))
        with ExitStack() as ctxA:
            QT = ctxA.enter_context(nc.sbuf_tensor(S.ns + "QT", [128, HQ * R], BF16))
            KT = ctxA.enter_context(nc.sbuf_tensor(S.ns + "KT", [128, HKV * T], BF16))
            Vtok = ctxA.enter_context(nc.sbuf_tensor(S.ns + "Vtok", [128, HKV * T], BF16))
            with ExitStack() as ctx1:
                xnh = ctx1.enter_context(nc.sbuf_tensor(S.ns + "xnh", [128, DC * H], BF16))
                with ExitStack() as ctxLN:
                    scr = make_ln_scratch(cfg, nc, ctxLN, S.ns + "lns")
                    emit_ln(cfg, nc, scr, S, XT, 0, H, xnh, H, 0, "lnh")
                    emit_ln(cfg, nc, scr, S, XT, H, R, xno, R, 0, "lno")
                S.barrier("b_ln")
                xp_kv = [(xnh, H, 0, "lnhxn", H, 0), (xno, R, 0, "lnoxn", R, H)]
                xp_q = [(xno, R, 0, "lnoxn", R, 0)]
                with ExitStack() as ctxW:
                    sinT = ctxW.enter_context(nc.sbuf_tensor(S.ns + "sinT", [128, T], F32))
                    cosT = ctxW.enter_context(nc.sbuf_tensor(S.ns + "cosT", [128, T], F32))
                    S.consts["sinT"] = sinT
                    S.consts["cosT"] = cosT
                    S.add("sp", lambda sp: sp.dma_start(out=sinT[:, :], in_=SINT),
                          writes=("sincos_s",), dma=True, dmagrp="passld",
                          tag="ldsin")
                    S.add("sp", lambda sp: sp.dma_start(out=cosT[:, :], in_=COST),
                          writes=("sincos",), dma=True, dmagrp="passld",
                          tag="ldcos")
                    S.add("sp", lambda sp: None, reads=("sincos_s",),
                          writes=("sincos",), tag="sincos_join")
                    S.consts["rope_tmp"] = ctxW.enter_context(
                        nc.sbuf_tensor(S.ns + "ropet1", [128, 512], F32))
                    S.consts["rope_tmp2"] = ctxW.enter_context(
                        nc.sbuf_tensor(S.ns + "ropet2", [128, 512], F32))
                    S.consts["rope_rot"] = ctxW.enter_context(
                        nc.sbuf_tensor(S.ns + "roperot", [128, 512], BF16))
                    wsb = ctxW.enter_context(
                        nc.sbuf_tensor(S.ns + "wsb", [128, 2 * DC * 1 * 128], BF16))
                    vtmp = ctxW.enter_context(nc.sbuf_tensor(S.ns + "vtmp", [128, 512], BF16))
                    ps_p = ctxW.enter_context(nc.psum_tensor(S.ns + "ps_p", [128, 1024], F32))
                    ps_tr = ctxW.enter_context(nc.psum_tensor(S.ns + "ps_tr", [128, 128], BF16))

                    def q_cb(S_, oc, dcol, tw, ps_ap, psbuf):
                        S_.add("v", lambda v: v.tensor_copy(
                            QT[:, oc * R + dcol: oc * R + dcol + tw], ps_ap),
                            reads=(psbuf,), writes=("QT", f"QT.{oc}"),
                            tag=f"qe{oc}.{dcol}")

                    def k_cb(S_, oc, dcol, tw, ps_ap, psbuf):
                        S_.add("v", lambda v: v.tensor_copy(
                            KT[:, oc * T + dcol: oc * T + dcol + tw], ps_ap),
                            reads=(psbuf,), writes=("KT", f"KT.{oc}"),
                            tag=f"ke{oc}.{dcol}")

                    def v_cb(S_, oc, dcol, tw, ps_ap, psbuf):
                        S_.add("v", lambda v, tw=tw: v.tensor_copy(vtmp[:, 0:tw], ps_ap),
                            reads=(psbuf,), writes=("vtmp",), tag=f"ve{oc}.{dcol}")
                        for k in range(tw // 128):
                            S_.add("t", lambda t, k=k: t.transpose(
                                ps_tr[:, :], vtmp[:, k * 128:(k + 1) * 128],
                                S.consts["ident_bf"][:, :]),
                                reads=("vtmp", "@ident_bf"), writes=("pstr",),
                                tag=f"vt{oc}.{dcol}.{k}")
                            cc = (oc * (T // 128) + (dcol + k * 128) // 128) * 128
                            S_.add("v", lambda v, cc=cc: v.tensor_copy(
                                Vtok[:, cc:cc + 128], ps_tr[:, :]),
                                reads=("pstr",), writes=("Vtok",),
                                tag=f"vte{oc}.{dcol}.{k}")

                    emit_sproj(cfg, nc, ctxW, S, WQ, "WQ", wsb, xp_q, 0, DC,
                               q_cb, "pq", ps_p, "psp", blk=1)
                    emit_sproj(cfg, nc, ctxW, S, WK, "WK", wsb, xp_kv, 0, HKV,
                               k_cb, "pk", ps_p, "psp", blk=1)
                    emit_sproj(cfg, nc, ctxW, S, WV, "WV", wsb, xp_kv, 0, HKV,
                               v_cb, "pv", ps_p, "psp", blk=1)
                    for h in range(HQ):
                        emit_rope(cfg, nc, S, QT, "QT", h * R, R, H, f"rq{h}.")
                    for kv in range(HKV):
                        emit_rope(cfg, nc, S, KT, "KT", kv * T, T, 0, f"rk{kv}.")
                S.barrier("b_qkv")
            with ExitStack() as ctxAT:
                def out_cb(S_, h, q0, qw, out_ps_ap, opsbuf, rec_ap, recbuf):
                    S_.add("v", lambda v: v.tensor_tensor(
                        attnT[:, h * R + q0: h * R + q0 + qw], out_ps_ap, rec_ap,
                        ALU.mult),
                        reads=(opsbuf, recbuf), writes=("attnT",), tag=f"ao{h}.{q0}")

                emit_attn(cfg, nc, ctxAT, S, S.ns + "at", heads=HQ, kv_of=None,
                          Tkeys=T, R=R, kwin=kwin, masks=masks, scale=L0_SCALE,
                          prep_head=lambda S_, h: None,
                          kt_ap=lambda h: (KT[:, (h // GQ) * T:(h // GQ) * T + T], "KT"),
                          v_tile=lambda h, kc: (
                              Vtok[:, ((h // GQ) * (T // 128) + kc) * 128:
                                   ((h // GQ) * (T // 128) + kc) * 128 + 128], "Vtok"),
                          q_ap=lambda h: (QT[:, h * R:h * R + R], "QT"),
                          out_cb=out_cb,
                          key_bias=lambda p0, pw: (hind[0:1, p0:p0 + pw],
                                                   "hind"))
            S.barrier("b_attn")
        with ExitStack() as ctxP:
            wsbP = ctxP.enter_context(nc.sbuf_tensor(S.ns + "wsbP", [128, 2 * DC * 2 * 128], BF16))
            xr = ctxP.enter_context(nc.sbuf_tensor(S.ns + "xr", [128, 2 * 512], F32))
            s1b = ctxP.enter_context(nc.sbuf_tensor(S.ns + "s1b", [128, 2 * 512], F32))
            ps_o = ctxP.enter_context(nc.psum_tensor(S.ns + "ps_o", [128, 1024], F32))

            def o_cb(S_, oc, dcol, tw, ps_ap, psbuf):
                sl = (oc * (R // 512) + dcol // 512) % 2
                xap = xr[:, sl * 512: sl * 512 + tw]
                sap = s1b[:, sl * 512: sl * 512 + tw]
                S_.add("sp", lambda sp, oc=oc, dcol=dcol, tw=tw, xap=xap: sp.dma_start(
                    out=xap, in_=XT_r[:, oc, H + dcol:H + dcol + tw]),
                    writes=(f"xr{sl}",), dma=True, tag=f"oxl{oc}.{dcol}")
                S_.add("v", lambda v, xap=xap, sap=sap: v.tensor_tensor(
                    sap, ps_ap, xap, ALU.add),
                    reads=(psbuf, f"xr{sl}"), writes=(f"s1b{sl}",), tag=f"os{oc}.{dcol}")
                S_.add("sp", lambda sp, oc=oc, dcol=dcol, tw=tw, sap=sap: sp.dma_start(
                    out=PS1_r[:, oc, dcol:dcol + tw], in_=sap),
                    reads=(f"s1b{sl}",), writes=(f"ps1.{oc}.{dcol}",), dma=True,
                    dmagrp="ps1st", tag=f"ost{oc}.{dcol}")

            emit_sproj(cfg, nc, ctxP, S, WO, "WO", wsbP,
                       [(attnT, R, 0, "attnT", R, 0)], 0, DC, o_cb, "po",
                       ps_o, "pso")
            S.barrier("b_proj")
        ctxAP.close()
        with ExitStack() as ctxH:
            hid = ctxH.enter_context(nc.sbuf_tensor(S.ns + "hid", [128, FC0 * R], BF16))
            with ExitStack() as ctxX:
                wf = ctxX.enter_context(nc.sbuf_tensor(S.ns + "wf", [128, 2 * DC * FB], F32))
                wq = ctxX.enter_context(nc.sbuf_tensor(S.ns + "wq", [128, 2 * DC * FB], BF16))
                qa = ctxX.enter_context(nc.sbuf_tensor(S.ns + "qa", [128, DC * FB], BF16))
                qb = ctxX.enter_context(nc.sbuf_tensor(S.ns + "qb", [128, DC * FB], BF16))
                sg = ctxX.enter_context(nc.sbuf_tensor(S.ns + "sg", [128, (FB // 128) * R], BF16))
                ps_f = ctxX.enter_context(nc.psum_tensor(S.ns + "ps_f", [128, 1024], F32))
                nfb = F0 // FB
                items = [(fb, m, Wd) for fb in range(nfb)
                         for m, Wd in ((0, W1T), (1, W2T))]

                def ld_quant(idx):
                    fb, m, Wd = items[idx]
                    sl = idx % 2
                    wap = wf[:, sl * DC * FB: (sl + 1) * DC * FB]
                    wqap = wq[:, sl * DC * FB: (sl + 1) * DC * FB]
                    S.add("sp", lambda sp, wap=wap, Wd=Wd, fb=fb: sp.dma_start(
                        out=wap.rearrange("p (dc f) -> p dc f", dc=DC),
                        in_=Wd.rearrange("(dc p) f -> p dc f", p=128)
                        [:, :, fb * FB:(fb + 1) * FB]),
                        writes=(f"wf{sl}",), dma=True, tag=f"fw{fb}.{m}")
                    emit_quant(S, "v", wqap, f"wq{sl}", wap, f"wf{sl}",
                               qa[:, :], "qa", qb[:, :], "qb",
                               tau[:, m:m + 1], tau[:, 4 + m:5 + m],
                               f"fq{fb}.{m}")

                ld_quant(0)
                for idx, (fb, m, Wd) in enumerate(items):
                    if idx + 1 < len(items):
                        ld_quant(idx + 1)
                    sl = idx % 2
                    wqap = wq[:, sl * DC * FB: (sl + 1) * DC * FB]
                    if True:
                        if m == 0:
                            def s_cb(S_, oi, oc_, t0, tw, ps_ap, psbuf, fb=fb):
                                S_.add("a", lambda a, oi=oi, t0=t0, tw=tw, ps_ap=ps_ap:
                                       a.activation(
                                    sg[:, oi * R + t0: oi * R + t0 + tw], ps_ap,
                                    AF.Sigmoid),
                                    reads=(psbuf,), writes=("sg", f"sg.{oi}"),
                                    tag=f"fs{fb}.{oi}.{t0}")
                            cb = s_cb
                        else:
                            def h_cb(S_, oi, oc_, t0, tw, ps_ap, psbuf, fb=fb):
                                cc = (fb * (FB // 128) + oi) * R + t0
                                S_.add("v", lambda v, oi=oi, cc=cc, t0=t0, tw=tw,
                                       ps_ap=ps_ap: v.tensor_tensor(
                                    hid[:, cc:cc + tw], ps_ap,
                                    sg[:, oi * R + t0: oi * R + t0 + tw], ALU.mult),
                                    reads=(psbuf, f"sg.{oi}"), writes=("hid",),
                                    tag=f"fh{fb}.{oi}.{t0}")
                            cb = h_cb
                        emit_proj(cfg, nc, ps_f, "psf", S, wqap, f"wq{sl}",
                                  FB // 128, xno, R, 0, "lnoxn", R,
                                  list(range(FB // 128)), cb, f"fx{fb}.{m}")
            S.barrier("b_x12")
            with ExitStack() as ctxY:
                w3f = ctxY.enter_context(nc.sbuf_tensor(S.ns + "w3f", [128, FC0 * 128], F32))
                w3q = ctxY.enter_context(nc.sbuf_tensor(S.ns + "w3q", [128, 2 * FC0 * 128], BF16))
                qa3 = ctxY.enter_context(nc.sbuf_tensor(S.ns + "qa3", [128, FC0 * 64], BF16))
                qb3 = ctxY.enter_context(nc.sbuf_tensor(S.ns + "qb3", [128, FC0 * 64], BF16))
                s1r = ctxY.enter_context(nc.sbuf_tensor(S.ns + "s1r", [128, 2 * 512], F32))
                xob = ctxY.enter_context(nc.sbuf_tensor(S.ns + "xob", [128, 2 * 512], F32))
                ps_y = ctxY.enter_context(nc.psum_tensor(S.ns + "ps_y", [128, 1024], F32))
                FCH = FC0 // 2

                def ld3(oc):
                    s3 = oc % 2
                    for hf in range(2):
                        fc0 = hf * FCH
                        fcn = FC0 - fc0 if hf else FCH
                        c0 = fc0 * 128
                        S.add("sp", lambda sp, oc=oc, fc0=fc0, fcn=fcn, c0=c0:
                              sp.dma_start(
                            out=w3f[:, c0:c0 + fcn * 128].rearrange(
                                "p (fc o) -> p fc o", o=128),
                            in_=W3T.rearrange("(fc p) o -> p fc o", p=128)
                            [:, fc0:fc0 + fcn, oc * 128:(oc + 1) * 128]),
                            writes=(f"w3f{hf}",), dma=True, tag=f"yw{oc}.{hf}")
                        emit_quant(S, "v", w3q[:, s3 * FC0 * 128 + c0:
                                               s3 * FC0 * 128 + c0 + fcn * 128],
                                   f"w3q{s3}",
                                   w3f[:, c0:c0 + fcn * 128], f"w3f{hf}",
                                   qa3[:, :], "qa3", qb3[:, :], "qb3",
                                   tau[:, 2:3], tau[:, 6:7], f"yq{oc}.{hf}")

                ld3(0)
                for oc in range(DC):
                    s3 = oc % 2
                    w3qap = w3q[:, s3 * FC0 * 128:(s3 + 1) * FC0 * 128]
                    if oc + 1 < DC:
                        ld3(oc + 1)

                    def y_cb(S_, oi, oc_, t0, tw, ps_ap, psbuf, oc=oc):
                        sl = (oc * (R // 512) + t0 // 512) % 2
                        rap = s1r[:, sl * 512: sl * 512 + tw]
                        oap = xob[:, sl * 512: sl * 512 + tw]
                        S_.add("sp", lambda sp, rap=rap, oc=oc, t0=t0, tw=tw:
                               sp.dma_start(out=rap,
                                            in_=PS1_r[:, oc, t0:t0 + tw]),
                               reads=(f"ps1.{oc}.{t0}",), writes=(f"s1r{sl}",),
                               dma=True, tag=f"ysl{oc}.{t0}")
                        S_.add("v", lambda v, rap=rap, oap=oap, ps_ap=ps_ap:
                               v.tensor_tensor(oap, ps_ap, rap, ALU.add),
                               reads=(psbuf, f"s1r{sl}"), writes=(f"xob{sl}",),
                               tag=f"yo{oc}.{t0}")
                        S_.add("sp", lambda sp, oap=oap, oc=oc, t0=t0, tw=tw:
                               sp.dma_start(out=X1_r[:, oc, t0:t0 + tw], in_=oap),
                               reads=(f"xob{sl}",),
                               writes=(f"{x1keyp}.{oc}.{t0}",),
                               dma=True, dmagrp="x1st", tag=f"yst{oc}.{t0}")

                    emit_proj(cfg, nc, ps_y, "psy", S, w3qap, f"w3q{s3}", 1,
                              hid, R, 0, "hid", R, [0], y_cb, f"y{oc}",
                              kdc=FC0)
    S.barrier("b_pass")


def l0_dram_io(nc, cfg):
    D, HKV, F0 = cfg.D, cfg.HKV, cfg.F0
    io = {}
    io["WQ"] = nc.dram_tensor("WQ", [D, D], F32, kind="ExternalInput").ap()
    io["WK"] = nc.dram_tensor("WK", [D, HKV * 128], F32, kind="ExternalInput").ap()
    io["WV"] = nc.dram_tensor("WV", [D, HKV * 128], F32, kind="ExternalInput").ap()
    io["WO"] = nc.dram_tensor("WO", [D, D], F32, kind="ExternalInput").ap()
    io["W1T"] = nc.dram_tensor("W1T", [D, F0], F32, kind="ExternalInput").ap()
    io["W2T"] = nc.dram_tensor("W2T", [D, F0], F32, kind="ExternalInput").ap()
    io["W3T"] = nc.dram_tensor("W3T", [F0, D], F32, kind="ExternalInput").ap()
    io["PS1"] = nc.dram_tensor("PS1", [D, cfg.R], F32).ap()
    return io


def build_l0(cfg):
    R, D = cfg.R, cfg.D
    H = cfg.W
    T = H + R
    nc = bass.Bass("TRN2", target_bir_lowering=False)
    io = l0_dram_io(nc, cfg)
    io["XT"] = nc.dram_tensor("XT", [D, T], F32, kind="ExternalInput").ap()
    io["SINT"] = nc.dram_tensor("SINT", [128, T], F32, kind="ExternalInput").ap()
    io["COST"] = nc.dram_tensor("COST", [128, T], F32, kind="ExternalInput").ap()
    io["HIND"] = nc.dram_tensor("HIND", [1, T], F32, kind="ExternalInput").ap()
    io["X1dst"] = nc.dram_tensor("X1OUT", [D, R], F32, kind="ExternalOutput").ap()
    io["x1keyp"] = "@x1o"
    kwin = l0_key_windows(cfg, H, R)
    masks, mconsts = l0_mask_tiles(cfg, H, R)
    with ExitStack() as ctx:
        S = Sched(nc, ctx)
        S.consts = {}
        cspecs = {"ident_f32": ([128, 128], F32), "ident_bf": ([128, 128], BF16),
                  "ones_bf": ([128, 128], BF16), "ones_f32": ([128, 128], F32),
                  "ones_row": ([1, 512], F32), "epsln": ([128, 1], F32)}
        for mname in mconsts:
            cspecs[mname] = ([128, 128], F32)
            cspecs[mname + "T"] = ([128, 128], F32)
        setup_consts(nc, ctx, S, cspecs)
        io["tau"] = ctx.enter_context(nc.sbuf_tensor("tau", [128, 8], F32))
        emit_l0_gamma(cfg, nc, ctx, S, io)
        emit_l0_pass(cfg, nc, S, io, kwin, masks)
        allout = [f"@x1o.{oc}.{t0}" for oc in range(cfg.DC)
                  for t0, _ in col_chunks(R, 512)]
        S.add("sp", lambda sp: sp.nop(), reads=tuple(allout), tag="fin")
        block = ctx.enter_context(nc.Block())
        S.emit(block)
    return nc, masks, mconsts



def l0_const_arrays(cfg, mconsts):
    d = {"c_ident_f32": np.eye(128, dtype=np.float32),
         "c_ident_bf": np.eye(128, dtype=BF),
         "c_ones_bf": np.ones((128, 128), BF),
         "c_ones_f32": np.ones((128, 128), np.float32),
         "c_ones_row": np.ones((1, 512), np.float32),
         "c_epsln": np.full((128, 1), EPS_LN, np.float32)}
    for name, arr in mconsts.items():
        d[f"c_{name}"] = arr
        d[f"c_{name}T"] = np.ascontiguousarray(arr.T)
    return d


def stage_l0(cfg, inputs, mconsts):
    H, R = cfg.W, cfg.R
    T = H + R
    x = np.asarray(inputs["L0_attn_x"], np.float32).reshape(cfg.B * cfg.T, cfg.D)
    xT = np.ascontiguousarray(x.T)
    sin, cos = host_sincos(cfg)
    consts = l0_const_arrays(cfg, mconsts)
    wq = np.ascontiguousarray(np.asarray(inputs["L0_attn_q_kernel"], np.float32))
    wk = np.ascontiguousarray(np.asarray(inputs["L0_attn_k_kernel"], np.float32))
    wv = np.ascontiguousarray(np.asarray(inputs["L0_attn_v_kernel"], np.float32))
    wo = np.ascontiguousarray(np.asarray(inputs["L0_attn_out_kernel"], np.float32))
    w1T = np.ascontiguousarray(np.asarray(inputs["L0_ffn_w1"], np.float32).T)
    w2T = np.ascontiguousarray(np.asarray(inputs["L0_ffn_w2"], np.float32).T)
    w3T = np.ascontiguousarray(np.asarray(inputs["L0_ffn_w3"], np.float32).T)
    nblk = cfg.T // cfg.W
    maps = []
    for c in range(cfg.NC):
        beta, n = divmod(c, nblk)
        own0 = beta * cfg.T + n * cfg.W
        if n == 0:
            xs = np.concatenate([np.zeros((cfg.D, H), np.float32),
                                 xT[:, own0:own0 + R]], 1)
            sins = np.concatenate([np.zeros((128, H), np.float32), sin[:, 0:R]], 1)
            coss = np.concatenate([np.zeros((128, H), np.float32), cos[:, 0:R]], 1)
            hindr = np.concatenate([np.full((1, H), NEG, np.float32),
                                    np.zeros((1, R), np.float32)], 1)
        else:
            p0 = n * cfg.W - H
            xs = np.ascontiguousarray(xT[:, own0 - H:own0 + R])
            sins = np.ascontiguousarray(sin[:, p0:p0 + T])
            coss = np.ascontiguousarray(cos[:, p0:p0 + T])
            hindr = np.zeros((1, T), np.float32)
        sins = sins.copy()
        sins[0:64, :] *= -1.0
        maps.append({"XT": np.ascontiguousarray(xs), "WQ": wq, "WK": wk, "WV": wv,
                     "WO": wo, "W1T": w1T, "W2T": w2T, "W3T": w3T,
                     "SINT": np.ascontiguousarray(sins),
                     "COST": np.ascontiguousarray(coss),
                     "HIND": np.ascontiguousarray(hindr), **consts})
    return maps



def emit_l1_body(cfg, nc, S, io, x1deps=(), Tloc=None, exchange_cb=None):
    R, D, DC, T = cfg.R, cfg.D, cfg.DC, cfg.T
    HQ, F1, FC1 = cfg.HQ, cfg.F1, cfg.FC1
    Tloc = Tloc if Tloc is not None else T
    X1T, WQKV, BQKVR, WG = io["X1T"], io["WQKV"], io["BQKVR"], io["WG"]
    WPROJ, BPROJR = io["WPROJ"], io["BPROJR"]
    W0A, W0B, W1A, W1B = io["W0A"], io["W0B"], io["W1A"], io["W1B"]
    B0BR, B1BR = io["B0BR"], io["B1BR"]
    YOUT, KT_s, V_s, QT_s = io["YOUT"], io["KT_s"], io["V_s"], io["QT_s"]
    X1T_r = X1T.rearrange("(dc p) t -> p dc t", p=128)
    Y_r = YOUT.rearrange("(dc p) t -> p dc t", p=128)

    with ExitStack() as ctx:

        m0 = ctx.enter_context(nc.sbuf_tensor("m0", [128, R], BF16))
        m1 = ctx.enter_context(nc.sbuf_tensor("m1", [128, R], BF16))
        m0r = ctx.enter_context(nc.sbuf_tensor("m0r", [1, 512], F32))

        with ExitStack() as ctxQ:
            xn1 = ctxQ.enter_context(nc.sbuf_tensor("xn1", [128, DC * Tloc], BF16))
            with ExitStack() as ctxLN:
                scr = make_ln_scratch(cfg, nc, ctxLN, "l1s")
                emit_ln(cfg, nc, scr, S, X1T, 0, Tloc, xn1, Tloc, 0, "l1n",
                        src_deps=x1deps)
            S.barrier("b_l1ln")
            XN1D_r = io["XN1D"].rearrange("(dc p) t -> p dc t", p=128)
            for dc in range(DC):
                S.add("sp", lambda sp, dc=dc: sp.dma_start(
                    out=XN1D_r[:, dc, :], in_=xn1[:, dc * Tloc:dc * Tloc + R]),
                    reads=("l1nxn",), writes=(f"xn1d.{dc}",), dma=True,
                    dmagrp="kvst", tag=f"xno{dc}")
            with ExitStack() as ctxW:
                wsb = ctxW.enter_context(nc.sbuf_tensor("wsb1", [128, 2 * DC * 2 * 128], BF16))
                bqs = ctxW.enter_context(nc.sbuf_tensor("bqs", [1, 2 * 128], F32))
                ktmp = ctxW.enter_context(nc.sbuf_tensor("ktmp", [128, 512], BF16))
                vstage = ctxW.enter_context(nc.sbuf_tensor("vstage", [128, 512], BF16))
                wg_sb = ctxW.enter_context(nc.sbuf_tensor("wg_sb", [128, DC * 2], BF16))
                gsb = ctxW.enter_context(nc.sbuf_tensor("gsb", [2, 512], F32))
                ps_p = ctxW.enter_context(nc.psum_tensor("ps_p1", [128, 1024], F32))
                ps_tr = ctxW.enter_context(nc.psum_tensor("ps_tr1", [128, 128], BF16))
                ps_g = ctxW.enter_context(nc.psum_tensor("ps_g1", [128, 512], F32))
                S.add("g", lambda g: g.dma_start(
                    out=wg_sb.rearrange("p (dc o) -> p dc o", o=2),
                    in_=WG.rearrange("(dc p) o -> p dc o", p=128)),
                    writes=("wg",), dma=True, tag="ldwg")

                bq_done = set()

                def bias_mm(S_, oc, dcol, tw, ps_ap, psbuf):
                    sl = oc % 2
                    if oc not in bq_done:
                        bq_done.add(oc)
                        S_.add("sp", lambda sp, oc=oc, sl=sl: sp.dma_start(
                            out=bqs[0:1, sl * 128:(sl + 1) * 128],
                            in_=BQKVR[0:1, oc * 128:(oc + 1) * 128]),
                            writes=(f"bqs{sl}",), dma=True, tag=f"ldbq{oc}")
                    S_.add("t", lambda t, oc=oc, sl=sl, tw=tw, ps_ap=ps_ap: t.matmul(
                        ps_ap, bqs[0:1, sl * 128:(sl + 1) * 128],
                        S.consts["ones_row"][0:1, 0:tw], start=False, stop=True),
                        reads=(f"bqs{sl}", "@ones_row"), writes=(psbuf,),
                        tag=f"qkvb{oc}.{dcol}")

                def qkv_cb(S_, oc, dcol, tw, ps_ap, psbuf):
                    h, part = divmod(oc, 3)
                    if part == 0:
                        S_.add("v", lambda v, tw=tw, ps_ap=ps_ap: v.tensor_copy(
                            ktmp[:, 0:tw], ps_ap),
                            reads=(psbuf,), writes=("ktmp",), tag=f"qe{oc}.{dcol}")
                        S_.add("sp", lambda sp, h=h, dcol=dcol, tw=tw: sp.dma_start(
                            out=QT_s[h * 128:(h + 1) * 128, dcol:dcol + tw],
                            in_=ktmp[:, 0:tw]),
                            reads=("ktmp",), writes=(f"qts.{h}.{dcol}",), dma=True, dmagrp="kvst",
                            tag=f"qst{oc}.{dcol}")
                    elif part == 1:
                        S_.add("v", lambda v, tw=tw, ps_ap=ps_ap: v.tensor_copy(
                            ktmp[:, 0:tw], ps_ap),
                            reads=(psbuf,), writes=("ktmp",), tag=f"ke{oc}.{dcol}")
                        S_.add("sp", lambda sp, h=h, dcol=dcol, tw=tw: sp.dma_start(
                            out=KT_s[h * 128:(h + 1) * 128, dcol:dcol + tw],
                            in_=ktmp[:, 0:tw]),
                            reads=("ktmp",), writes=(f"kts.{h}.{dcol}",), dma=True, dmagrp="kvst",
                            tag=f"kst{oc}.{dcol}")
                    else:
                        S_.add("v", lambda v, tw=tw, ps_ap=ps_ap: v.tensor_copy(
                            vstage[:, 0:tw], ps_ap),
                            reads=(psbuf,), writes=("vstage",), tag=f"vev{oc}.{dcol}")
                        S_.add("sp", lambda sp, h=h, dcol=dcol, tw=tw: sp.dma_start(
                            out=V_s[h * 128:(h + 1) * 128, dcol:dcol + tw],
                            in_=vstage[:, 0:tw]),
                            reads=("vstage",), writes=(f"vs.{h}.{dcol}",), dma=True, dmagrp="kvst",
                            tag=f"vst{oc}.{dcol}")

                xp_all = [(xn1, Tloc, 0, "l1nxn", Tloc, 0)]
                xp_own = [(xn1, Tloc, 0, "l1nxn", R, 0)]
                for h in range(HQ):
                    emit_sproj(cfg, nc, ctxW, S, WQKV, "WQKV", wsb, xp_own,
                               3 * h, 1, qkv_cb, f"pq{h}", ps_p, "psp1",
                               blk=2, extra_mm=bias_mm)
                    emit_sproj(cfg, nc, ctxW, S, WQKV, "WQKV", wsb, xp_all,
                               3 * h + 1, 2, qkv_cb, f"pkv{h}", ps_p, "psp1",
                               blk=2, extra_mm=bias_mm)
                for t0, tw in col_chunks(R, 512):
                    for dc in range(DC):
                        S.add("t", lambda t, dc=dc, t0=t0, tw=tw: t.matmul(
                            ps_g[0:2, 0:tw], wg_sb[:, dc * 2:(dc + 1) * 2],
                            xn1[:, dc * Tloc + t0:dc * Tloc + t0 + tw],
                            start=(dc == 0), stop=(dc == DC - 1)),
                            reads=("wg", "l1nxn"), writes=("psg1",), tag=f"g{t0}.{dc}")
                    S.add("v", lambda v, tw=tw: v.tensor_copy(gsb[:, 0:tw],
                                                              ps_g[0:2, 0:tw]),
                          reads=("psg1",), writes=("gsb",), tag=f"ge{t0}")
                    S.add("t", lambda t, tw=tw: t.matmul(
                        ps_g[0:1, 0:tw], S.consts["epm"][:, :], gsb[0:2, 0:tw],
                        start=True, stop=True),
                        reads=("gsb", "@epm"), writes=("psg1",), tag=f"gd{t0}")
                    S.add("v", lambda v, tw=tw: v.tensor_scalar(
                        m0r[0:1, 0:tw], ps_g[0:1, 0:tw], 0.0, None, ALU.is_ge),
                        reads=("psg1",), writes=("m0r",), tag=f"gm{t0}")
                    S.add("t", lambda t, tw=tw: t.matmul(
                        ps_g[:, 0:tw], S.consts["ones_f32"][0:1, 0:128],
                        m0r[0:1, 0:tw], start=True, stop=True),
                        reads=("m0r", "@ones_f32"), writes=("psg1",), tag=f"gb{t0}")
                    S.add("v", lambda v, t0=t0, tw=tw: v.tensor_copy(
                        m0[:, t0:t0 + tw], ps_g[:, 0:tw]),
                        reads=("psg1",), writes=("m0",), tag=f"gmc{t0}")
                    S.add("v", lambda v, t0=t0, tw=tw: v.tensor_scalar(
                        m1[:, t0:t0 + tw], m0[:, t0:t0 + tw], -1.0, 1.0,
                        ALU.mult, ALU.add),
                        reads=("m0",), writes=("m1",), tag=f"gm1{t0}")
            S.barrier("b_l1q")
        if exchange_cb is not None:
            exchange_cb(S)
            S.barrier("b_xch")
        kt_src = io.get("kt_src", KT_s)
        v_src = io.get("v_src", V_s)
        kdep = ("@xdone",) if exchange_cb is not None else None

        with ExitStack() as ctxY:
            y_acc = ctxY.enter_context(nc.sbuf_tensor("y_acc", [128, DC * R], F32))
            with ExitStack() as ctxB:
                attnT = ctxB.enter_context(nc.sbuf_tensor("attnT1", [128, HQ * R], BF16))
                with ExitStack() as ctxAT:
                    kts = ctxAT.enter_context(nc.sbuf_tensor("kts", [128, 2 * T], BF16))
                    vts = ctxAT.enter_context(nc.sbuf_tensor("vts", [128, 2 * T], BF16))
                    qts = ctxAT.enter_context(nc.sbuf_tensor("qts", [128, 2 * R], BF16))

                    def prep_head(S_, h):
                        sl = h % 2
                        kr = kdep if kdep is not None else tuple(
                            f"kts.{h}.{d}" for d, _ in col_chunks(Tloc, 512))
                        vr = kdep if kdep is not None else tuple(
                            f"vs.{h}.{d}" for d, _ in col_chunks(Tloc, 512))
                        S_.add("g", lambda g, h=h, sl=sl: g.dma_start(
                            out=kts[:, sl * T: sl * T + T],
                            in_=kt_src[h * 128:(h + 1) * 128, :]),
                            reads=kr,
                            writes=(f"ktsl{sl}",), dma=True, tag=f"hk{h}")
                        S_.add("g", lambda g, h=h, sl=sl: g.dma_start(
                            out=vts[:, sl * T: sl * T + T],
                            in_=v_src[h * 128:(h + 1) * 128, :]),
                            reads=vr,
                            writes=(f"vtsl{sl}",), dma=True, tag=f"hv{h}")
                        S_.add("g", lambda g, h=h, sl=sl: g.dma_start(
                            out=qts[:, sl * R: sl * R + R],
                            in_=QT_s[h * 128:(h + 1) * 128, :]),
                            reads=tuple(f"qts.{h}.{d}" for d, _ in col_chunks(R, 512)),
                            writes=(f"qtsl{sl}",), dma=True, tag=f"hq{h}")

                    def out_cb(S_, h, q0, qw, out_ps_ap, opsbuf, rec_ap, recbuf):
                        S_.add("v", lambda v: v.tensor_tensor(
                            attnT[:, h * R + q0: h * R + q0 + qw], out_ps_ap,
                            rec_ap, ALU.mult),
                            reads=(opsbuf, recbuf), writes=("attnT1",),
                            tag=f"ao1{h}.{q0}")

                    emit_attn(cfg, nc, ctxAT, S, "a1", heads=HQ, kv_of=None,
                              Tkeys=T, R=R,
                              kwin=[(kc, 0, R) for kc in range(T // 128)],
                              masks={}, scale=L1_SCALE,
                              prep_head=prep_head,
                              kt_ap=lambda h: (kts[:, (h % 2) * T:(h % 2) * T + T],
                                               f"ktsl{h % 2}"),
                              v_dmaj=lambda h: (vts[:, (h % 2) * T:
                                                    (h % 2) * T + T], f"vtsl{h % 2}"),
                              q_ap=lambda h: (qts[:, (h % 2) * R:(h % 2) * R + R],
                                              f"qtsl{h % 2}"),
                              out_cb=out_cb)
                S.barrier("b_l1at")
                with ExitStack() as ctxP:
                    wsbP = ctxP.enter_context(
                        nc.sbuf_tensor("wsbP1", [128, 2 * DC * 2 * 128], BF16))
                    bpr = ctxP.enter_context(nc.sbuf_tensor("bpr", [1, D], F32))
                    x1r = ctxP.enter_context(nc.sbuf_tensor("x1r", [128, 2 * 512], F32))
                    ps_o = ctxP.enter_context(nc.psum_tensor("ps_o1", [128, 1024], F32))
                    S.add("sp", lambda sp: sp.dma_start(out=bpr[:, :], in_=BPROJR),
                          writes=("bpr",), dma=True, tag="ldbpr")

                    def pbias_mm(S_, oc, dcol, tw, ps_ap, psbuf):
                        S_.add("t", lambda t, oc=oc, tw=tw, ps_ap=ps_ap: t.matmul(
                            ps_ap, bpr[0:1, oc * 128:oc * 128 + 128],
                            S.consts["ones_row"][0:1, 0:tw], start=False, stop=True),
                            reads=("bpr", "@ones_row"), writes=(psbuf,),
                            tag=f"pb{oc}.{dcol}")

                    def p_cb(S_, oc, dcol, tw, ps_ap, psbuf):
                        sl = (oc * (R // 512) + dcol // 512) % 2
                        xap = x1r[:, sl * 512: sl * 512 + tw]
                        S_.add("sp", lambda sp, oc=oc, dcol=dcol, tw=tw, xap=xap:
                               sp.dma_start(out=xap,
                                            in_=X1T_r[:, oc, dcol:dcol + tw]),
                               reads=tuple(x1deps), writes=(f"x1r{sl}",), dma=True,
                               tag=f"pxl{oc}.{dcol}")
                        S_.add("v", lambda v, oc=oc, dcol=dcol, tw=tw, xap=xap,
                               ps_ap=ps_ap: v.tensor_tensor(
                            y_acc[:, oc * R + dcol: oc * R + dcol + tw], ps_ap,
                            xap, ALU.add),
                            reads=(psbuf, f"x1r{sl}"), writes=("y_acc", f"ya.{oc}.{dcol}"),
                            tag=f"pya{oc}.{dcol}")

                    emit_sproj(cfg, nc, ctxP, S, WPROJ, "WPROJ", wsbP,
                               [(attnT, R, 0, "attnT1", R, 0)], 0, DC, p_cb,
                               "pp", ps_o, "pso1", blk=2, extra_mm=pbias_mm)
                S.barrier("b_l1p")
            with ExitStack() as ctxF:
                hbuf = ctxF.enter_context(nc.sbuf_tensor("hbuf", [128, FC1 * R], BF16))
                xn1o = ctxF.enter_context(nc.sbuf_tensor("xn1o", [128, DC * R], BF16))
                XN1D_r2 = io["XN1D"].rearrange("(dc p) t -> p dc t", p=128)
                for dc in range(DC):
                    S.add("sp", lambda sp, dc=dc: sp.dma_start(
                        out=xn1o[:, dc * R:(dc + 1) * R], in_=XN1D_r2[:, dc, :]),
                        reads=(f"xn1d.{dc}",), writes=("xn1o",), dma=True,
                        dmagrp="kvst", tag=f"xnl{dc}")
                wsbF = ctxF.enter_context(
                    nc.sbuf_tensor("wsbF", [128, 2 * max(DC, FC1) * 1 * 128], BF16))
                brow = ctxF.enter_context(nc.sbuf_tensor("brow", [1, 2 * D], F32))
                tsb = ctxF.enter_context(nc.sbuf_tensor("tsb", [128, 512], F32))
                ps_f = ctxF.enter_context(nc.psum_tensor("ps_f1", [128, 1024], F32))
                S.add("sp", lambda sp: sp.dma_start(out=brow[0:1, 0:D], in_=B0BR),
                      writes=("b0br",), dma=True, tag="ldb0b")
                S.add("sp", lambda sp: sp.dma_start(out=brow[0:1, D:2 * D], in_=B1BR),
                      writes=("b1br",), dma=True, tag="ldb1b")

                for ex in range(2):
                    WA, WB = (W0A, W0B) if ex == 0 else (W1A, W1B)
                    bpp = S.consts["b0app" if ex == 0 else "b1app"]
                    bppbuf = "@b0app" if ex == 0 else "@b1app"
                    mask_sb, maskbuf = (m0, "m0") if ex == 0 else (m1, "m1")
                    act = AF.Gelu if ex == 0 else AF.Relu

                    def h_cb(S_, oc, dcol, tw, ps_ap, psbuf, ex=ex, act=act,
                             bpp=bpp, bppbuf=bppbuf):
                        S_.add("a", lambda a, oc=oc, dcol=dcol, tw=tw, ps_ap=ps_ap:
                               a.activation(
                            hbuf[:, oc * R + dcol: oc * R + dcol + tw], ps_ap,
                            act, bias=bpp[:, oc:oc + 1]),
                            reads=(psbuf, bppbuf), writes=("hbuf", f"hb.{ex}.{oc}"),
                            tag=f"he{ex}.{oc}.{dcol}")

                    emit_sproj(cfg, nc, ctxF, S, WA, f"WA{ex}", wsbF,
                               [(xn1o, R, 0, "xn1o", R, 0)], 0, FC1, h_cb,
                               f"fa{ex}", ps_f, "psf1", blk=1)
                    S.barrier(f"b_fa{ex}")

                    def obias_mm(S_, oc, dcol, tw, ps_ap, psbuf, ex=ex):
                        S_.add("t", lambda t, oc=oc, tw=tw, ps_ap=ps_ap, ex=ex:
                               t.matmul(
                            ps_ap, brow[0:1, ex * D + oc * 128: ex * D + oc * 128 + 128],
                            S.consts["ones_row"][0:1, 0:tw], start=False, stop=True),
                            reads=("b0br" if ex == 0 else "b1br", "@ones_row"),
                            writes=(psbuf,), tag=f"ob{ex}.{oc}.{dcol}")

                    def o_cb(S_, oc, dcol, tw, ps_ap, psbuf, ex=ex,
                             mask_sb=mask_sb, maskbuf=maskbuf):
                        S_.add("v", lambda v, dcol=dcol, tw=tw, ps_ap=ps_ap,
                               mask_sb=mask_sb: v.tensor_tensor(
                            tsb[:, 0:tw], ps_ap, mask_sb[:, dcol:dcol + tw],
                            ALU.mult),
                            reads=(psbuf, maskbuf), writes=("tsb",),
                            tag=f"om{ex}.{oc}.{dcol}")
                        S_.add("v", lambda v, oc=oc, dcol=dcol, tw=tw: v.tensor_tensor(
                            y_acc[:, oc * R + dcol: oc * R + dcol + tw],
                            y_acc[:, oc * R + dcol: oc * R + dcol + tw],
                            tsb[:, 0:tw], ALU.add),
                            reads=("tsb", f"ya.{oc}.{dcol}", "y_acc"),
                            writes=("y_acc", f"ya.{oc}.{dcol}"),
                            tag=f"oy{ex}.{oc}.{dcol}")

                    emit_sproj(cfg, nc, ctxF, S, WB, f"WB{ex}", wsbF,
                               [(hbuf, R, 0, "hbuf", R, 0)], 0, DC, o_cb,
                               f"fb{ex}", ps_f, "psf1", blk=1, kdc=FC1,
                               extra_mm=obias_mm)
                    S.barrier(f"b_fb{ex}")
                for oc in range(DC):
                    S.add("sp", lambda sp, oc=oc: sp.dma_start(
                        out=Y_r[:, oc, :], in_=y_acc[:, oc * R:(oc + 1) * R]),
                        reads=("y_acc",), writes=(f"yo.{oc}",), dma=True, dmagrp="yst",
                        tag=f"yst{oc}")
                S.add("sp", lambda sp: sp.nop(),
                      reads=tuple(f"yo.{oc}" for oc in range(DC)), tag="fin1")


def build_l1(cfg):
    R, D, T, FC1 = cfg.R, cfg.D, cfg.T, cfg.FC1
    F1 = cfg.F1
    nc = bass.Bass("TRN2", target_bir_lowering=False)
    io = l1_dram_io(nc, cfg)
    io["X1T"] = nc.dram_tensor("X1T", [D, T], F32, kind="ExternalInput").ap()
    io["YOUT"] = nc.dram_tensor("YOUT", [D, R], F32, kind="ExternalOutput").ap()
    with ExitStack() as ctx:
        S = Sched(nc, ctx)
        S.consts = {}
        setup_consts(nc, ctx, S, {
            "ident_f32": ([128, 128], F32), "ident_bf": ([128, 128], BF16),
            "ones_bf": ([128, 128], BF16), "ones_row": ([1, 512], F32),
            "ones_f32": ([128, 128], F32), "epsln": ([128, 1], F32),
            "b0app": ([128, FC1], F32), "b1app": ([128, FC1], F32),
            "epm": ([2, 1], F32)})
        emit_l1_body(cfg, nc, S, io)
        block = ctx.enter_context(nc.Block())
        S.emit(block)
    return nc


def l1_dram_io(nc, cfg, Tloc=None):
    D, T, R, F1, FC1 = cfg.D, cfg.T, cfg.R, cfg.F1, cfg.FC1
    T = Tloc if Tloc is not None else T
    io = {}
    io["WQKV"] = nc.dram_tensor("WQKV", [D, 3 * D], F32, kind="ExternalInput").ap()
    io["BQKVR"] = nc.dram_tensor("BQKVR", [1, 3 * D], F32, kind="ExternalInput").ap()
    io["WG"] = nc.dram_tensor("WG", [D, 2], F32, kind="ExternalInput").ap()
    io["WPROJ"] = nc.dram_tensor("WPROJ", [D, D], F32, kind="ExternalInput").ap()
    io["BPROJR"] = nc.dram_tensor("BPROJR", [1, D], F32, kind="ExternalInput").ap()
    io["W0A"] = nc.dram_tensor("W0A", [D, F1], F32, kind="ExternalInput").ap()
    io["W0B"] = nc.dram_tensor("W0B", [F1, D], F32, kind="ExternalInput").ap()
    io["B0BR"] = nc.dram_tensor("B0BR", [1, D], F32, kind="ExternalInput").ap()
    io["W1A"] = nc.dram_tensor("W1A", [D, F1], F32, kind="ExternalInput").ap()
    io["W1B"] = nc.dram_tensor("W1B", [F1, D], F32, kind="ExternalInput").ap()
    io["B1BR"] = nc.dram_tensor("B1BR", [1, D], F32, kind="ExternalInput").ap()
    io["KT_s"] = nc.dram_tensor("KT_s", [D, T], BF16).ap()
    io["V_s"] = nc.dram_tensor("V_s", [D, T], BF16).ap()
    io["QT_s"] = nc.dram_tensor("QT_s", [D, R], BF16).ap()
    io["XN1D"] = nc.dram_tensor("XN1D", [D, R], BF16).ap()
    return io


def stage_l1(cfg, inputs, x1T):
    D, R, T, FC1 = cfg.D, cfg.R, cfg.T, cfg.FC1
    wqkv = np.ascontiguousarray(np.asarray(inputs["L1_attn_Wqkv"], np.float32).T)
    bqkvr = np.asarray(inputs["L1_attn_bqkv"], np.float32).reshape(1, 3 * D)
    wg = np.ascontiguousarray(np.asarray(inputs["L1_ffn_Wg"], np.float32).T)
    epm = np.array([[1.0], [-1.0]], np.float32)
    wproj = np.ascontiguousarray(np.asarray(inputs["L1_attn_Wproj"], np.float32).T)
    bprojr = np.asarray(inputs["L1_attn_bproj"], np.float32).reshape(1, D)
    w0a = np.ascontiguousarray(np.asarray(inputs["L1_ffn_W0a"], np.float32).T)
    b0app = np.ascontiguousarray(
        np.asarray(inputs["L1_ffn_b0a"], np.float32).reshape(FC1, 128).T)
    w0b = np.ascontiguousarray(np.asarray(inputs["L1_ffn_W0b"], np.float32).T)
    b0br = np.asarray(inputs["L1_ffn_b0b"], np.float32).reshape(1, D)
    w1a = np.ascontiguousarray(np.asarray(inputs["L1_ffn_W1a"], np.float32).T)
    b1app = np.ascontiguousarray(
        np.asarray(inputs["L1_ffn_b1a"], np.float32).reshape(FC1, 128).T)
    w1b = np.ascontiguousarray(np.asarray(inputs["L1_ffn_W1b"], np.float32).T)
    b1br = np.asarray(inputs["L1_ffn_b1b"], np.float32).reshape(1, D)
    base = {"WQKV": wqkv, "BQKVR": bqkvr, "WG": wg, "WPROJ": wproj,
            "BPROJR": bprojr, "W0A": w0a, "W0B": w0b,
            "B0BR": b0br, "W1A": w1a, "W1B": w1b, "B1BR": b1br,
            "c_ident_f32": np.eye(128, dtype=np.float32),
            "c_ones_bf": np.ones((128, 128), BF),
            "c_ones_f32": np.ones((128, 128), np.float32),
            "c_ones_row": np.ones((1, 512), np.float32),
            "c_epsln": np.full((128, 1), EPS_LN, np.float32),
            "c_b0app": b0app, "c_b1app": b1app, "c_epm": epm,
            "c_ident_bf": np.eye(128, dtype=BF)}
    qpc = cfg.NC // cfg.B
    maps = []
    for c in range(cfg.NC):
        beta, j = divmod(c, qpc)
        xb = x1T[:, beta * T:(beta + 1) * T]
        rot = np.concatenate([xb[:, j * R:(j + 1) * R], xb[:, :j * R],
                              xb[:, (j + 1) * R:]], axis=1)
        maps.append(dict(base, X1T=np.ascontiguousarray(rot)))
    return maps



MAGIC_POLL_N = 2000000


def emit_exchange(cfg, nc, S, io, Tloc):
    D, T, R = cfg.D, cfg.T, cfg.R
    KT_s, V_s = io["KT_s"], io["V_s"]
    K_sh, V_sh, FLG = io["K_sh"], io["V_sh"], io["FLG"]
    cid_sb, nonce_sb = io["cid_sb"], io["nonce_sb"]
    poll_sb, sx = io["poll_sb"], io["sx"]

    kvreads = [f"kts.{h}.{d}" for h in range(cfg.HQ) for d, _ in col_chunks(Tloc, 512)]
    kvreads += [f"vs.{h}.{d}" for h in range(cfg.HQ) for d, _ in col_chunks(Tloc, 512)]
    kvreads += ["@cid_sb", "@nonce_sb"]

    def fn(g):
        with (g.register("rs") as rs, g.register("ro") as ro,
              g.register("fl") as fl, g.register("dd") as dd,
              g.register("ii") as ii, g.register("cnt") as cnt,
              g.register("rn") as rn):
            g.reg_load(rs, cid_sb[:1, :1])
            g.reg_mod(rs, rs, 2)
            g.reg_load(rn, nonce_sb[:1, :1])
            g.reg_mul(ro, rs, Tloc)
            g.dma_start(out=bass.AP(K_sh.tensor, ro, [[T, D], [1, 1], [1, Tloc]]),
                        in_=KT_s).then_inc(sx, 16)
            g.reg_mul(ro, rs, Tloc)
            g.dma_start(out=bass.AP(V_sh.tensor, ro, [[T, D], [1, 1], [1, Tloc]]),
                        in_=V_s).then_inc(sx, 16)
            g.wait_ge(sx, 32)
            g.reg_mul(ro, rs, 16)
            g.dma_start(out=bass.AP(FLG.tensor, ro, [[16, 1], [1, 1], [1, 16]]),
                        in_=nonce_sb[:, :]).then_inc(sx, 16)
            g.reg_mul(ro, rs, -16)
            g.reg_add(ro, ro, 16)
            g.reg_mov(dd, 1)
            g.reg_mov(ii, 0)
            g.reg_mov(cnt, 48)
            with g.While(dd):
                g.reg_add(cnt, cnt, 16)
                g.dma_start(out=poll_sb[:, :],
                            in_=bass.AP(FLG.tensor, ro, [[16, 1], [1, 1], [1, 16]])
                            ).then_inc(sx, 16)
                g.wait_ge(sx, cnt)
                g.reg_load(fl, poll_sb[:1, :1])
                g.reg_sub(dd, fl, rn)
                g.reg_add(ii, ii, 1)
                with g.If_cmp(ii, MAGIC_POLL_N, "IS_GE"):
                    g.reg_mov(dd, 0)
        return g.nop()

    S.add("g", fn, reads=tuple(kvreads), writes=("@xdone",), tag="exch")


def build_fused(cfg, npass=4):
    R, D, T = cfg.R, cfg.D, cfg.T
    H = cfg.W
    TL0 = H + R
    Tloc = npass * R
    FC1 = cfg.FC1
    nc = bass.Bass("TRN2", target_bir_lowering=False)
    io0 = l0_dram_io(nc, cfg)
    io1 = l1_dram_io(nc, cfg, Tloc=Tloc)
    X1OWN = nc.dram_tensor("X1OWN", [D, Tloc], F32).ap()
    io1["X1T"] = X1OWN
    io1["YOUT"] = nc.dram_tensor("YOUT", [D, R], F32, kind="ExternalOutput").ap()
    xts, sins, coss, hinds = [], [], [], []
    for p in range(npass):
        xts.append(nc.dram_tensor(f"XT{p}", [D, TL0], F32, kind="ExternalInput").ap())
        sins.append(nc.dram_tensor(f"SINT{p}", [128, TL0], F32, kind="ExternalInput").ap())
        coss.append(nc.dram_tensor(f"COST{p}", [128, TL0], F32, kind="ExternalInput").ap())
        hinds.append(nc.dram_tensor(f"HIND{p}", [1, TL0], F32, kind="ExternalInput").ap())
    if npass == 2:
        CID = nc.dram_tensor("CID", [1, 1], I32, kind="ExternalInput").ap()
        NONCE = nc.dram_tensor("NONCE", [1, 16], I32, kind="ExternalInput").ap()
        io1["K_sh"] = nc.dram_tensor("K_sh", [D, T], BF16, addr_space="Shared").ap()
        io1["V_sh"] = nc.dram_tensor("V_sh", [D, T], BF16, addr_space="Shared").ap()
        io1["FLG"] = nc.dram_tensor("FLG_sh", [2, 16], I32, addr_space="Shared").ap()
        io1["kt_src"] = io1["K_sh"]
        io1["v_src"] = io1["V_sh"]
    kwin = l0_key_windows(cfg, H, R)
    masks, mconsts = l0_mask_tiles(cfg, H, R)
    with ExitStack() as ctx:
        S = Sched(nc, ctx)
        S.consts = {}
        cspecs = {"ident_f32": ([128, 128], F32), "ident_bf": ([128, 128], BF16),
                  "ones_bf": ([128, 128], BF16), "ones_f32": ([128, 128], F32),
                  "ones_row": ([1, 512], F32), "epsln": ([128, 1], F32),
                  "b0app": ([128, FC1], F32), "b1app": ([128, FC1], F32),
                  "epm": ([2, 1], F32)}
        for mname in mconsts:
            cspecs[mname] = ([128, 128], F32)
            cspecs[mname + "T"] = ([128, 128], F32)
        setup_consts(nc, ctx, S, cspecs)
        exchange_cb = None
        if npass == 2:
            io1["cid_sb"] = ctx.enter_context(nc.sbuf_tensor("cid_sb", [1, 1], I32))
            io1["nonce_sb"] = ctx.enter_context(nc.sbuf_tensor("nonce_sb", [1, 16], I32))
            S.add("sp", lambda sp: sp.dma_start(out=io1["cid_sb"][:, :], in_=CID),
                  writes=("@cid_sb",), dma=True, dmagrp="consts", tag="ldcid")
            S.add("sp", lambda sp: sp.dma_start(out=io1["nonce_sb"][:, :], in_=NONCE),
                  writes=("@nonce_sb",), dma=True, dmagrp="consts", tag="ldnonce")
            io1["poll_sb"] = ctx.enter_context(nc.sbuf_tensor("xpoll", [1, 16], I32))
            io1["sx"] = ctx.enter_context(nc.semaphore("s_xch"))
            exchange_cb = lambda S_: emit_exchange(cfg, nc, S_, io1, Tloc)
        io0["tau"] = ctx.enter_context(nc.sbuf_tensor("tau", [128, 8], F32))
        emit_l0_gamma(cfg, nc, ctx, S, io0)
        x1deps = []
        for p in range(npass):
            S.ns = f"p{p}."
            iop = dict(io0, XT=xts[p], SINT=sins[p], COST=coss[p],
                       HIND=hinds[p], X1dst=X1OWN[:, p * R:(p + 1) * R],
                       x1keyp=f"@x1.{p}")
            emit_l0_pass(cfg, nc, S, iop, kwin, masks)
            x1deps += [f"@x1.{p}.{oc}.{t0}" for oc in range(cfg.DC)
                       for t0, _ in col_chunks(R, 512)]
        S.ns = "l1."
        emit_l1_body(cfg, nc, S, io1, x1deps=tuple(x1deps), Tloc=Tloc,
                     exchange_cb=exchange_cb)
        S.ns = ""
        block = ctx.enter_context(nc.Block())
        S.emit(block)
    return nc, masks, mconsts


def stage_fused(cfg, inputs, mconsts, npass=4):
    H, R, T = cfg.W, cfg.R, cfg.T
    TL0 = H + R
    nblk = cfg.NC // cfg.B
    x = np.asarray(inputs["L0_attn_x"], np.float32).reshape(cfg.B * cfg.T, cfg.D)
    xT = np.ascontiguousarray(x.T)
    sin, cos = host_sincos(cfg)
    sinF = sin.copy()
    sinF[0:64, :] *= -1.0
    consts = l0_const_arrays(cfg, mconsts)
    FC1 = cfg.FC1
    consts["c_b0app"] = np.ascontiguousarray(
        np.asarray(inputs["L1_ffn_b0a"], np.float32).reshape(FC1, 128).T)
    consts["c_b1app"] = np.ascontiguousarray(
        np.asarray(inputs["L1_ffn_b1a"], np.float32).reshape(FC1, 128).T)
    consts["c_epm"] = np.array([[1.0], [-1.0]], np.float32)
    D = cfg.D
    base = {
        "WQ": np.ascontiguousarray(np.asarray(inputs["L0_attn_q_kernel"], np.float32)),
        "WK": np.ascontiguousarray(np.asarray(inputs["L0_attn_k_kernel"], np.float32)),
        "WV": np.ascontiguousarray(np.asarray(inputs["L0_attn_v_kernel"], np.float32)),
        "WO": np.ascontiguousarray(np.asarray(inputs["L0_attn_out_kernel"], np.float32)),
        "W1T": np.ascontiguousarray(np.asarray(inputs["L0_ffn_w1"], np.float32).T),
        "W2T": np.ascontiguousarray(np.asarray(inputs["L0_ffn_w2"], np.float32).T),
        "W3T": np.ascontiguousarray(np.asarray(inputs["L0_ffn_w3"], np.float32).T),
        "WQKV": np.ascontiguousarray(np.asarray(inputs["L1_attn_Wqkv"], np.float32).T),
        "BQKVR": np.asarray(inputs["L1_attn_bqkv"], np.float32).reshape(1, 3 * D),
        "WG": np.ascontiguousarray(np.asarray(inputs["L1_ffn_Wg"], np.float32).T),
        "WPROJ": np.ascontiguousarray(np.asarray(inputs["L1_attn_Wproj"], np.float32).T),
        "BPROJR": np.asarray(inputs["L1_attn_bproj"], np.float32).reshape(1, D),
        "W0A": np.ascontiguousarray(np.asarray(inputs["L1_ffn_W0a"], np.float32).T),
        "W0B": np.ascontiguousarray(np.asarray(inputs["L1_ffn_W0b"], np.float32).T),
        "B0BR": np.asarray(inputs["L1_ffn_b0b"], np.float32).reshape(1, D),
        "W1A": np.ascontiguousarray(np.asarray(inputs["L1_ffn_W1a"], np.float32).T),
        "W1B": np.ascontiguousarray(np.asarray(inputs["L1_ffn_W1b"], np.float32).T),
        "B1BR": np.asarray(inputs["L1_ffn_b1b"], np.float32).reshape(1, D),
        **consts}
    nonce = np.random.randint(1, 2 ** 31 - 1, dtype=np.int32)
    maps = []
    for c in range(cfg.NC):
        beta, j = divmod(c, nblk)
        d = dict(base)
        if npass == 2:
            order = [[0, 1], [2, 3], [1, 0], [3, 2]][j]
            d["CID"] = np.full((1, 1), c, np.int32)
            d["NONCE"] = np.full((1, 16), nonce, np.int32)
        else:
            order = [(j + p) % nblk for p in range(nblk)]
        for p in range(npass):
            n = order[p]
            own0 = beta * T + n * R
            if n == 0:
                xs = np.concatenate([np.zeros((cfg.D, H), np.float32),
                                     xT[:, own0:own0 + R]], 1)
                ss = np.concatenate([np.zeros((128, H), np.float32),
                                     sinF[:, 0:R]], 1)
                cc = np.concatenate([np.zeros((128, H), np.float32),
                                     cos[:, 0:R]], 1)
                hh = np.concatenate([np.full((1, H), NEG, np.float32),
                                     np.zeros((1, R), np.float32)], 1)
            else:
                p0 = n * R - H
                xs = xT[:, own0 - H:own0 + R]
                ss = sinF[:, p0:p0 + TL0]
                cc = cos[:, p0:p0 + TL0]
                hh = np.zeros((1, TL0), np.float32)
            d[f"XT{p}"] = np.ascontiguousarray(xs)
            d[f"SINT{p}"] = np.ascontiguousarray(ss)
            d[f"COST{p}"] = np.ascontiguousarray(cc)
            d[f"HIND{p}"] = np.ascontiguousarray(hh)
        maps.append(d)
    return maps


def run_fused(cfg, inputs, runner=None, npass=4):
    nc, masks, mconsts = build_fused(cfg, npass=npass)
    maps = stage_fused(cfg, inputs, mconsts, npass=npass)
    if runner is None:
        res = run_bass_kernel_spmd(nc, maps, core_ids=list(range(cfg.NC)))
        outs = [res.results[c] for c in range(cfg.NC)]
    else:
        outs = runner(nc, maps)
    yT = np.zeros((cfg.D, cfg.B * cfg.T), np.float32)
    nblk = cfg.NC // cfg.B
    for c in range(cfg.NC):
        beta, j = divmod(c, nblk)
        qb = [[0, 1], [2, 3], [1, 0], [3, 2]][j][0] if npass == 2 else j
        yT[:, beta * cfg.T + qb * cfg.R: beta * cfg.T + (qb + 1) * cfg.R] = \
            np.asarray(outs[c]["YOUT"])
    return np.ascontiguousarray(yT.T).reshape(cfg.B, cfg.T, cfg.D)



FUSED = True
NPASS = 2


def kernel(**inputs):
    cfg = CFG
    inputs.pop("L1_attn_key", None)
    inputs.pop("L1_attn_value", None)
    if FUSED:
        return run_fused(cfg, inputs, npass=NPASS)
    nc0, masks, mconsts = build_l0(cfg)
    maps0 = stage_l0(cfg, inputs, mconsts)
    res0 = run_bass_kernel_spmd(nc0, maps0, core_ids=list(range(cfg.NC)))
    x1T = np.zeros((cfg.D, cfg.B * cfg.T), np.float32)
    nblk = cfg.T // cfg.W
    for c in range(cfg.NC):
        beta, n = divmod(c, nblk)
        own0 = beta * cfg.T + n * cfg.W
        x1T[:, own0:own0 + cfg.R] = np.asarray(res0.results[c]["X1OUT"])
    nc1 = build_l1(cfg)
    maps1 = stage_l1(cfg, inputs, x1T)
    res1 = run_bass_kernel_spmd(nc1, maps1, core_ids=list(range(cfg.NC)))
    yT = np.zeros((cfg.D, cfg.B * cfg.T), np.float32)
    qpc = cfg.NC // cfg.B
    for c in range(cfg.NC):
        beta, j = divmod(c, qpc)
        yT[:, beta * cfg.T + j * cfg.R: beta * cfg.T + (j + 1) * cfg.R] = \
            np.asarray(res1.results[c]["YOUT"])
    return np.ascontiguousarray(yT.T).reshape(cfg.B, cfg.T, cfg.D)


# version 17
# speedup vs baseline: 2.2044x; 1.0009x over previous
import math
from contextlib import ExitStack
from dataclasses import dataclass

import numpy as np
import ml_dtypes

import concourse.bass as bass
import concourse.mybir as mybir
from concourse.bass_utils import run_bass_kernel_spmd

F32 = mybir.dt.float32
BF16 = mybir.dt.bfloat16
I32 = mybir.dt.int32
AF = mybir.ActivationFunctionType
ALU = mybir.AluOpType
AX = mybir.AxisListType

BF = ml_dtypes.bfloat16
NEG = -60000.0
EPS_LN = 1e-5
EPS_Q = 1e-8
L0_SCALE = 1.0 / math.sqrt(128.0)
L1_SCALE = math.sqrt(128.0)


@dataclass(frozen=True)
class Cfg:
    D: int = 2048
    HQ: int = 16
    HKV: int = 4
    W: int = 1024
    T: int = 4096
    B: int = 2
    F0: int = 5632
    F1: int = 4096
    NC: int = 8

    @property
    def DC(self):
        return self.D // 128

    @property
    def R(self):
        return self.B * self.T // self.NC

    @property
    def FC0(self):
        return self.F0 // 128

    @property
    def FC1(self):
        return self.F1 // 128

    @property
    def GQ(self):
        return self.HQ // self.HKV


CFG = Cfg()


def col_chunks(total, size, start=0):
    out = []
    s = start
    end = start + total
    while s < end:
        out.append((s, min(size, end - s)))
        s += size
    return out



class Sched:

    DMA_POOL = 20

    def __init__(self, nc, ctx):
        self.nc = nc
        self.ctx = ctx
        self.ops = []
        self.writer = {}
        self.readers = {}
        self.last_of = {}
        self.ns = ""

    def _k(self, b):
        return b if b.startswith("@") else self.ns + b

    def add(self, engine, fn, reads=(), writes=(), dma=False, dmagrp=None,
            tag=""):
        oid = len(self.ops)
        raw_w0 = writes[0] if writes else None
        reads = [self._k(b) for b in reads]
        writes = [self._k(b) for b in writes]
        deps = set()
        for b in reads:
            w = self.writer.get(b)
            if w is not None:
                deps.add(w)
        for b in writes:
            w = self.writer.get(b)
            if w is not None:
                deps.add(w)
            deps.update(self.readers.get(b, ()))
        dmakey = (dmagrp or raw_w0) if dma else None
        self.ops.append(dict(oid=oid, e=engine, fn=fn, deps=deps, dma=dma,
                             dmakey=dmakey, tag=self.ns + tag))
        self.last_of[("d", engine, dmakey) if dma else ("c", engine)] = oid
        for b in reads:
            self.readers.setdefault(b, set()).add(oid)
        for b in writes:
            self.writer[b] = oid
            self.readers[b] = set()
        return oid

    def dep(self, oid_from_buf):
        return self.writer.get(oid_from_buf)

    def barrier(self, tag="bar"):
        snap = dict(self.last_of)
        for e in ("sp", "g", "t", "v", "a"):
            oid = len(self.ops)
            deps = set(snap.values())
            self.ops.append(dict(oid=oid, e=e, fn=(lambda eng: eng.nop()),
                                 deps=deps, dma=False, dmakey=None,
                                 tag=f"{tag}.{e}"))
            self.last_of[("c", e)] = oid

    def emit(self, block):
        ops = self.ops
        for o in ops:
            o["rdeps"] = set()
            for d in o["deps"]:
                if ops[d]["e"] == "t" and o["e"] == "t" and not ops[d]["dma"] and not o["dma"]:
                    continue
                if d == o["oid"]:
                    continue
                o["rdeps"].add(d)
        awaited = set()
        for o in ops:
            awaited.update(o["rdeps"])

        sems = {}

        def get_sem(key):
            if key not in sems:
                h = self.ctx.enter_context(self.nc.semaphore(f"s_{len(sems)}"))
                sems[key] = [h, 0]
            return sems[key]

        for o in ops:
            if o["dma"]:
                s = get_sem(("d", o["e"], o["dmakey"]))
                s[1] += 16
                o["sem"], o["tick"] = s, s[1]
            elif o["oid"] in awaited:
                s = get_sem(("c", o["e"]))
                s[1] += 1
                o["sem"], o["tick"] = s, s[1]
            else:
                o["sem"], o["tick"] = None, None

        hw = {}
        dma_ops = {}
        for o in ops:
            if o["dma"]:
                dma_ops.setdefault(id(o["sem"]), []).append((o["oid"], o["tick"]))

        import bisect

        def dma_wait_val(sem, waiter_oid):
            lst = dma_ops[id(sem)]
            i = bisect.bisect_left(lst, (waiter_oid, -1)) - 1
            return lst[i][1]

        def body_for(ename):
            ops_e = [o for o in ops if o["e"] == ename]

            def body(eng):
                for o in ops_e:
                    for d in sorted(o["rdeps"]):
                        od = ops[d]
                        sem, tick = od["sem"], od["tick"]
                        assert sem is not None, (ename, o["tag"], od["tag"])
                        if od["dma"]:
                            tick = dma_wait_val(sem, o["oid"])
                        k = (ename, id(sem))
                        if hw.get(k, 0) < tick:
                            eng.wait_ge(sem[0], tick)
                            hw[k] = tick
                    inst = o["fn"](eng)
                    if o["sem"] is not None:
                        if inst is None:
                            assert not o["dma"], o["tag"]
                            eng.sem_inc(o["sem"][0], 1)
                        else:
                            inst.then_inc(o["sem"][0], 16 if o["dma"] else 1)
            return body

        for ename in ("sp", "g", "t", "v", "a"):
            if any(o["e"] == ename for o in self.ops):
                getattr(block, {"sp": "sync", "g": "gpsimd", "t": "tensor",
                                "v": "vector", "a": "scalar"}[ename])(body_for(ename))



def host_sincos(cfg):
    hd, t, theta = 128, cfg.T, 10000.0
    fraction = np.arange(0, hd, 2, dtype=np.float64) / hd
    timescale = theta ** fraction
    ang = np.arange(t, dtype=np.float64)[:, None] / timescale[None, :]
    ang = np.concatenate([ang, ang], axis=-1)
    return (np.ascontiguousarray(np.sin(ang).astype(np.float32).T),
            np.ascontiguousarray(np.cos(ang).astype(np.float32).T))


def l0_key_windows(cfg, H, R):
    out = []
    for kc in range((H + R) // 128):
        c0 = kc * 128
        qs = max(0, c0 - H)
        qe = min(R, (c0 + 127) - H + cfg.W + 1)
        if qs < qe:
            out.append((kc, qs, qe))
    return out


def l0_mask_tiles(cfg, H, R):
    tiles, uniq = {}, {}
    for kc, qs, qe in l0_key_windows(cfg, H, R):
        c0 = kc * 128
        for qb in range(qs - qs % 128, qe, 128):
            j = np.arange(128)[:, None]
            q = qb + np.arange(128)[None, :]
            valid = (q >= (c0 + j) - H) & (q <= (c0 + j) - H + cfg.W)
            inplay = (q >= max(qb, qs)) & (q < qe)
            m = np.where(valid | ~inplay, np.float32(0.0), np.float32(NEG))
            if not (m == 0).all():
                key = m.tobytes()
                if key not in uniq:
                    uniq[key] = f"msk{len(uniq)}"
                tiles[(kc, qb)] = uniq[key]
    consts = {}
    for key, name in uniq.items():
        consts[name] = np.frombuffer(key, dtype=np.float32).reshape(128, 128).copy()
    return tiles, consts



def make_ln_scratch(cfg, nc, ctx, name):
    DC = cfg.DC
    CH = 128
    return dict(
        CH=CH, sid=name,
        xf=ctx.enter_context(nc.sbuf_tensor(f"{name}xf", [128, 2 * DC * CH], F32)),
        sq=ctx.enter_context(nc.sbuf_tensor(f"{name}sq", [128, 2 * DC * CH], BF16)),
        st=ctx.enter_context(nc.sbuf_tensor(f"{name}st", [128, 4 * CH], F32)),
        df=ctx.enter_context(nc.sbuf_tensor(f"{name}df", [128, CH], F32)),
        ps=ctx.enter_context(nc.psum_tensor(f"{name}ps", [128, 1024], F32)),
    )


def emit_ln(cfg, nc, scr, S, src_dram, col0, ncols, xn_sb, xn_stride, dst0,
            pfx, src_deps=()):
    DC, D = cfg.DC, cfg.D
    CH = scr["CH"]
    xf, sq, st, df, ps = (scr[k] for k in ("xf", "sq", "st", "df", "ps"))
    sid = scr["sid"]
    ones_bf = S.consts["ones_bf"]
    src_r = src_dram.rearrange("(dc p) t -> p dc t", p=128)

    chunks = col_chunks(ncols, CH, start=col0)

    def ln_stage(i):
        t0, tw = chunks[i]
        b = i % 2
        xfb = xf[:, b * DC * CH: b * DC * CH + DC * tw]
        S.add("sp", lambda sp, xfb=xfb, t0=t0, tw=tw: sp.dma_start(
            out=xfb.rearrange("p (dc t) -> p dc t", dc=DC),
            in_=src_r[:, :, t0:t0 + tw]),
            reads=tuple(src_deps), writes=(f"{sid}xf{b}",), dma=True,
            tag=f"{pfx}ld{i}")
        for dc in range(DC):
            S.add("a", lambda a, dc=dc, b=b, xfb=xfb, tw=tw: a.activation(
                sq[:, b * DC * CH + dc * CH: b * DC * CH + dc * CH + tw],
                xfb[:, dc * tw:(dc + 1) * tw], AF.Square),
                reads=(f"{sid}xf{b}",), writes=(f"{sid}sq{b}.{dc}",),
                tag=f"{pfx}sq{i}.{dc}")

    ln_stage(0)
    for i, (t0, tw) in enumerate(chunks):
        b = i % 2
        xfb = xf[:, b * DC * CH: b * DC * CH + DC * tw]
        if i + 1 < len(chunks):
            ln_stage(i + 1)
        ones_f32 = S.consts["ones_f32"]
        for dc in range(DC):
            S.add("t", lambda t, dc=dc, b=b, xfb=xfb, tw=tw: t.matmul(
                ps[:, 0:tw], ones_f32[:, 0:128],
                xfb[:, dc * tw:(dc + 1) * tw],
                start=(dc == 0), stop=(dc == DC - 1)),
                reads=("@ones_f32", f"{sid}xf{b}"), writes=(f"{sid}psm",), tag=f"{pfx}mm{i}.{dc}")
        for dc in range(DC):
            S.add("t", lambda t, dc=dc, b=b, tw=tw: t.matmul(
                ps[:, 512:512 + tw], ones_bf[:, 0:128],
                sq[:, b * DC * CH + dc * CH: b * DC * CH + dc * CH + tw],
                start=(dc == 0), stop=(dc == DC - 1)),
                reads=("@ones_bf", f"{sid}sq{b}.{dc}"), writes=(f"{sid}pss",), tag=f"{pfx}ms{i}.{dc}")
        mean = st[:, 0:tw]
        m2 = st[:, CH:CH + tw]
        var = st[:, 2 * CH:2 * CH + tw]
        rstd = st[:, 3 * CH:3 * CH + tw]
        S.add("v", lambda v, mean=mean, tw=tw: v.tensor_scalar(
            mean, ps[:, 0:tw], 1.0 / D, None, ALU.mult),
            reads=(f"{sid}psm",), writes=(f"{sid}mean",), tag=f"{pfx}mean{i}")
        S.add("v", lambda v, mean=mean, m2=m2: v.tensor_tensor(m2, mean, mean, ALU.mult),
            reads=(f"{sid}mean",), writes=(f"{sid}m2",), tag=f"{pfx}m2{i}")
        S.add("v", lambda v, var=var, m2=m2, tw=tw: v.scalar_tensor_tensor(
            var, ps[:, 512:512 + tw], 1.0 / D, m2, ALU.mult, ALU.subtract),
            reads=(f"{sid}pss", f"{sid}m2"), writes=(f"{sid}var",), tag=f"{pfx}var{i}")
        S.add("a", lambda a, var=var: a.activation(var, var, AF.Ln,
                                                   bias=S.consts["epsln"][:, 0:1]),
            reads=(f"{sid}var", "@epsln"), writes=(f"{sid}var",), tag=f"{pfx}lg{i}")
        S.add("a", lambda a, var=var, rstd=rstd: a.activation(rstd, var, AF.Exp, scale=-0.5),
            reads=(f"{sid}var",), writes=(f"{sid}rstd",), tag=f"{pfx}rs{i}")
        for dc in range(DC):
            S.add("v", lambda v, dc=dc, xfb=xfb, mean=mean, tw=tw: v.tensor_tensor(
                df[:, 0:tw], xfb[:, dc * tw:(dc + 1) * tw], mean, ALU.subtract),
                reads=(f"{sid}xf{b}", f"{sid}mean"), writes=(f"{sid}df",),
                tag=f"{pfx}sub{i}.{dc}")
            d0 = dc * xn_stride + dst0 + (t0 - col0)
            S.add("v", lambda v, d0=d0, rstd=rstd, tw=tw: v.tensor_tensor(
                xn_sb[:, d0:d0 + tw], df[:, 0:tw], rstd, ALU.mult),
                reads=(f"{sid}df", f"{sid}rstd"),
                writes=(f"{pfx}xn", f"{pfx}xn.{dc}.{t0}"), tag=f"{pfx}mul{i}.{dc}")


def emit_proj(cfg, nc, ps, psname, S, w_sb, wbuf, woc, xn_sb, xn_stride,
              xn_col0, xn_buf, ncols, oc_list, out_cb, pfx, kdc=None,
              extra_mm=None):
    DC = kdc if kdc is not None else cfg.DC
    ncc = len(col_chunks(ncols, 512))
    for oi, oc in enumerate(oc_list):
        for ci, (t0, tw) in enumerate(col_chunks(ncols, 512)):
            b = (oi * ncc + ci) % 2
            ps_ap = ps[:, b * 512: b * 512 + tw]
            psbuf = f"{psname}{b}"
            for dc in range(DC):
                S.add("t", lambda t, dc=dc, oi=oi, t0=t0, tw=tw, ps_ap=ps_ap: t.matmul(
                    ps_ap,
                    w_sb[:, (dc * woc + oi) * 128: (dc * woc + oi) * 128 + 128],
                    xn_sb[:, dc * xn_stride + xn_col0 + t0:
                          dc * xn_stride + xn_col0 + t0 + tw],
                    start=(dc == 0), stop=(dc == DC - 1 and extra_mm is None)),
                    reads=(wbuf, xn_buf), writes=(psbuf,), tag=f"{pfx}mm{oc}.{t0}.{dc}")
            if extra_mm is not None:
                extra_mm(S, oi, oc, t0, tw, ps_ap, psbuf)
            out_cb(S, oi, oc, t0, tw, ps_ap, psbuf)


def emit_rope(cfg, nc, S, buf_sb, bufname, col0, ncols, sc_col0, pfx):
    rot = S.consts["rope_rot"]
    t1 = S.consts["rope_tmp"]
    t2 = S.consts["rope_tmp2"]
    sin, cos = S.consts["sinT"], S.consts["cosT"]
    for t0, tw in col_chunks(ncols, 512):
        c = col0 + t0
        s = sc_col0 + t0
        S.add("sp", lambda sp, c=c, tw=tw: sp.dma_start(
            out=rot[0:64, 0:tw], in_=buf_sb[64:128, c:c + tw]),
            reads=(bufname,), writes=("rope_rot_l",), dma=True, dmagrp="ropedma",
            tag=f"{pfx}d1.{t0}")
        S.add("sp", lambda sp, c=c, tw=tw: sp.dma_start(
            out=rot[64:128, 0:tw], in_=buf_sb[0:64, c:c + tw]),
            reads=(bufname,), writes=("rope_rot_h",), dma=True, dmagrp="ropedma",
            tag=f"{pfx}d2.{t0}")
        S.add("g", lambda g, c=c, s=s, tw=tw: g.tensor_tensor(
            t1[:, 0:tw], buf_sb[:, c:c + tw], cos[:, s:s + tw], ALU.mult),
            reads=(bufname, "sincos"), writes=("rope_t1",), tag=f"{pfx}r1.{t0}")
        S.add("g", lambda g, s=s, tw=tw: g.tensor_tensor(
            t2[:, 0:tw], rot[:, 0:tw], sin[:, s:s + tw], ALU.mult),
            reads=("rope_rot_l", "rope_rot_h", "sincos"), writes=("rope_t2",),
            tag=f"{pfx}r2.{t0}")
        S.add("g", lambda g, c=c, tw=tw: g.tensor_tensor(
            buf_sb[:, c:c + tw], t1[:, 0:tw], t2[:, 0:tw], ALU.add),
            reads=("rope_t1", "rope_t2"), writes=(bufname,), tag=f"{pfx}r3.{t0}")



def emit_attn(cfg, nc, ctx, S, pfx, *, heads, kv_of, Tkeys, R, kwin, masks,
              scale, prep_head, kt_ap, q_ap, out_cb, v_tile=None, v_dmaj=None,
              key_bias=None):
    ident_bf = S.consts["ident_bf"]
    ones_bf = S.consts["ones_bf"]

    ntiles = R // 128
    tlo, thi = [], []
    for ti in range(ntiles):
        qt0 = ti * 128
        lo = min(kc * 128 for kc, qs, qe in kwin if qs < qt0 + 128 and qe > qt0)
        hi = max(kc * 128 + 128 for kc, qs, qe in kwin if qs < qt0 + 128 and qe > qt0)
        tlo.append(lo)
        thi.append(hi)
    PMAX = max(h - l for l, h in zip(tlo, thi))
    NPMAX = (PMAX + 511) // 512

    sa_ps = ctx.enter_context(nc.psum_tensor(f"{pfx}sa", [128, 1024], F32))
    tp_ps = [ctx.enter_context(nc.psum_tensor(f"{pfx}tp{i}", [128, 512], BF16))
             for i in range(2)]
    vt_ps = [ctx.enter_context(nc.psum_tensor(f"{pfx}vt{i}", [128, 512], BF16))
             for i in range(2)]
    den_ps = ctx.enter_context(nc.psum_tensor(f"{pfx}den", [128, 128], F32))
    out_ps = ctx.enter_context(nc.psum_tensor(f"{pfx}out", [128, 128], F32))

    pp = ctx.enter_context(nc.sbuf_tensor(f"{pfx}pp", [128, 2 * PMAX], BF16))
    ptsb = ctx.enter_context(nc.sbuf_tensor(f"{pfx}pts", [128, 2 * 512], BF16))
    mg = ctx.enter_context(nc.sbuf_tensor(f"{pfx}mg", [128, 2 * NPMAX], F32))
    mps = ctx.enter_context(nc.sbuf_tensor(f"{pfx}mps", [128, 2 * NPMAX], F32))
    alp = ctx.enter_context(nc.sbuf_tensor(f"{pfx}alp", [128, 2 * NPMAX], F32))
    mts = ctx.enter_context(nc.sbuf_tensor(f"{pfx}mts", [128, 2], F32))
    den_sb = ctx.enter_context(nc.sbuf_tensor(f"{pfx}densb", [128, 2 * 128], F32))
    rec_sb = ctx.enter_context(nc.sbuf_tensor(f"{pfx}rec", [128, 2 * 128], F32))
    if v_dmaj is not None:
        vtt = ctx.enter_context(nc.sbuf_tensor(f"{pfx}vtt", [128, 2 * Tkeys], BF16))
        vkcs = sorted({kc for kc, _, _ in kwin})

        def v_tile(h, kc):
            hs = h % 2
            return vtt[:, hs * Tkeys + kc * 128: hs * Tkeys + kc * 128 + 128],                 f"{pfx}vtt{hs}"

    sa_bank = [0]
    pgrp = [0]
    vgrp = [0]

    def vtr_groups(h):
        hs = h % 2
        vap, vbuf = v_dmaj(h)
        vgroups = [vkcs[i:i + 4] for i in range(0, len(vkcs), 4)]
        for gi, g4 in enumerate(vgroups):
            sl = vgrp[0] % 2
            vgrp[0] += 1
            for i, kc in enumerate(g4):
                S.add("t", lambda t, sl=sl, i=i, kc=kc, vap=vap: t.transpose(
                    vt_ps[sl][:, i * 128: i * 128 + 128],
                    vap[:, kc * 128: kc * 128 + 128], ident_bf[:, :]),
                    reads=(vbuf, "@ident_bf"), writes=(f"{pfx}vt{sl}",),
                    tag=f"{pfx}VT{h}.{kc}")
            gn = len(g4)
            c0 = g4[0] * 128
            if gi % 2:
                S.add("v", lambda v, sl=sl, gn=gn, c0=c0, hs=hs: v.tensor_copy(
                    vtt[:, hs * Tkeys + c0: hs * Tkeys + c0 + gn * 128],
                    vt_ps[sl][:, 0: gn * 128]),
                    reads=(f"{pfx}vt{sl}",), writes=(f"{pfx}vtt{hs}",),
                    tag=f"{pfx}VE{h}.{gi}")
            else:
                S.add("a", lambda a, sl=sl, gn=gn, c0=c0, hs=hs: a.activation(
                    vtt[:, hs * Tkeys + c0: hs * Tkeys + c0 + gn * 128],
                    vt_ps[sl][:, 0: gn * 128], AF.Copy),
                    reads=(f"{pfx}vt{sl}",), writes=(f"{pfx}vtt{hs}",),
                    tag=f"{pfx}VE{h}.{gi}")
            yield

    def genA(j):
        h, ti = divmod(j, ntiles)
        hs = j % 2
        ktap, ktbuf = kt_ap(h)
        qap, qbuf = q_ap(h)
        qt0 = ti * 128
        lo = tlo[ti]
        pieces = col_chunks(thi[ti] - lo, 512, start=lo)
        np_ = len(pieces)
        vg = vtr_groups(h) if (v_dmaj is not None and ti == 0) else None
        for pi, (p0, pw) in enumerate(pieces):
            b = sa_bank[0]
            sa_bank[0] = (b + 1) % 2
            sap = sa_ps[:, b * 512: b * 512 + pw]
            sbf = f"{pfx}sa{b}"
            S.add("t", lambda t, sap=sap, p0=p0, pw=pw, ktap=ktap, qap=qap,
                  qt0=qt0, stp=(key_bias is None): t.matmul(
                  sap, qap[:, qt0:qt0 + 128], ktap[:, p0:p0 + pw],
                  start=True, stop=stp),
                  reads=(qbuf, ktbuf), writes=(sbf,),
                  tag=f"{pfx}A{h}.{ti}.{p0}")
            if key_bias is not None:
                kb_ap, kb_buf = key_bias(p0, pw)
                S.add("t", lambda t, sap=sap, kb_ap=kb_ap, pw=pw: t.matmul(
                    sap, ones_bf[0:1, 0:128], kb_ap, start=False, stop=True),
                    reads=("@ones_bf", kb_buf), writes=(sbf,),
                    tag=f"{pfx}Ah{h}.{ti}.{p0}")
            for kc in range(p0 // 128, (p0 + pw) // 128):
                mt = masks.get((kc, qt0))
                if mt is not None:
                    S.add("v", lambda v, b=b, kc=kc, p0=p0, mt=mt: v.tensor_tensor(
                        sa_ps[:, b * 512 + kc * 128 - p0:
                              b * 512 + kc * 128 - p0 + 128],
                        sa_ps[:, b * 512 + kc * 128 - p0:
                              b * 512 + kc * 128 - p0 + 128],
                        S.consts[mt + "T"][:, :], ALU.add),
                        reads=(sbf, "@" + mt + "T"), writes=(sbf,),
                        tag=f"{pfx}Am{h}.{ti}.{kc}")
            S.add("v", lambda v, sap=sap, pi=pi, hs=hs: v.reduce_max(
                mg[:, hs * NPMAX + pi: hs * NPMAX + pi + 1], sap, axis=AX.X),
                reads=(sbf,), writes=(f"{pfx}mg{hs}",),
                tag=f"{pfx}Ar{h}.{ti}.{pi}")
            S.add("v", lambda v, pi=pi, hs=hs: v.tensor_scalar(
                mps[:, hs * NPMAX + pi: hs * NPMAX + pi + 1],
                mg[:, hs * NPMAX + pi: hs * NPMAX + pi + 1], -float(scale),
                None, ALU.mult),
                reads=(f"{pfx}mg{hs}",), writes=(f"{pfx}mps{hs}",),
                tag=f"{pfx}As{h}.{ti}.{pi}")
            S.add("a", lambda a, sap=sap, p0=p0, pw=pw, lo=lo, pi=pi, hs=hs:
                  a.activation(
                pp[:, hs * PMAX + p0 - lo: hs * PMAX + p0 - lo + pw], sap,
                AF.Exp, scale=float(scale),
                bias=mps[:, hs * NPMAX + pi: hs * NPMAX + pi + 1]),
                reads=(sbf, f"{pfx}mps{hs}"), writes=(f"{pfx}pp{hs}.{pi}",),
                tag=f"{pfx}Ae{h}.{ti}.{pi}")
            if vg is not None:
                next(vg, None)
            yield
        if vg is not None:
            for _ in vg:
                yield
        S.add("v", lambda v, np_=np_, hs=hs: v.reduce_max(
            mts[:, hs:hs + 1], mg[:, hs * NPMAX: hs * NPMAX + np_], axis=AX.X),
            reads=(f"{pfx}mg{hs}",), writes=(f"{pfx}mtp{hs}",),
            tag=f"{pfx}At{h}.{ti}")
        S.add("v", lambda v, hs=hs: v.tensor_scalar(
            mts[:, hs:hs + 1], mts[:, hs:hs + 1], -float(scale), None, ALU.mult),
            reads=(f"{pfx}mtp{hs}",), writes=(f"{pfx}mts{hs}",),
            tag=f"{pfx}Ats{h}.{ti}")
        S.add("a", lambda a, np_=np_, hs=hs: a.activation(
            alp[:, hs * NPMAX: hs * NPMAX + np_],
            mg[:, hs * NPMAX: hs * NPMAX + np_], AF.Exp, scale=float(scale),
            bias=mts[:, hs:hs + 1]),
            reads=(f"{pfx}mg{hs}", f"{pfx}mts{hs}"), writes=(f"{pfx}alp{hs}",),
            tag=f"{pfx}Aa{h}.{ti}")
        if np_ > 1:
            for pi, (p0, pw) in enumerate(pieces):
                S.add("v", lambda v, p0=p0, pw=pw, lo=lo, pi=pi, hs=hs:
                      v.tensor_scalar(
                    pp[:, hs * PMAX + p0 - lo: hs * PMAX + p0 - lo + pw],
                    pp[:, hs * PMAX + p0 - lo: hs * PMAX + p0 - lo + pw],
                    alp[:, hs * NPMAX + pi: hs * NPMAX + pi + 1], None,
                    ALU.mult),
                    reads=(f"{pfx}pp{hs}.{pi}", f"{pfx}alp{hs}"),
                    writes=(f"{pfx}pp{hs}.{pi}",),
                    tag=f"{pfx}Ax{h}.{ti}.{pi}")
        yield

    def genC(j):
        h, ti = divmod(j, ntiles)
        hs = j % 2
        qt0 = ti * 128
        lo = tlo[ti]
        kcs = list(range(lo // 128, thi[ti] // 128))
        groups = [kcs[i:i + 4] for i in range(0, len(kcs), 4)]

        def tr_group(gi):
            sl = pgrp[0] % 2
            pgrp[0] += 1
            for i, kc in enumerate(groups[gi]):
                S.add("t", lambda t, sl=sl, i=i, kc=kc, hs=hs, lo=lo: t.transpose(
                    tp_ps[sl][:, i * 128: i * 128 + 128],
                    pp[:, hs * PMAX + kc * 128 - lo:
                       hs * PMAX + kc * 128 - lo + 128], ident_bf[:, :]),
                    reads=(f"{pfx}pp{hs}.{gi}", "@ident_bf"),
                    writes=(f"{pfx}tp{sl}",),
                    tag=f"{pfx}T{h}.{ti}.{kc}")
            gn = len(groups[gi])
            if gi % 2:
                S.add("v", lambda v, sl=sl, gn=gn: v.tensor_copy(
                    ptsb[:, sl * 512: sl * 512 + gn * 128],
                    tp_ps[sl][:, 0: gn * 128]),
                    reads=(f"{pfx}tp{sl}",), writes=(f"{pfx}pts{sl}",),
                    tag=f"{pfx}Te{h}.{ti}.{gi}")
            else:
                S.add("a", lambda a, sl=sl, gn=gn: a.activation(
                    ptsb[:, sl * 512: sl * 512 + gn * 128],
                    tp_ps[sl][:, 0: gn * 128], AF.Copy),
                    reads=(f"{pfx}tp{sl}",), writes=(f"{pfx}pts{sl}",),
                    tag=f"{pfx}Te{h}.{ti}.{gi}")
            return sl

        sls = {0: tr_group(0)}
        nk = len(kcs)
        ki = 0
        for gi in range(len(groups)):
            if gi + 1 < len(groups):
                sls[gi + 1] = tr_group(gi + 1)
                yield
            sl = sls[gi]
            for i, kc in enumerate(groups[gi]):
                vap, vbuf = v_tile(h, kc)
                first = ki == 0
                last = ki == nk - 1
                S.add("t", lambda t, sl=sl, i=i, first=first, last=last: t.matmul(
                    den_ps[:, 0:128], ones_bf[:, 0:128],
                    ptsb[:, sl * 512 + i * 128: sl * 512 + i * 128 + 128],
                    start=first, stop=last),
                    reads=("@ones_bf", f"{pfx}pts{sl}"), writes=(f"{pfx}denps",),
                    tag=f"{pfx}Bd{h}.{ti}.{kc}")
                S.add("t", lambda t, sl=sl, i=i, vap=vap, vbuf=vbuf, first=first,
                      last=last: t.matmul(
                    out_ps[:, 0:128], vap,
                    ptsb[:, sl * 512 + i * 128: sl * 512 + i * 128 + 128],
                    start=first, stop=last),
                    reads=(vbuf, f"{pfx}pts{sl}"), writes=(f"{pfx}outps",),
                    tag=f"{pfx}Bv{h}.{ti}.{kc}")
                ki += 1
            yield
        S.add("v", lambda v, hs=hs: v.tensor_copy(
            den_sb[:, hs * 128: hs * 128 + 128], den_ps[:, 0:128]),
            reads=(f"{pfx}denps",), writes=(f"{pfx}densb{hs}",),
            tag=f"{pfx}F1{h}.{ti}")
        S.add("a", lambda a, hs=hs: a.activation(
            den_sb[:, hs * 128: hs * 128 + 128],
            den_sb[:, hs * 128: hs * 128 + 128], AF.Ln),
            reads=(f"{pfx}densb{hs}",), writes=(f"{pfx}densb{hs}",),
            tag=f"{pfx}F2{h}.{ti}")
        S.add("a", lambda a, hs=hs: a.activation(
            rec_sb[:, hs * 128: hs * 128 + 128],
            den_sb[:, hs * 128: hs * 128 + 128], AF.Exp, scale=-1.0),
            reads=(f"{pfx}densb{hs}",), writes=(f"{pfx}recsb{hs}",),
            tag=f"{pfx}F3{h}.{ti}")
        out_cb(S, h, qt0, 128, out_ps[:, 0:128], f"{pfx}outps",
               rec_sb[:, hs * 128: hs * 128 + 128], f"{pfx}recsb{hs}")
        yield

    ntot = heads * ntiles
    prep_ti = min(2, ntiles - 1)
    prep_head(S, 0)
    ga = genA(0)
    for _ in ga:
        pass
    for j in range(ntot):
        hj, tij = divmod(j, ntiles)
        if tij == prep_ti and hj + 1 < heads:
            prep_head(S, hj + 1)
        gc = genC(j)
        ga = genA(j + 1) if j + 1 < ntot else None
        done_a = ga is None
        done_c = False
        while not (done_a and done_c):
            if not done_a:
                try:
                    next(ga)
                except StopIteration:
                    done_a = True
            if not done_c:
                try:
                    next(gc)
                except StopIteration:
                    done_c = True


def emit_quant(S, eng, dst_ap, dstbuf, src_ap, srcbuf, qa_ap, qabuf, qb_ap,
               qbbuf, tau_ap, ntau_ap, tag):
    S.add(eng, lambda e: e.tensor_scalar(qa_ap, src_ap, tau_ap, None, ALU.is_gt),
          reads=(srcbuf, "@tau"), writes=(qabuf,), tag=f"{tag}a")
    S.add(eng, lambda e: e.tensor_scalar(qb_ap, src_ap, ntau_ap, None, ALU.is_lt),
          reads=(srcbuf, "@tau"), writes=(qbbuf,), tag=f"{tag}b")
    S.add(eng, lambda e: e.tensor_tensor(dst_ap, qa_ap, qb_ap, ALU.subtract),
          reads=(qabuf, qbbuf), writes=(dstbuf,), tag=f"{tag}s")


def setup_consts(nc, ctx, S, specs):
    for name, (shape, dt) in specs.items():
        dr = nc.dram_tensor(f"c_{name}", list(shape), dt, kind="ExternalInput").ap()
        sb = ctx.enter_context(nc.sbuf_tensor(f"cs_{name}", list(shape), dt))
        S.consts[name] = sb
        S.add("sp", lambda sp, sb=sb, dr=dr: sp.dma_start(out=sb[:, :], in_=dr),
              writes=("@" + name,), dma=True, dmagrp="consts",
              tag=f"const_{name}")


def emit_sproj(cfg, nc, ctx, S, W_dram, wname, wsb, xparts, oc0, noc, out_cb,
               pfx, ps, psname, blk=2, kdc=None, extra_mm=None, cast=True):
    DCk = kdc if kdc is not None else cfg.DC
    w_r = W_dram.rearrange("(dc p) o -> p dc o", p=128)
    eng = "g" if cast else "sp"
    for bi, (o0, ow) in enumerate(col_chunks(noc * 128, blk * 128, start=oc0 * 128)):
        sl = bi % 2
        wap = wsb[:, sl * DCk * blk * 128: sl * DCk * blk * 128 + DCk * ow]
        wnm = wsb.name if hasattr(wsb, "name") else wsb.tensor.name
        if wnm.startswith(S.ns) and S.ns:
            wnm = wnm[len(S.ns):]
        wbuf = f"W&{wnm}^{sl}"
        S.add(eng, lambda e, wap=wap, o0=o0, ow=ow: e.dma_start(
            out=wap.rearrange("p (dc o) -> p dc o", dc=DCk),
            in_=w_r[:, :, o0:o0 + ow]),
            writes=(wbuf,), dma=True, tag=f"{pfx}wld{bi}")
        for xn_sb, stride, col0, xbuf, ncols, dst0 in xparts:
            emit_proj(cfg, nc, ps, psname, S, wap, wbuf, ow // 128,
                      xn_sb, stride, col0, xbuf, ncols,
                      list(range(ow // 128)),
                      (lambda S_, oi, oc_, t0, tw, ps_ap, psbuf,
                       o0=o0, dst0=dst0:
                       out_cb(S_, o0 // 128 + oi, dst0 + t0, tw, ps_ap, psbuf)),
                      f"{pfx}b{bi}", kdc=DCk,
                      extra_mm=(None if extra_mm is None else
                                (lambda S_, oi, oc_, t0, tw, ps_ap, psbuf,
                                 o0=o0, dst0=dst0:
                                 extra_mm(S_, o0 // 128 + oi, dst0 + t0, tw,
                                          ps_ap, psbuf))))



def emit_l0_gamma(cfg, nc, ctx, S, io):
    D, DC, F0, FC0 = cfg.D, cfg.DC, cfg.F0, cfg.FC0
    W1T, W2T, W3T = io["W1T"], io["W2T"], io["W3T"]
    tau = io["tau"]
    with ExitStack() as ctxG:
        gt = ctxG.enter_context(nc.sbuf_tensor("gt", [128, 2 * F0], F32))
        gacc = ctxG.enter_context(nc.sbuf_tensor("gacc", [128, 4], F32))
        gtmp = ctxG.enter_context(nc.sbuf_tensor("gtmp", [128, 1], F32))
        ps_g = ctxG.enter_context(nc.psum_tensor("ps_g", [128, 512], F32))
        S.add("v", lambda v: v.memset(gacc[:, :], 0.0), writes=("gacc",), tag="gz")
        for m, (Wd, rows, cols) in enumerate(
                [(W1T, DC, F0), (W2T, DC, F0), (W3T, FC0, D)]):
            for r in range(rows):
                sl = r % 2
                gap = gt[:, sl * F0: sl * F0 + cols]
                S.add("sp", lambda sp, gap=gap, Wd=Wd, r=r, cols=cols:
                      sp.dma_start(out=gap, in_=Wd[r * 128:(r + 1) * 128, :]),
                      writes=(f"gt{sl}",), dma=True, tag=f"gld{m}.{r}")
                S.add("v", lambda v, gap=gap: v.reduce_sum(
                    gtmp[:, :], gap, axis=AX.X, apply_absolute_value=True),
                    reads=(f"gt{sl}",), writes=("gtmp",), tag=f"gr{m}.{r}")
                S.add("v", lambda v, m=m: v.tensor_tensor(
                    gacc[:, m:m + 1], gacc[:, m:m + 1], gtmp[:, :], ALU.add),
                    reads=("gtmp", "gacc"), writes=("gacc",), tag=f"ga{m}.{r}")
        S.add("t", lambda t: t.matmul(ps_g[:, 0:4], S.consts["ones_f32"][:, :],
                                      gacc[:, :], start=True, stop=True),
              reads=("gacc", "@ones_f32"), writes=("psg",), tag="gmm")
        S.add("v", lambda v: v.tensor_copy(gacc[:, :], ps_g[:, 0:4]),
              reads=("psg",), writes=("gacc",), tag="gev")
        Ne = float(D * F0)
        for m in range(3):
            S.add("a", lambda a, m=m: a.activation(
                tau[:, m:m + 1], gacc[:, m:m + 1], AF.Copy,
                bias=0.5 * EPS_Q, scale=0.5 / Ne),
                reads=("gacc",), writes=("@tau",), tag=f"gt{m}")
            S.add("a", lambda a, m=m: a.activation(
                tau[:, 4 + m:5 + m], gacc[:, m:m + 1], AF.Copy,
                bias=-0.5 * EPS_Q, scale=-0.5 / Ne),
                reads=("gacc",), writes=("@tau",), tag=f"gnt{m}")
    S.barrier("b_gam")


def emit_l0_pass(cfg, nc, S, io, kwin, masks):
    R, D, DC = cfg.R, cfg.D, cfg.DC
    H = cfg.W
    T = H + R
    HKV, HQ, GQ, F0, FC0 = cfg.HKV, cfg.HQ, cfg.GQ, cfg.F0, cfg.FC0
    FB = 256
    XT, SINT, COST, HIND = io["XT"], io["SINT"], io["COST"], io["HIND"]
    WQ, WK, WV, WO = io["WQ"], io["WK"], io["WV"], io["WO"]
    W1T, W2T, W3T = io["W1T"], io["W2T"], io["W3T"]
    PS1, X1dst, tau = io["PS1"], io["X1dst"], io["tau"]
    x1keyp = io["x1keyp"]
    XT_r = XT.rearrange("(dc p) t -> p dc t", p=128)
    PS1_r = PS1.rearrange("(dc p) t -> p dc t", p=128)
    X1_r = X1dst.rearrange("(dc p) t -> p dc t", p=128)

    with ExitStack() as ctx:
        xno = ctx.enter_context(nc.sbuf_tensor(S.ns + "xno", [128, DC * R], BF16))
        hind = ctx.enter_context(nc.sbuf_tensor(S.ns + "hind", [1, T], BF16))
        S.add("g", lambda g: g.dma_start(out=hind[:, :], in_=HIND),
              writes=("hind",), dma=True, dmagrp="passld", tag="ldhind")

        ctxAP = ctx.enter_context(ExitStack())
        attnT = ctxAP.enter_context(nc.sbuf_tensor(S.ns + "attnT", [128, HQ * R], BF16))
        with ExitStack() as ctxA:
            QT = ctxA.enter_context(nc.sbuf_tensor(S.ns + "QT", [128, HQ * R], BF16))
            KT = ctxA.enter_context(nc.sbuf_tensor(S.ns + "KT", [128, HKV * T], BF16))
            Vtok = ctxA.enter_context(nc.sbuf_tensor(S.ns + "Vtok", [128, HKV * T], BF16))
            with ExitStack() as ctx1:
                xnh = ctx1.enter_context(nc.sbuf_tensor(S.ns + "xnh", [128, DC * H], BF16))
                with ExitStack() as ctxLN:
                    scr = make_ln_scratch(cfg, nc, ctxLN, S.ns + "lns")
                    emit_ln(cfg, nc, scr, S, XT, 0, H, xnh, H, 0, "lnh")
                    emit_ln(cfg, nc, scr, S, XT, H, R, xno, R, 0, "lno")
                S.barrier("b_ln")
                xp_kv = [(xnh, H, 0, "lnhxn", H, 0), (xno, R, 0, "lnoxn", R, H)]
                xp_q = [(xno, R, 0, "lnoxn", R, 0)]
                with ExitStack() as ctxW:
                    sinT = ctxW.enter_context(nc.sbuf_tensor(S.ns + "sinT", [128, T], F32))
                    cosT = ctxW.enter_context(nc.sbuf_tensor(S.ns + "cosT", [128, T], F32))
                    S.consts["sinT"] = sinT
                    S.consts["cosT"] = cosT
                    S.add("sp", lambda sp: sp.dma_start(out=sinT[:, :], in_=SINT),
                          writes=("sincos_s",), dma=True, dmagrp="passld",
                          tag="ldsin")
                    S.add("sp", lambda sp: sp.dma_start(out=cosT[:, :], in_=COST),
                          writes=("sincos",), dma=True, dmagrp="passld",
                          tag="ldcos")
                    S.add("sp", lambda sp: None, reads=("sincos_s",),
                          writes=("sincos",), tag="sincos_join")
                    S.consts["rope_tmp"] = ctxW.enter_context(
                        nc.sbuf_tensor(S.ns + "ropet1", [128, 512], F32))
                    S.consts["rope_tmp2"] = ctxW.enter_context(
                        nc.sbuf_tensor(S.ns + "ropet2", [128, 512], F32))
                    S.consts["rope_rot"] = ctxW.enter_context(
                        nc.sbuf_tensor(S.ns + "roperot", [128, 512], BF16))
                    wsb = ctxW.enter_context(
                        nc.sbuf_tensor(S.ns + "wsb", [128, 2 * DC * 1 * 128], BF16))
                    vtmp = ctxW.enter_context(nc.sbuf_tensor(S.ns + "vtmp", [128, 512], BF16))
                    ps_p = ctxW.enter_context(nc.psum_tensor(S.ns + "ps_p", [128, 1024], F32))
                    ps_tr = ctxW.enter_context(nc.psum_tensor(S.ns + "ps_tr", [128, 128], BF16))

                    def q_cb(S_, oc, dcol, tw, ps_ap, psbuf):
                        S_.add("v", lambda v: v.tensor_copy(
                            QT[:, oc * R + dcol: oc * R + dcol + tw], ps_ap),
                            reads=(psbuf,), writes=("QT", f"QT.{oc}"),
                            tag=f"qe{oc}.{dcol}")

                    def k_cb(S_, oc, dcol, tw, ps_ap, psbuf):
                        S_.add("v", lambda v: v.tensor_copy(
                            KT[:, oc * T + dcol: oc * T + dcol + tw], ps_ap),
                            reads=(psbuf,), writes=("KT", f"KT.{oc}"),
                            tag=f"ke{oc}.{dcol}")

                    def v_cb(S_, oc, dcol, tw, ps_ap, psbuf):
                        S_.add("v", lambda v, tw=tw: v.tensor_copy(vtmp[:, 0:tw], ps_ap),
                            reads=(psbuf,), writes=("vtmp",), tag=f"ve{oc}.{dcol}")
                        for k in range(tw // 128):
                            S_.add("t", lambda t, k=k: t.transpose(
                                ps_tr[:, :], vtmp[:, k * 128:(k + 1) * 128],
                                S.consts["ident_bf"][:, :]),
                                reads=("vtmp", "@ident_bf"), writes=("pstr",),
                                tag=f"vt{oc}.{dcol}.{k}")
                            cc = (oc * (T // 128) + (dcol + k * 128) // 128) * 128
                            S_.add("v", lambda v, cc=cc: v.tensor_copy(
                                Vtok[:, cc:cc + 128], ps_tr[:, :]),
                                reads=("pstr",), writes=("Vtok",),
                                tag=f"vte{oc}.{dcol}.{k}")

                    emit_sproj(cfg, nc, ctxW, S, WQ, "WQ", wsb, xp_q, 0, DC,
                               q_cb, "pq", ps_p, "psp", blk=1)
                    emit_sproj(cfg, nc, ctxW, S, WK, "WK", wsb, xp_kv, 0, HKV,
                               k_cb, "pk", ps_p, "psp", blk=1)
                    emit_sproj(cfg, nc, ctxW, S, WV, "WV", wsb, xp_kv, 0, HKV,
                               v_cb, "pv", ps_p, "psp", blk=1)
                    for h in range(HQ):
                        emit_rope(cfg, nc, S, QT, "QT", h * R, R, H, f"rq{h}.")
                    for kv in range(HKV):
                        emit_rope(cfg, nc, S, KT, "KT", kv * T, T, 0, f"rk{kv}.")
                S.barrier("b_qkv")
            with ExitStack() as ctxAT:
                def out_cb(S_, h, q0, qw, out_ps_ap, opsbuf, rec_ap, recbuf):
                    S_.add("v", lambda v: v.tensor_tensor(
                        attnT[:, h * R + q0: h * R + q0 + qw], out_ps_ap, rec_ap,
                        ALU.mult),
                        reads=(opsbuf, recbuf), writes=("attnT",), tag=f"ao{h}.{q0}")

                emit_attn(cfg, nc, ctxAT, S, S.ns + "at", heads=HQ, kv_of=None,
                          Tkeys=T, R=R, kwin=kwin, masks=masks, scale=L0_SCALE,
                          prep_head=lambda S_, h: None,
                          kt_ap=lambda h: (KT[:, (h // GQ) * T:(h // GQ) * T + T], "KT"),
                          v_tile=lambda h, kc: (
                              Vtok[:, ((h // GQ) * (T // 128) + kc) * 128:
                                   ((h // GQ) * (T // 128) + kc) * 128 + 128], "Vtok"),
                          q_ap=lambda h: (QT[:, h * R:h * R + R], "QT"),
                          out_cb=out_cb,
                          key_bias=lambda p0, pw: (hind[0:1, p0:p0 + pw],
                                                   "hind"))
            S.barrier("b_attn")
        with ExitStack() as ctxP:
            wsbP = ctxP.enter_context(nc.sbuf_tensor(S.ns + "wsbP", [128, 2 * DC * 2 * 128], BF16))
            xr = ctxP.enter_context(nc.sbuf_tensor(S.ns + "xr", [128, 2 * 512], F32))
            s1b = ctxP.enter_context(nc.sbuf_tensor(S.ns + "s1b", [128, 2 * 512], F32))
            ps_o = ctxP.enter_context(nc.psum_tensor(S.ns + "ps_o", [128, 1024], F32))

            def o_cb(S_, oc, dcol, tw, ps_ap, psbuf):
                sl = (oc * (R // 512) + dcol // 512) % 2
                xap = xr[:, sl * 512: sl * 512 + tw]
                sap = s1b[:, sl * 512: sl * 512 + tw]
                S_.add("sp", lambda sp, oc=oc, dcol=dcol, tw=tw, xap=xap: sp.dma_start(
                    out=xap, in_=XT_r[:, oc, H + dcol:H + dcol + tw]),
                    writes=(f"xr{sl}",), dma=True, tag=f"oxl{oc}.{dcol}")
                S_.add("v", lambda v, xap=xap, sap=sap: v.tensor_tensor(
                    sap, ps_ap, xap, ALU.add),
                    reads=(psbuf, f"xr{sl}"), writes=(f"s1b{sl}",), tag=f"os{oc}.{dcol}")
                S_.add("sp", lambda sp, oc=oc, dcol=dcol, tw=tw, sap=sap: sp.dma_start(
                    out=PS1_r[:, oc, dcol:dcol + tw], in_=sap),
                    reads=(f"s1b{sl}",), writes=(f"ps1.{oc}.{dcol}",), dma=True,
                    dmagrp="ps1st", tag=f"ost{oc}.{dcol}")

            emit_sproj(cfg, nc, ctxP, S, WO, "WO", wsbP,
                       [(attnT, R, 0, "attnT", R, 0)], 0, DC, o_cb, "po",
                       ps_o, "pso")
            S.barrier("b_proj")
        ctxAP.close()
        with ExitStack() as ctxH:
            hid = ctxH.enter_context(nc.sbuf_tensor(S.ns + "hid", [128, FC0 * R], BF16))
            with ExitStack() as ctxX:
                wf = ctxX.enter_context(nc.sbuf_tensor(S.ns + "wf", [128, 2 * DC * FB], F32))
                wq = ctxX.enter_context(nc.sbuf_tensor(S.ns + "wq", [128, 2 * DC * FB], BF16))
                qa = ctxX.enter_context(nc.sbuf_tensor(S.ns + "qa", [128, DC * FB], BF16))
                qb = ctxX.enter_context(nc.sbuf_tensor(S.ns + "qb", [128, DC * FB], BF16))
                sg = ctxX.enter_context(nc.sbuf_tensor(S.ns + "sg", [128, (FB // 128) * R], BF16))
                ps_f = ctxX.enter_context(nc.psum_tensor(S.ns + "ps_f", [128, 1024], F32))
                nfb = F0 // FB
                items = [(fb, m, Wd) for fb in range(nfb)
                         for m, Wd in ((0, W1T), (1, W2T))]

                def ld_quant(idx):
                    fb, m, Wd = items[idx]
                    sl = idx % 2
                    wap = wf[:, sl * DC * FB: (sl + 1) * DC * FB]
                    wqap = wq[:, sl * DC * FB: (sl + 1) * DC * FB]
                    S.add("sp", lambda sp, wap=wap, Wd=Wd, fb=fb: sp.dma_start(
                        out=wap.rearrange("p (dc f) -> p dc f", dc=DC),
                        in_=Wd.rearrange("(dc p) f -> p dc f", p=128)
                        [:, :, fb * FB:(fb + 1) * FB]),
                        writes=(f"wf{sl}",), dma=True, tag=f"fw{fb}.{m}")
                    emit_quant(S, "v", wqap, f"wq{sl}", wap, f"wf{sl}",
                               qa[:, :], "qa", qb[:, :], "qb",
                               tau[:, m:m + 1], tau[:, 4 + m:5 + m],
                               f"fq{fb}.{m}")

                ld_quant(0)
                for idx, (fb, m, Wd) in enumerate(items):
                    if idx + 1 < len(items):
                        ld_quant(idx + 1)
                    sl = idx % 2
                    wqap = wq[:, sl * DC * FB: (sl + 1) * DC * FB]
                    if True:
                        if m == 0:
                            def s_cb(S_, oi, oc_, t0, tw, ps_ap, psbuf, fb=fb):
                                S_.add("a", lambda a, oi=oi, t0=t0, tw=tw, ps_ap=ps_ap:
                                       a.activation(
                                    sg[:, oi * R + t0: oi * R + t0 + tw], ps_ap,
                                    AF.Sigmoid),
                                    reads=(psbuf,), writes=("sg", f"sg.{oi}"),
                                    tag=f"fs{fb}.{oi}.{t0}")
                            cb = s_cb
                        else:
                            def h_cb(S_, oi, oc_, t0, tw, ps_ap, psbuf, fb=fb):
                                cc = (fb * (FB // 128) + oi) * R + t0
                                S_.add("v", lambda v, oi=oi, cc=cc, t0=t0, tw=tw,
                                       ps_ap=ps_ap: v.tensor_tensor(
                                    hid[:, cc:cc + tw], ps_ap,
                                    sg[:, oi * R + t0: oi * R + t0 + tw], ALU.mult),
                                    reads=(psbuf, f"sg.{oi}"), writes=("hid",),
                                    tag=f"fh{fb}.{oi}.{t0}")
                            cb = h_cb
                        emit_proj(cfg, nc, ps_f, "psf", S, wqap, f"wq{sl}",
                                  FB // 128, xno, R, 0, "lnoxn", R,
                                  list(range(FB // 128)), cb, f"fx{fb}.{m}")
            S.barrier("b_x12")
            with ExitStack() as ctxY:
                w3f = ctxY.enter_context(nc.sbuf_tensor(S.ns + "w3f", [128, FC0 * 128], F32))
                w3q = ctxY.enter_context(nc.sbuf_tensor(S.ns + "w3q", [128, 2 * FC0 * 128], BF16))
                qa3 = ctxY.enter_context(nc.sbuf_tensor(S.ns + "qa3", [128, FC0 * 64], BF16))
                qb3 = ctxY.enter_context(nc.sbuf_tensor(S.ns + "qb3", [128, FC0 * 64], BF16))
                s1r = ctxY.enter_context(nc.sbuf_tensor(S.ns + "s1r", [128, 2 * 512], F32))
                xob = ctxY.enter_context(nc.sbuf_tensor(S.ns + "xob", [128, 2 * 512], F32))
                ps_y = ctxY.enter_context(nc.psum_tensor(S.ns + "ps_y", [128, 1024], F32))
                FCH = FC0 // 2

                def ld3(oc):
                    s3 = oc % 2
                    for hf in range(2):
                        fc0 = hf * FCH
                        fcn = FC0 - fc0 if hf else FCH
                        c0 = fc0 * 128
                        S.add("sp", lambda sp, oc=oc, fc0=fc0, fcn=fcn, c0=c0:
                              sp.dma_start(
                            out=w3f[:, c0:c0 + fcn * 128].rearrange(
                                "p (fc o) -> p fc o", o=128),
                            in_=W3T.rearrange("(fc p) o -> p fc o", p=128)
                            [:, fc0:fc0 + fcn, oc * 128:(oc + 1) * 128]),
                            writes=(f"w3f{hf}",), dma=True, tag=f"yw{oc}.{hf}")
                        emit_quant(S, "v", w3q[:, s3 * FC0 * 128 + c0:
                                               s3 * FC0 * 128 + c0 + fcn * 128],
                                   f"w3q{s3}",
                                   w3f[:, c0:c0 + fcn * 128], f"w3f{hf}",
                                   qa3[:, :], "qa3", qb3[:, :], "qb3",
                                   tau[:, 2:3], tau[:, 6:7], f"yq{oc}.{hf}")

                ld3(0)
                for oc in range(DC):
                    s3 = oc % 2
                    w3qap = w3q[:, s3 * FC0 * 128:(s3 + 1) * FC0 * 128]
                    if oc + 1 < DC:
                        ld3(oc + 1)

                    def y_cb(S_, oi, oc_, t0, tw, ps_ap, psbuf, oc=oc):
                        sl = (oc * (R // 512) + t0 // 512) % 2
                        rap = s1r[:, sl * 512: sl * 512 + tw]
                        oap = xob[:, sl * 512: sl * 512 + tw]
                        S_.add("sp", lambda sp, rap=rap, oc=oc, t0=t0, tw=tw:
                               sp.dma_start(out=rap,
                                            in_=PS1_r[:, oc, t0:t0 + tw]),
                               reads=(f"ps1.{oc}.{t0}",), writes=(f"s1r{sl}",),
                               dma=True, tag=f"ysl{oc}.{t0}")
                        S_.add("v", lambda v, rap=rap, oap=oap, ps_ap=ps_ap:
                               v.tensor_tensor(oap, ps_ap, rap, ALU.add),
                               reads=(psbuf, f"s1r{sl}"), writes=(f"xob{sl}",),
                               tag=f"yo{oc}.{t0}")
                        S_.add("sp", lambda sp, oap=oap, oc=oc, t0=t0, tw=tw:
                               sp.dma_start(out=X1_r[:, oc, t0:t0 + tw], in_=oap),
                               reads=(f"xob{sl}",),
                               writes=(f"{x1keyp}.{oc}.{t0}",),
                               dma=True, dmagrp="x1st", tag=f"yst{oc}.{t0}")

                    emit_proj(cfg, nc, ps_y, "psy", S, w3qap, f"w3q{s3}", 1,
                              hid, R, 0, "hid", R, [0], y_cb, f"y{oc}",
                              kdc=FC0)
    S.barrier("b_pass")


def l0_dram_io(nc, cfg):
    D, HKV, F0 = cfg.D, cfg.HKV, cfg.F0
    io = {}
    io["WQ"] = nc.dram_tensor("WQ", [D, D], F32, kind="ExternalInput").ap()
    io["WK"] = nc.dram_tensor("WK", [D, HKV * 128], F32, kind="ExternalInput").ap()
    io["WV"] = nc.dram_tensor("WV", [D, HKV * 128], F32, kind="ExternalInput").ap()
    io["WO"] = nc.dram_tensor("WO", [D, D], F32, kind="ExternalInput").ap()
    io["W1T"] = nc.dram_tensor("W1T", [D, F0], F32, kind="ExternalInput").ap()
    io["W2T"] = nc.dram_tensor("W2T", [D, F0], F32, kind="ExternalInput").ap()
    io["W3T"] = nc.dram_tensor("W3T", [F0, D], F32, kind="ExternalInput").ap()
    io["PS1"] = nc.dram_tensor("PS1", [D, cfg.R], F32).ap()
    return io


def build_l0(cfg):
    R, D = cfg.R, cfg.D
    H = cfg.W
    T = H + R
    nc = bass.Bass("TRN2", target_bir_lowering=False)
    io = l0_dram_io(nc, cfg)
    io["XT"] = nc.dram_tensor("XT", [D, T], F32, kind="ExternalInput").ap()
    io["SINT"] = nc.dram_tensor("SINT", [128, T], F32, kind="ExternalInput").ap()
    io["COST"] = nc.dram_tensor("COST", [128, T], F32, kind="ExternalInput").ap()
    io["HIND"] = nc.dram_tensor("HIND", [1, T], F32, kind="ExternalInput").ap()
    io["X1dst"] = nc.dram_tensor("X1OUT", [D, R], F32, kind="ExternalOutput").ap()
    io["x1keyp"] = "@x1o"
    kwin = l0_key_windows(cfg, H, R)
    masks, mconsts = l0_mask_tiles(cfg, H, R)
    with ExitStack() as ctx:
        S = Sched(nc, ctx)
        S.consts = {}
        cspecs = {"ident_f32": ([128, 128], F32), "ident_bf": ([128, 128], BF16),
                  "ones_bf": ([128, 128], BF16), "ones_f32": ([128, 128], F32),
                  "ones_row": ([1, 512], F32), "epsln": ([128, 1], F32)}
        for mname in mconsts:
            cspecs[mname] = ([128, 128], F32)
            cspecs[mname + "T"] = ([128, 128], F32)
        setup_consts(nc, ctx, S, cspecs)
        io["tau"] = ctx.enter_context(nc.sbuf_tensor("tau", [128, 8], F32))
        emit_l0_gamma(cfg, nc, ctx, S, io)
        emit_l0_pass(cfg, nc, S, io, kwin, masks)
        allout = [f"@x1o.{oc}.{t0}" for oc in range(cfg.DC)
                  for t0, _ in col_chunks(R, 512)]
        S.add("sp", lambda sp: sp.nop(), reads=tuple(allout), tag="fin")
        block = ctx.enter_context(nc.Block())
        S.emit(block)
    return nc, masks, mconsts



def l0_const_arrays(cfg, mconsts):
    d = {"c_ident_f32": np.eye(128, dtype=np.float32),
         "c_ident_bf": np.eye(128, dtype=BF),
         "c_ones_bf": np.ones((128, 128), BF),
         "c_ones_f32": np.ones((128, 128), np.float32),
         "c_ones_row": np.ones((1, 512), np.float32),
         "c_epsln": np.full((128, 1), EPS_LN, np.float32)}
    for name, arr in mconsts.items():
        d[f"c_{name}"] = arr
        d[f"c_{name}T"] = np.ascontiguousarray(arr.T)
    return d


def stage_l0(cfg, inputs, mconsts):
    H, R = cfg.W, cfg.R
    T = H + R
    x = np.asarray(inputs["L0_attn_x"], np.float32).reshape(cfg.B * cfg.T, cfg.D)
    xT = np.ascontiguousarray(x.T)
    sin, cos = host_sincos(cfg)
    consts = l0_const_arrays(cfg, mconsts)
    wq = np.ascontiguousarray(np.asarray(inputs["L0_attn_q_kernel"], np.float32))
    wk = np.ascontiguousarray(np.asarray(inputs["L0_attn_k_kernel"], np.float32))
    wv = np.ascontiguousarray(np.asarray(inputs["L0_attn_v_kernel"], np.float32))
    wo = np.ascontiguousarray(np.asarray(inputs["L0_attn_out_kernel"], np.float32))
    w1T = np.ascontiguousarray(np.asarray(inputs["L0_ffn_w1"], np.float32).T)
    w2T = np.ascontiguousarray(np.asarray(inputs["L0_ffn_w2"], np.float32).T)
    w3T = np.ascontiguousarray(np.asarray(inputs["L0_ffn_w3"], np.float32).T)
    nblk = cfg.T // cfg.W
    maps = []
    for c in range(cfg.NC):
        beta, n = divmod(c, nblk)
        own0 = beta * cfg.T + n * cfg.W
        if n == 0:
            xs = np.concatenate([np.zeros((cfg.D, H), np.float32),
                                 xT[:, own0:own0 + R]], 1)
            sins = np.concatenate([np.zeros((128, H), np.float32), sin[:, 0:R]], 1)
            coss = np.concatenate([np.zeros((128, H), np.float32), cos[:, 0:R]], 1)
            hindr = np.concatenate([np.full((1, H), NEG, np.float32),
                                    np.zeros((1, R), np.float32)], 1)
        else:
            p0 = n * cfg.W - H
            xs = np.ascontiguousarray(xT[:, own0 - H:own0 + R])
            sins = np.ascontiguousarray(sin[:, p0:p0 + T])
            coss = np.ascontiguousarray(cos[:, p0:p0 + T])
            hindr = np.zeros((1, T), np.float32)
        sins = sins.copy()
        sins[0:64, :] *= -1.0
        maps.append({"XT": np.ascontiguousarray(xs), "WQ": wq, "WK": wk, "WV": wv,
                     "WO": wo, "W1T": w1T, "W2T": w2T, "W3T": w3T,
                     "SINT": np.ascontiguousarray(sins),
                     "COST": np.ascontiguousarray(coss),
                     "HIND": np.ascontiguousarray(hindr), **consts})
    return maps



def emit_l1_body(cfg, nc, S, io, x1deps=(), Tloc=None, exchange_cb=None):
    R, D, DC, T = cfg.R, cfg.D, cfg.DC, cfg.T
    HQ, F1, FC1 = cfg.HQ, cfg.F1, cfg.FC1
    Tloc = Tloc if Tloc is not None else T
    X1T, WQKV, BQKVR, WG = io["X1T"], io["WQKV"], io["BQKVR"], io["WG"]
    WPROJ, BPROJR = io["WPROJ"], io["BPROJR"]
    W0A, W0B, W1A, W1B = io["W0A"], io["W0B"], io["W1A"], io["W1B"]
    B0BR, B1BR = io["B0BR"], io["B1BR"]
    YOUT, KT_s, V_s, QT_s = io["YOUT"], io["KT_s"], io["V_s"], io["QT_s"]
    X1T_r = X1T.rearrange("(dc p) t -> p dc t", p=128)
    Y_r = YOUT.rearrange("(dc p) t -> p dc t", p=128)

    with ExitStack() as ctx:

        m0 = ctx.enter_context(nc.sbuf_tensor("m0", [128, R], BF16))
        m1 = ctx.enter_context(nc.sbuf_tensor("m1", [128, R], BF16))
        m0r = ctx.enter_context(nc.sbuf_tensor("m0r", [1, 512], F32))

        with ExitStack() as ctxQ:
            xn1 = ctxQ.enter_context(nc.sbuf_tensor("xn1", [128, DC * Tloc], BF16))
            with ExitStack() as ctxLN:
                scr = make_ln_scratch(cfg, nc, ctxLN, "l1s")
                emit_ln(cfg, nc, scr, S, X1T, 0, Tloc, xn1, Tloc, 0, "l1n",
                        src_deps=x1deps)
            S.barrier("b_l1ln")
            XN1D_r = io["XN1D"].rearrange("(dc p) t -> p dc t", p=128)
            for dc in range(DC):
                S.add("sp", lambda sp, dc=dc: sp.dma_start(
                    out=XN1D_r[:, dc, :], in_=xn1[:, dc * Tloc:dc * Tloc + R]),
                    reads=("l1nxn",), writes=(f"xn1d.{dc}",), dma=True,
                    dmagrp="kvst", tag=f"xno{dc}")
            with ExitStack() as ctxW:
                wsb = ctxW.enter_context(nc.sbuf_tensor("wsb1", [128, 2 * DC * 2 * 128], BF16))
                bqs = ctxW.enter_context(nc.sbuf_tensor("bqs", [1, 2 * 128], F32))
                ktmp = ctxW.enter_context(nc.sbuf_tensor("ktmp", [128, 512], BF16))
                vstage = ctxW.enter_context(nc.sbuf_tensor("vstage", [128, 512], BF16))
                wg_sb = ctxW.enter_context(nc.sbuf_tensor("wg_sb", [128, DC * 2], BF16))
                gsb = ctxW.enter_context(nc.sbuf_tensor("gsb", [2, 512], F32))
                ps_p = ctxW.enter_context(nc.psum_tensor("ps_p1", [128, 1024], F32))
                ps_tr = ctxW.enter_context(nc.psum_tensor("ps_tr1", [128, 128], BF16))
                ps_g = ctxW.enter_context(nc.psum_tensor("ps_g1", [128, 512], F32))
                S.add("g", lambda g: g.dma_start(
                    out=wg_sb.rearrange("p (dc o) -> p dc o", o=2),
                    in_=WG.rearrange("(dc p) o -> p dc o", p=128)),
                    writes=("wg",), dma=True, tag="ldwg")

                bq_done = set()

                def bias_mm(S_, oc, dcol, tw, ps_ap, psbuf):
                    sl = oc % 2
                    if oc not in bq_done:
                        bq_done.add(oc)
                        S_.add("sp", lambda sp, oc=oc, sl=sl: sp.dma_start(
                            out=bqs[0:1, sl * 128:(sl + 1) * 128],
                            in_=BQKVR[0:1, oc * 128:(oc + 1) * 128]),
                            writes=(f"bqs{sl}",), dma=True, tag=f"ldbq{oc}")
                    S_.add("t", lambda t, oc=oc, sl=sl, tw=tw, ps_ap=ps_ap: t.matmul(
                        ps_ap, bqs[0:1, sl * 128:(sl + 1) * 128],
                        S.consts["ones_row"][0:1, 0:tw], start=False, stop=True),
                        reads=(f"bqs{sl}", "@ones_row"), writes=(psbuf,),
                        tag=f"qkvb{oc}.{dcol}")

                def qkv_cb(S_, oc, dcol, tw, ps_ap, psbuf):
                    h, part = divmod(oc, 3)
                    if part == 0:
                        S_.add("v", lambda v, tw=tw, ps_ap=ps_ap: v.tensor_copy(
                            ktmp[:, 0:tw], ps_ap),
                            reads=(psbuf,), writes=("ktmp",), tag=f"qe{oc}.{dcol}")
                        S_.add("sp", lambda sp, h=h, dcol=dcol, tw=tw: sp.dma_start(
                            out=QT_s[h * 128:(h + 1) * 128, dcol:dcol + tw],
                            in_=ktmp[:, 0:tw]),
                            reads=("ktmp",), writes=(f"qts.{h}.{dcol}",), dma=True, dmagrp="kvst",
                            tag=f"qst{oc}.{dcol}")
                    elif part == 1:
                        S_.add("v", lambda v, tw=tw, ps_ap=ps_ap: v.tensor_copy(
                            ktmp[:, 0:tw], ps_ap),
                            reads=(psbuf,), writes=("ktmp",), tag=f"ke{oc}.{dcol}")
                        S_.add("sp", lambda sp, h=h, dcol=dcol, tw=tw: sp.dma_start(
                            out=KT_s[h * 128:(h + 1) * 128, dcol:dcol + tw],
                            in_=ktmp[:, 0:tw]),
                            reads=("ktmp",), writes=(f"kts.{h}.{dcol}",), dma=True, dmagrp="kvst",
                            tag=f"kst{oc}.{dcol}")
                    else:
                        S_.add("v", lambda v, tw=tw, ps_ap=ps_ap: v.tensor_copy(
                            vstage[:, 0:tw], ps_ap),
                            reads=(psbuf,), writes=("vstage",), tag=f"vev{oc}.{dcol}")
                        S_.add("sp", lambda sp, h=h, dcol=dcol, tw=tw: sp.dma_start(
                            out=V_s[h * 128:(h + 1) * 128, dcol:dcol + tw],
                            in_=vstage[:, 0:tw]),
                            reads=("vstage",), writes=(f"vs.{h}.{dcol}",), dma=True, dmagrp="kvst",
                            tag=f"vst{oc}.{dcol}")

                xp_all = [(xn1, Tloc, 0, "l1nxn", Tloc, 0)]
                xp_own = [(xn1, Tloc, 0, "l1nxn", R, 0)]
                for h in range(HQ):
                    emit_sproj(cfg, nc, ctxW, S, WQKV, "WQKV", wsb, xp_own,
                               3 * h, 1, qkv_cb, f"pq{h}", ps_p, "psp1",
                               blk=2, extra_mm=bias_mm)
                    emit_sproj(cfg, nc, ctxW, S, WQKV, "WQKV", wsb, xp_all,
                               3 * h + 1, 2, qkv_cb, f"pkv{h}", ps_p, "psp1",
                               blk=2, extra_mm=bias_mm)
                for t0, tw in col_chunks(R, 512):
                    for dc in range(DC):
                        S.add("t", lambda t, dc=dc, t0=t0, tw=tw: t.matmul(
                            ps_g[0:2, 0:tw], wg_sb[:, dc * 2:(dc + 1) * 2],
                            xn1[:, dc * Tloc + t0:dc * Tloc + t0 + tw],
                            start=(dc == 0), stop=(dc == DC - 1)),
                            reads=("wg", "l1nxn"), writes=("psg1",), tag=f"g{t0}.{dc}")
                    S.add("v", lambda v, tw=tw: v.tensor_copy(gsb[:, 0:tw],
                                                              ps_g[0:2, 0:tw]),
                          reads=("psg1",), writes=("gsb",), tag=f"ge{t0}")
                    S.add("t", lambda t, tw=tw: t.matmul(
                        ps_g[0:1, 0:tw], S.consts["epm"][:, :], gsb[0:2, 0:tw],
                        start=True, stop=True),
                        reads=("gsb", "@epm"), writes=("psg1",), tag=f"gd{t0}")
                    S.add("v", lambda v, tw=tw: v.tensor_scalar(
                        m0r[0:1, 0:tw], ps_g[0:1, 0:tw], 0.0, None, ALU.is_ge),
                        reads=("psg1",), writes=("m0r",), tag=f"gm{t0}")
                    S.add("t", lambda t, tw=tw: t.matmul(
                        ps_g[:, 0:tw], S.consts["ones_f32"][0:1, 0:128],
                        m0r[0:1, 0:tw], start=True, stop=True),
                        reads=("m0r", "@ones_f32"), writes=("psg1",), tag=f"gb{t0}")
                    S.add("v", lambda v, t0=t0, tw=tw: v.tensor_copy(
                        m0[:, t0:t0 + tw], ps_g[:, 0:tw]),
                        reads=("psg1",), writes=("m0",), tag=f"gmc{t0}")
                    S.add("v", lambda v, t0=t0, tw=tw: v.tensor_scalar(
                        m1[:, t0:t0 + tw], m0[:, t0:t0 + tw], -1.0, 1.0,
                        ALU.mult, ALU.add),
                        reads=("m0",), writes=("m1",), tag=f"gm1{t0}")
            S.barrier("b_l1q")
        if exchange_cb is not None:
            exchange_cb(S)
            S.barrier("b_xch")
        kt_src = io.get("kt_src", KT_s)
        v_src = io.get("v_src", V_s)
        kdep = ("@xdone",) if exchange_cb is not None else None

        with ExitStack() as ctxY:
            y_acc = ctxY.enter_context(nc.sbuf_tensor("y_acc", [128, DC * R], F32))
            with ExitStack() as ctxB:
                attnT = ctxB.enter_context(nc.sbuf_tensor("attnT1", [128, HQ * R], BF16))
                with ExitStack() as ctxAT:
                    kts = ctxAT.enter_context(nc.sbuf_tensor("kts", [128, 2 * T], BF16))
                    vts = ctxAT.enter_context(nc.sbuf_tensor("vts", [128, 2 * T], BF16))
                    qts = ctxAT.enter_context(nc.sbuf_tensor("qts", [128, 2 * R], BF16))

                    def prep_head(S_, h):
                        sl = h % 2
                        kr = kdep if kdep is not None else tuple(
                            f"kts.{h}.{d}" for d, _ in col_chunks(Tloc, 512))
                        vr = kdep if kdep is not None else tuple(
                            f"vs.{h}.{d}" for d, _ in col_chunks(Tloc, 512))
                        S_.add("g", lambda g, h=h, sl=sl: g.dma_start(
                            out=kts[:, sl * T: sl * T + T],
                            in_=kt_src[h * 128:(h + 1) * 128, :]),
                            reads=kr,
                            writes=(f"ktsl{sl}",), dma=True, tag=f"hk{h}")
                        S_.add("g", lambda g, h=h, sl=sl: g.dma_start(
                            out=vts[:, sl * T: sl * T + T],
                            in_=v_src[h * 128:(h + 1) * 128, :]),
                            reads=vr,
                            writes=(f"vtsl{sl}",), dma=True, tag=f"hv{h}")
                        S_.add("g", lambda g, h=h, sl=sl: g.dma_start(
                            out=qts[:, sl * R: sl * R + R],
                            in_=QT_s[h * 128:(h + 1) * 128, :]),
                            reads=tuple(f"qts.{h}.{d}" for d, _ in col_chunks(R, 512)),
                            writes=(f"qtsl{sl}",), dma=True, tag=f"hq{h}")

                    def out_cb(S_, h, q0, qw, out_ps_ap, opsbuf, rec_ap, recbuf):
                        S_.add("v", lambda v: v.tensor_tensor(
                            attnT[:, h * R + q0: h * R + q0 + qw], out_ps_ap,
                            rec_ap, ALU.mult),
                            reads=(opsbuf, recbuf), writes=("attnT1",),
                            tag=f"ao1{h}.{q0}")

                    emit_attn(cfg, nc, ctxAT, S, "a1", heads=HQ, kv_of=None,
                              Tkeys=T, R=R,
                              kwin=[(kc, 0, R) for kc in range(T // 128)],
                              masks={}, scale=L1_SCALE,
                              prep_head=prep_head,
                              kt_ap=lambda h: (kts[:, (h % 2) * T:(h % 2) * T + T],
                                               f"ktsl{h % 2}"),
                              v_dmaj=lambda h: (vts[:, (h % 2) * T:
                                                    (h % 2) * T + T], f"vtsl{h % 2}"),
                              q_ap=lambda h: (qts[:, (h % 2) * R:(h % 2) * R + R],
                                              f"qtsl{h % 2}"),
                              out_cb=out_cb)
                S.barrier("b_l1at")
                with ExitStack() as ctxP:
                    wsbP = ctxP.enter_context(
                        nc.sbuf_tensor("wsbP1", [128, 2 * DC * 2 * 128], BF16))
                    bpr = ctxP.enter_context(nc.sbuf_tensor("bpr", [1, D], F32))
                    x1r = ctxP.enter_context(nc.sbuf_tensor("x1r", [128, 2 * 512], F32))
                    ps_o = ctxP.enter_context(nc.psum_tensor("ps_o1", [128, 1024], F32))
                    S.add("sp", lambda sp: sp.dma_start(out=bpr[:, :], in_=BPROJR),
                          writes=("bpr",), dma=True, tag="ldbpr")

                    def pbias_mm(S_, oc, dcol, tw, ps_ap, psbuf):
                        S_.add("t", lambda t, oc=oc, tw=tw, ps_ap=ps_ap: t.matmul(
                            ps_ap, bpr[0:1, oc * 128:oc * 128 + 128],
                            S.consts["ones_row"][0:1, 0:tw], start=False, stop=True),
                            reads=("bpr", "@ones_row"), writes=(psbuf,),
                            tag=f"pb{oc}.{dcol}")

                    def p_cb(S_, oc, dcol, tw, ps_ap, psbuf):
                        sl = (oc * (R // 512) + dcol // 512) % 2
                        xap = x1r[:, sl * 512: sl * 512 + tw]
                        S_.add("sp", lambda sp, oc=oc, dcol=dcol, tw=tw, xap=xap:
                               sp.dma_start(out=xap,
                                            in_=X1T_r[:, oc, dcol:dcol + tw]),
                               reads=tuple(x1deps), writes=(f"x1r{sl}",), dma=True,
                               tag=f"pxl{oc}.{dcol}")
                        S_.add("v", lambda v, oc=oc, dcol=dcol, tw=tw, xap=xap,
                               ps_ap=ps_ap: v.tensor_tensor(
                            y_acc[:, oc * R + dcol: oc * R + dcol + tw], ps_ap,
                            xap, ALU.add),
                            reads=(psbuf, f"x1r{sl}"), writes=("y_acc", f"ya.{oc}.{dcol}"),
                            tag=f"pya{oc}.{dcol}")

                    emit_sproj(cfg, nc, ctxP, S, WPROJ, "WPROJ", wsbP,
                               [(attnT, R, 0, "attnT1", R, 0)], 0, DC, p_cb,
                               "pp", ps_o, "pso1", blk=2, extra_mm=pbias_mm)
                S.barrier("b_l1p")
            with ExitStack() as ctxF:
                hbuf = ctxF.enter_context(nc.sbuf_tensor("hbuf", [128, FC1 * R], BF16))
                xn1o = ctxF.enter_context(nc.sbuf_tensor("xn1o", [128, DC * R], BF16))
                XN1D_r2 = io["XN1D"].rearrange("(dc p) t -> p dc t", p=128)
                for dc in range(DC):
                    S.add("sp", lambda sp, dc=dc: sp.dma_start(
                        out=xn1o[:, dc * R:(dc + 1) * R], in_=XN1D_r2[:, dc, :]),
                        reads=(f"xn1d.{dc}",), writes=("xn1o",), dma=True,
                        dmagrp="kvst", tag=f"xnl{dc}")
                wsbF = ctxF.enter_context(
                    nc.sbuf_tensor("wsbF", [128, 2 * max(DC, FC1) * 1 * 128], BF16))
                brow = ctxF.enter_context(nc.sbuf_tensor("brow", [1, 2 * D], F32))
                tsb = ctxF.enter_context(nc.sbuf_tensor("tsb", [128, 512], F32))
                ps_f = ctxF.enter_context(nc.psum_tensor("ps_f1", [128, 1024], F32))
                S.add("sp", lambda sp: sp.dma_start(out=brow[0:1, 0:D], in_=B0BR),
                      writes=("b0br",), dma=True, tag="ldb0b")
                S.add("sp", lambda sp: sp.dma_start(out=brow[0:1, D:2 * D], in_=B1BR),
                      writes=("b1br",), dma=True, tag="ldb1b")

                for ex in range(2):
                    WA, WB = (W0A, W0B) if ex == 0 else (W1A, W1B)
                    bpp = S.consts["b0app" if ex == 0 else "b1app"]
                    bppbuf = "@b0app" if ex == 0 else "@b1app"
                    mask_sb, maskbuf = (m0, "m0") if ex == 0 else (m1, "m1")
                    act = AF.Gelu if ex == 0 else AF.Relu

                    def h_cb(S_, oc, dcol, tw, ps_ap, psbuf, ex=ex, act=act,
                             bpp=bpp, bppbuf=bppbuf):
                        S_.add("a", lambda a, oc=oc, dcol=dcol, tw=tw, ps_ap=ps_ap:
                               a.activation(
                            hbuf[:, oc * R + dcol: oc * R + dcol + tw], ps_ap,
                            act, bias=bpp[:, oc:oc + 1]),
                            reads=(psbuf, bppbuf), writes=("hbuf", f"hb.{ex}.{oc}"),
                            tag=f"he{ex}.{oc}.{dcol}")

                    emit_sproj(cfg, nc, ctxF, S, WA, f"WA{ex}", wsbF,
                               [(xn1o, R, 0, "xn1o", R, 0)], 0, FC1, h_cb,
                               f"fa{ex}", ps_f, "psf1", blk=1)
                    S.barrier(f"b_fa{ex}")

                    def obias_mm(S_, oc, dcol, tw, ps_ap, psbuf, ex=ex):
                        S_.add("t", lambda t, oc=oc, tw=tw, ps_ap=ps_ap, ex=ex:
                               t.matmul(
                            ps_ap, brow[0:1, ex * D + oc * 128: ex * D + oc * 128 + 128],
                            S.consts["ones_row"][0:1, 0:tw], start=False, stop=True),
                            reads=("b0br" if ex == 0 else "b1br", "@ones_row"),
                            writes=(psbuf,), tag=f"ob{ex}.{oc}.{dcol}")

                    def o_cb(S_, oc, dcol, tw, ps_ap, psbuf, ex=ex,
                             mask_sb=mask_sb, maskbuf=maskbuf):
                        S_.add("v", lambda v, dcol=dcol, tw=tw, ps_ap=ps_ap,
                               mask_sb=mask_sb: v.tensor_tensor(
                            tsb[:, 0:tw], ps_ap, mask_sb[:, dcol:dcol + tw],
                            ALU.mult),
                            reads=(psbuf, maskbuf), writes=("tsb",),
                            tag=f"om{ex}.{oc}.{dcol}")
                        S_.add("v", lambda v, oc=oc, dcol=dcol, tw=tw: v.tensor_tensor(
                            y_acc[:, oc * R + dcol: oc * R + dcol + tw],
                            y_acc[:, oc * R + dcol: oc * R + dcol + tw],
                            tsb[:, 0:tw], ALU.add),
                            reads=("tsb", f"ya.{oc}.{dcol}", "y_acc"),
                            writes=("y_acc", f"ya.{oc}.{dcol}"),
                            tag=f"oy{ex}.{oc}.{dcol}")

                    emit_sproj(cfg, nc, ctxF, S, WB, f"WB{ex}", wsbF,
                               [(hbuf, R, 0, "hbuf", R, 0)], 0, DC, o_cb,
                               f"fb{ex}", ps_f, "psf1", blk=1, kdc=FC1,
                               extra_mm=obias_mm)
                    S.barrier(f"b_fb{ex}")
                for oc in range(DC):
                    S.add("sp", lambda sp, oc=oc: sp.dma_start(
                        out=Y_r[:, oc, :], in_=y_acc[:, oc * R:(oc + 1) * R]),
                        reads=("y_acc",), writes=(f"yo.{oc}",), dma=True, dmagrp="yst",
                        tag=f"yst{oc}")
                S.add("sp", lambda sp: sp.nop(),
                      reads=tuple(f"yo.{oc}" for oc in range(DC)), tag="fin1")


def build_l1(cfg):
    R, D, T, FC1 = cfg.R, cfg.D, cfg.T, cfg.FC1
    F1 = cfg.F1
    nc = bass.Bass("TRN2", target_bir_lowering=False)
    io = l1_dram_io(nc, cfg)
    io["X1T"] = nc.dram_tensor("X1T", [D, T], F32, kind="ExternalInput").ap()
    io["YOUT"] = nc.dram_tensor("YOUT", [D, R], F32, kind="ExternalOutput").ap()
    with ExitStack() as ctx:
        S = Sched(nc, ctx)
        S.consts = {}
        setup_consts(nc, ctx, S, {
            "ident_f32": ([128, 128], F32), "ident_bf": ([128, 128], BF16),
            "ones_bf": ([128, 128], BF16), "ones_row": ([1, 512], F32),
            "ones_f32": ([128, 128], F32), "epsln": ([128, 1], F32),
            "b0app": ([128, FC1], F32), "b1app": ([128, FC1], F32),
            "epm": ([2, 1], F32)})
        emit_l1_body(cfg, nc, S, io)
        block = ctx.enter_context(nc.Block())
        S.emit(block)
    return nc


def l1_dram_io(nc, cfg, Tloc=None):
    D, T, R, F1, FC1 = cfg.D, cfg.T, cfg.R, cfg.F1, cfg.FC1
    T = Tloc if Tloc is not None else T
    io = {}
    io["WQKV"] = nc.dram_tensor("WQKV", [D, 3 * D], F32, kind="ExternalInput").ap()
    io["BQKVR"] = nc.dram_tensor("BQKVR", [1, 3 * D], F32, kind="ExternalInput").ap()
    io["WG"] = nc.dram_tensor("WG", [D, 2], F32, kind="ExternalInput").ap()
    io["WPROJ"] = nc.dram_tensor("WPROJ", [D, D], F32, kind="ExternalInput").ap()
    io["BPROJR"] = nc.dram_tensor("BPROJR", [1, D], F32, kind="ExternalInput").ap()
    io["W0A"] = nc.dram_tensor("W0A", [D, F1], F32, kind="ExternalInput").ap()
    io["W0B"] = nc.dram_tensor("W0B", [F1, D], F32, kind="ExternalInput").ap()
    io["B0BR"] = nc.dram_tensor("B0BR", [1, D], F32, kind="ExternalInput").ap()
    io["W1A"] = nc.dram_tensor("W1A", [D, F1], F32, kind="ExternalInput").ap()
    io["W1B"] = nc.dram_tensor("W1B", [F1, D], F32, kind="ExternalInput").ap()
    io["B1BR"] = nc.dram_tensor("B1BR", [1, D], F32, kind="ExternalInput").ap()
    io["KT_s"] = nc.dram_tensor("KT_s", [D, T], BF16).ap()
    io["V_s"] = nc.dram_tensor("V_s", [D, T], BF16).ap()
    io["QT_s"] = nc.dram_tensor("QT_s", [D, R], BF16).ap()
    io["XN1D"] = nc.dram_tensor("XN1D", [D, R], BF16).ap()
    return io


def stage_l1(cfg, inputs, x1T):
    D, R, T, FC1 = cfg.D, cfg.R, cfg.T, cfg.FC1
    wqkv = np.ascontiguousarray(np.asarray(inputs["L1_attn_Wqkv"], np.float32).T)
    bqkvr = np.asarray(inputs["L1_attn_bqkv"], np.float32).reshape(1, 3 * D)
    wg = np.ascontiguousarray(np.asarray(inputs["L1_ffn_Wg"], np.float32).T)
    epm = np.array([[1.0], [-1.0]], np.float32)
    wproj = np.ascontiguousarray(np.asarray(inputs["L1_attn_Wproj"], np.float32).T)
    bprojr = np.asarray(inputs["L1_attn_bproj"], np.float32).reshape(1, D)
    w0a = np.ascontiguousarray(np.asarray(inputs["L1_ffn_W0a"], np.float32).T)
    b0app = np.ascontiguousarray(
        np.asarray(inputs["L1_ffn_b0a"], np.float32).reshape(FC1, 128).T)
    w0b = np.ascontiguousarray(np.asarray(inputs["L1_ffn_W0b"], np.float32).T)
    b0br = np.asarray(inputs["L1_ffn_b0b"], np.float32).reshape(1, D)
    w1a = np.ascontiguousarray(np.asarray(inputs["L1_ffn_W1a"], np.float32).T)
    b1app = np.ascontiguousarray(
        np.asarray(inputs["L1_ffn_b1a"], np.float32).reshape(FC1, 128).T)
    w1b = np.ascontiguousarray(np.asarray(inputs["L1_ffn_W1b"], np.float32).T)
    b1br = np.asarray(inputs["L1_ffn_b1b"], np.float32).reshape(1, D)
    base = {"WQKV": wqkv, "BQKVR": bqkvr, "WG": wg, "WPROJ": wproj,
            "BPROJR": bprojr, "W0A": w0a, "W0B": w0b,
            "B0BR": b0br, "W1A": w1a, "W1B": w1b, "B1BR": b1br,
            "c_ident_f32": np.eye(128, dtype=np.float32),
            "c_ones_bf": np.ones((128, 128), BF),
            "c_ones_f32": np.ones((128, 128), np.float32),
            "c_ones_row": np.ones((1, 512), np.float32),
            "c_epsln": np.full((128, 1), EPS_LN, np.float32),
            "c_b0app": b0app, "c_b1app": b1app, "c_epm": epm,
            "c_ident_bf": np.eye(128, dtype=BF)}
    qpc = cfg.NC // cfg.B
    maps = []
    for c in range(cfg.NC):
        beta, j = divmod(c, qpc)
        xb = x1T[:, beta * T:(beta + 1) * T]
        rot = np.concatenate([xb[:, j * R:(j + 1) * R], xb[:, :j * R],
                              xb[:, (j + 1) * R:]], axis=1)
        maps.append(dict(base, X1T=np.ascontiguousarray(rot)))
    return maps



MAGIC_POLL_N = 2000000


def emit_exchange(cfg, nc, S, io, Tloc):
    D, T, R = cfg.D, cfg.T, cfg.R
    KT_s, V_s = io["KT_s"], io["V_s"]
    K_sh, V_sh, FLG = io["K_sh"], io["V_sh"], io["FLG"]
    cid_sb, nonce_sb = io["cid_sb"], io["nonce_sb"]
    poll_sb, sx = io["poll_sb"], io["sx"]

    kvreads = [f"kts.{h}.{d}" for h in range(cfg.HQ) for d, _ in col_chunks(Tloc, 512)]
    kvreads += [f"vs.{h}.{d}" for h in range(cfg.HQ) for d, _ in col_chunks(Tloc, 512)]
    kvreads += ["@cid_sb", "@nonce_sb"]

    def fn(g):
        with (g.register("rs") as rs, g.register("ro") as ro,
              g.register("fl") as fl, g.register("dd") as dd,
              g.register("ii") as ii, g.register("cnt") as cnt,
              g.register("rn") as rn):
            g.reg_load(rs, cid_sb[:1, :1])
            g.reg_mod(rs, rs, 2)
            g.reg_load(rn, nonce_sb[:1, :1])
            g.reg_mul(ro, rs, Tloc)
            g.dma_start(out=bass.AP(K_sh.tensor, ro, [[T, D], [1, 1], [1, Tloc]]),
                        in_=KT_s).then_inc(sx, 16)
            g.reg_mul(ro, rs, Tloc)
            g.dma_start(out=bass.AP(V_sh.tensor, ro, [[T, D], [1, 1], [1, Tloc]]),
                        in_=V_s).then_inc(sx, 16)
            g.wait_ge(sx, 32)
            g.reg_mul(ro, rs, 16)
            g.dma_start(out=bass.AP(FLG.tensor, ro, [[16, 1], [1, 1], [1, 16]]),
                        in_=nonce_sb[:, :]).then_inc(sx, 16)
            g.reg_mul(ro, rs, -16)
            g.reg_add(ro, ro, 16)
            g.reg_mov(dd, 1)
            g.reg_mov(ii, 0)
            g.reg_mov(cnt, 48)
            with g.While(dd):
                g.reg_add(cnt, cnt, 16)
                g.dma_start(out=poll_sb[:, :],
                            in_=bass.AP(FLG.tensor, ro, [[16, 1], [1, 1], [1, 16]])
                            ).then_inc(sx, 16)
                g.wait_ge(sx, cnt)
                g.reg_load(fl, poll_sb[:1, :1])
                g.reg_sub(dd, fl, rn)
                g.reg_add(ii, ii, 1)
                with g.If_cmp(ii, MAGIC_POLL_N, "IS_GE"):
                    g.reg_mov(dd, 0)
        return g.nop()

    S.add("g", fn, reads=tuple(kvreads), writes=("@xdone",), tag="exch")


def build_fused(cfg, npass=4):
    R, D, T = cfg.R, cfg.D, cfg.T
    H = cfg.W
    TL0 = H + R
    Tloc = npass * R
    FC1 = cfg.FC1
    nc = bass.Bass("TRN2", target_bir_lowering=False)
    io0 = l0_dram_io(nc, cfg)
    io1 = l1_dram_io(nc, cfg, Tloc=Tloc)
    X1OWN = nc.dram_tensor("X1OWN", [D, Tloc], F32).ap()
    io1["X1T"] = X1OWN
    io1["YOUT"] = nc.dram_tensor("YOUT", [D, R], F32, kind="ExternalOutput").ap()
    xts, sins, coss, hinds = [], [], [], []
    for p in range(npass):
        xts.append(nc.dram_tensor(f"XT{p}", [D, TL0], F32, kind="ExternalInput").ap())
        sins.append(nc.dram_tensor(f"SINT{p}", [128, TL0], F32, kind="ExternalInput").ap())
        coss.append(nc.dram_tensor(f"COST{p}", [128, TL0], F32, kind="ExternalInput").ap())
        hinds.append(nc.dram_tensor(f"HIND{p}", [1, TL0], F32, kind="ExternalInput").ap())
    if npass == 2:
        CID = nc.dram_tensor("CID", [1, 1], I32, kind="ExternalInput").ap()
        NONCE = nc.dram_tensor("NONCE", [1, 16], I32, kind="ExternalInput").ap()
        io1["K_sh"] = nc.dram_tensor("K_sh", [D, T], BF16, addr_space="Shared").ap()
        io1["V_sh"] = nc.dram_tensor("V_sh", [D, T], BF16, addr_space="Shared").ap()
        io1["FLG"] = nc.dram_tensor("FLG_sh", [2, 16], I32, addr_space="Shared").ap()
        io1["kt_src"] = io1["K_sh"]
        io1["v_src"] = io1["V_sh"]
    kwin = l0_key_windows(cfg, H, R)
    masks, mconsts = l0_mask_tiles(cfg, H, R)
    with ExitStack() as ctx:
        S = Sched(nc, ctx)
        S.consts = {}
        cspecs = {"ident_f32": ([128, 128], F32), "ident_bf": ([128, 128], BF16),
                  "ones_bf": ([128, 128], BF16), "ones_f32": ([128, 128], F32),
                  "ones_row": ([1, 512], F32), "epsln": ([128, 1], F32),
                  "b0app": ([128, FC1], F32), "b1app": ([128, FC1], F32),
                  "epm": ([2, 1], F32)}
        for mname in mconsts:
            cspecs[mname] = ([128, 128], F32)
            cspecs[mname + "T"] = ([128, 128], F32)
        setup_consts(nc, ctx, S, cspecs)
        exchange_cb = None
        if npass == 2:
            io1["cid_sb"] = ctx.enter_context(nc.sbuf_tensor("cid_sb", [1, 1], I32))
            io1["nonce_sb"] = ctx.enter_context(nc.sbuf_tensor("nonce_sb", [1, 16], I32))
            S.add("sp", lambda sp: sp.dma_start(out=io1["cid_sb"][:, :], in_=CID),
                  writes=("@cid_sb",), dma=True, dmagrp="consts", tag="ldcid")
            S.add("sp", lambda sp: sp.dma_start(out=io1["nonce_sb"][:, :], in_=NONCE),
                  writes=("@nonce_sb",), dma=True, dmagrp="consts", tag="ldnonce")
            io1["poll_sb"] = ctx.enter_context(nc.sbuf_tensor("xpoll", [1, 16], I32))
            io1["sx"] = ctx.enter_context(nc.semaphore("s_xch"))
            exchange_cb = lambda S_: emit_exchange(cfg, nc, S_, io1, Tloc)
        io0["tau"] = ctx.enter_context(nc.sbuf_tensor("tau", [128, 8], F32))
        emit_l0_gamma(cfg, nc, ctx, S, io0)
        x1deps = []
        for p in range(npass):
            S.ns = f"p{p}."
            iop = dict(io0, XT=xts[p], SINT=sins[p], COST=coss[p],
                       HIND=hinds[p], X1dst=X1OWN[:, p * R:(p + 1) * R],
                       x1keyp=f"@x1.{p}")
            emit_l0_pass(cfg, nc, S, iop, kwin, masks)
            x1deps += [f"@x1.{p}.{oc}.{t0}" for oc in range(cfg.DC)
                       for t0, _ in col_chunks(R, 512)]
        S.ns = "l1."
        emit_l1_body(cfg, nc, S, io1, x1deps=tuple(x1deps), Tloc=Tloc,
                     exchange_cb=exchange_cb)
        S.ns = ""
        block = ctx.enter_context(nc.Block())
        S.emit(block)
    return nc, masks, mconsts


def stage_fused(cfg, inputs, mconsts, npass=4):
    H, R, T = cfg.W, cfg.R, cfg.T
    TL0 = H + R
    nblk = cfg.NC // cfg.B
    x = np.asarray(inputs["L0_attn_x"], np.float32).reshape(cfg.B * cfg.T, cfg.D)
    xT = np.ascontiguousarray(x.T)
    sin, cos = host_sincos(cfg)
    sinF = sin.copy()
    sinF[0:64, :] *= -1.0
    consts = l0_const_arrays(cfg, mconsts)
    FC1 = cfg.FC1
    consts["c_b0app"] = np.ascontiguousarray(
        np.asarray(inputs["L1_ffn_b0a"], np.float32).reshape(FC1, 128).T)
    consts["c_b1app"] = np.ascontiguousarray(
        np.asarray(inputs["L1_ffn_b1a"], np.float32).reshape(FC1, 128).T)
    consts["c_epm"] = np.array([[1.0], [-1.0]], np.float32)
    D = cfg.D
    base = {
        "WQ": np.ascontiguousarray(np.asarray(inputs["L0_attn_q_kernel"], np.float32)),
        "WK": np.ascontiguousarray(np.asarray(inputs["L0_attn_k_kernel"], np.float32)),
        "WV": np.ascontiguousarray(np.asarray(inputs["L0_attn_v_kernel"], np.float32)),
        "WO": np.ascontiguousarray(np.asarray(inputs["L0_attn_out_kernel"], np.float32)),
        "W1T": np.ascontiguousarray(np.asarray(inputs["L0_ffn_w1"], np.float32).T),
        "W2T": np.ascontiguousarray(np.asarray(inputs["L0_ffn_w2"], np.float32).T),
        "W3T": np.ascontiguousarray(np.asarray(inputs["L0_ffn_w3"], np.float32).T),
        "WQKV": np.ascontiguousarray(np.asarray(inputs["L1_attn_Wqkv"], np.float32).T),
        "BQKVR": np.asarray(inputs["L1_attn_bqkv"], np.float32).reshape(1, 3 * D),
        "WG": np.ascontiguousarray(np.asarray(inputs["L1_ffn_Wg"], np.float32).T),
        "WPROJ": np.ascontiguousarray(np.asarray(inputs["L1_attn_Wproj"], np.float32).T),
        "BPROJR": np.asarray(inputs["L1_attn_bproj"], np.float32).reshape(1, D),
        "W0A": np.ascontiguousarray(np.asarray(inputs["L1_ffn_W0a"], np.float32).T),
        "W0B": np.ascontiguousarray(np.asarray(inputs["L1_ffn_W0b"], np.float32).T),
        "B0BR": np.asarray(inputs["L1_ffn_b0b"], np.float32).reshape(1, D),
        "W1A": np.ascontiguousarray(np.asarray(inputs["L1_ffn_W1a"], np.float32).T),
        "W1B": np.ascontiguousarray(np.asarray(inputs["L1_ffn_W1b"], np.float32).T),
        "B1BR": np.asarray(inputs["L1_ffn_b1b"], np.float32).reshape(1, D),
        **consts}
    nonce = np.random.randint(1, 2 ** 31 - 1, dtype=np.int32)
    maps = []
    for c in range(cfg.NC):
        beta, j = divmod(c, nblk)
        d = dict(base)
        if npass == 2:
            order = [[0, 1], [2, 3], [1, 0], [3, 2]][j]
            d["CID"] = np.full((1, 1), c, np.int32)
            d["NONCE"] = np.full((1, 16), nonce, np.int32)
        else:
            order = [(j + p) % nblk for p in range(nblk)]
        for p in range(npass):
            n = order[p]
            own0 = beta * T + n * R
            if n == 0:
                xs = np.concatenate([np.zeros((cfg.D, H), np.float32),
                                     xT[:, own0:own0 + R]], 1)
                ss = np.concatenate([np.zeros((128, H), np.float32),
                                     sinF[:, 0:R]], 1)
                cc = np.concatenate([np.zeros((128, H), np.float32),
                                     cos[:, 0:R]], 1)
                hh = np.concatenate([np.full((1, H), NEG, np.float32),
                                     np.zeros((1, R), np.float32)], 1)
            else:
                p0 = n * R - H
                xs = xT[:, own0 - H:own0 + R]
                ss = sinF[:, p0:p0 + TL0]
                cc = cos[:, p0:p0 + TL0]
                hh = np.zeros((1, TL0), np.float32)
            d[f"XT{p}"] = np.ascontiguousarray(xs)
            d[f"SINT{p}"] = np.ascontiguousarray(ss)
            d[f"COST{p}"] = np.ascontiguousarray(cc)
            d[f"HIND{p}"] = np.ascontiguousarray(hh)
        maps.append(d)
    return maps


def run_fused(cfg, inputs, runner=None, npass=4):
    nc, masks, mconsts = build_fused(cfg, npass=npass)
    maps = stage_fused(cfg, inputs, mconsts, npass=npass)
    if runner is None:
        res = run_bass_kernel_spmd(nc, maps, core_ids=list(range(cfg.NC)))
        outs = [res.results[c] for c in range(cfg.NC)]
    else:
        outs = runner(nc, maps)
    yT = np.zeros((cfg.D, cfg.B * cfg.T), np.float32)
    nblk = cfg.NC // cfg.B
    for c in range(cfg.NC):
        beta, j = divmod(c, nblk)
        qb = [[0, 1], [2, 3], [1, 0], [3, 2]][j][0] if npass == 2 else j
        yT[:, beta * cfg.T + qb * cfg.R: beta * cfg.T + (qb + 1) * cfg.R] = \
            np.asarray(outs[c]["YOUT"])
    return np.ascontiguousarray(yT.T).reshape(cfg.B, cfg.T, cfg.D)



FUSED = True
NPASS = 2


def kernel(**inputs):
    cfg = CFG
    inputs.pop("L1_attn_key", None)
    inputs.pop("L1_attn_value", None)
    if FUSED:
        return run_fused(cfg, inputs, npass=NPASS)
    nc0, masks, mconsts = build_l0(cfg)
    maps0 = stage_l0(cfg, inputs, mconsts)
    res0 = run_bass_kernel_spmd(nc0, maps0, core_ids=list(range(cfg.NC)))
    x1T = np.zeros((cfg.D, cfg.B * cfg.T), np.float32)
    nblk = cfg.T // cfg.W
    for c in range(cfg.NC):
        beta, n = divmod(c, nblk)
        own0 = beta * cfg.T + n * cfg.W
        x1T[:, own0:own0 + cfg.R] = np.asarray(res0.results[c]["X1OUT"])
    nc1 = build_l1(cfg)
    maps1 = stage_l1(cfg, inputs, x1T)
    res1 = run_bass_kernel_spmd(nc1, maps1, core_ids=list(range(cfg.NC)))
    yT = np.zeros((cfg.D, cfg.B * cfg.T), np.float32)
    qpc = cfg.NC // cfg.B
    for c in range(cfg.NC):
        beta, j = divmod(c, qpc)
        yT[:, beta * cfg.T + j * cfg.R: beta * cfg.T + (j + 1) * cfg.R] = \
            np.asarray(res1.results[c]["YOUT"])
    return np.ascontiguousarray(yT.T).reshape(cfg.B, cfg.T, cfg.D)
